# version 03
# baseline (speedup 1.0000x reference)
from contextlib import ExitStack

import numpy as np
import concourse.bass as bass
import concourse.mybir as mybir

F32 = mybir.dt.float32
BF16 = mybir.dt.bfloat16
I32 = mybir.dt.int32
AF = mybir.ActivationFunctionType
ALU = mybir.AluOpType
AX = mybir.AxisListType

TOK = 2048
CM = 1024
ID = 3072
QMAX = 32767.0

CONST_LIST = [
    ("inv_qmax", 1.0 / QMAX),
    ("eps12", 1e-12),
    ("c65536", 65536.0),
    ("c65535", 65535.0),
    ("sc216", 1.0 / 65536.0),
    ("nsc216", -1.0 / 65536.0),
    ("inv7", 1.0 / 7.0),
    ("ln_eps", 1e-5),
    ("neg_half", -0.5),
    ("inv_cm", 1.0 / CM),
    ("att_scale", 0.125),
    ("big", 30000.0),
    ("win_budget", 25.5),
    ("c64", 64.0),
    ("half", 0.5),
    ("neg_one", -1.0),
    ("neg_big", -30000.0),
    ("c1024", 1024.0),
    ("c817", 817.5),
]
CIX = {k: i for i, (k, _) in enumerate(CONST_LIST)}
NCONST = len(CONST_LIST)


def consts_array():
    v = np.array([val for _, val in CONST_LIST], dtype=np.float32)
    return np.tile(v[None, :], (128, 1))


def cc(cs, name):
    i = CIX[name]
    return cs[:, i:i + 1]


def common_inputs(nc):
    consts = nc.dram_tensor("consts", [128, NCONST], F32, kind="ExternalInput").ap()
    onesb = nc.dram_tensor("onesb", [128, 128], BF16, kind="ExternalInput").ap()
    identf = nc.dram_tensor("identf", [128, 128], F32, kind="ExternalInput").ap()
    onesf = nc.dram_tensor("onesf", [128, 128], F32, kind="ExternalInput").ap()
    return consts, onesb, identf, onesf


def _scale_prep(nc, blk, cs, onesf_s, lmin8, srow, sb, psb, s_sc, s_bc, nsc):
    def dve_scale(v):
        v.reduce_max(srow[0:1, 0:1], lmin8[:], axis=AX.X).then_inc(s_sc, 1)
        v.wait_ge(s_sc, 1)
        v.tensor_copy(srow[0:1, 3:4], cs[0:1, 0:1])
        v.tensor_scalar(srow[0:1, 0:1], srow[0:1, 0:1],
                        cc(cs, "inv_qmax")[0:1, :], cc(cs, "eps12")[0:1, :],
                        ALU.mult, ALU.max).then_inc(s_sc, 1)
        v.wait_ge(s_sc, 2)
        v.tensor_copy(srow[0:1, 3:4], cs[0:1, 0:1])
        v.reciprocal(srow[0:1, 1:2], srow[0:1, 0:1]).then_inc(s_sc, 1)
        v.wait_ge(s_sc, 3)
        v.tensor_copy(srow[0:1, 3:4], cs[0:1, 0:1])
        v.tensor_scalar(srow[0:1, 2:3], srow[0:1, 0:1],
                        cc(cs, "neg_one")[0:1, :], None, ALU.mult)\
            .then_inc(s_sc, 1)
    blk.vector(dve_scale)

    def pe_bc(t):
        t.wait_ge(s_sc, 4)
        t.matmul(psb[:, 0:3], onesf_s[0:1, 0:128], srow[0:1, 0:3],
                 start=True, stop=True).then_inc(s_bc, 1)
    blk.tensor(pe_bc)

    def act_bc(a):
        a.wait_ge(s_bc, 1)
        a.copy(sb[:], psb[:, 0:3]).then_inc(s_bc, 1)
    blk.scalar(act_bc)


def dve_round4(v, dq, yq, qi, out, sb, s_cnt, base):
    v.scalar_tensor_tensor(dq, qi, -1.0, yq, ALU.mult, ALU.add)\
        .then_inc(s_cnt, 1)
    v.wait_ge(s_cnt, base + 1)
    v.scalar_tensor_tensor(yq, dq, 0.5, qi, ALU.is_ge, ALU.add)\
        .then_inc(s_cnt, 1)
    v.wait_ge(s_cnt, base + 2)
    v.scalar_tensor_tensor(dq, dq, -0.5, yq, ALU.is_ge, ALU.add)\
        .then_inc(s_cnt, 1)
    v.wait_ge(s_cnt, base + 3)
    v.tensor_scalar(out, dq, sb[:, 0:1], sb[:, 2:3], ALU.mult, ALU.add)\
        .then_inc(s_cnt, 1)


def _xpart_max_store(nc, blk, idf_s, mxcol, ncols, sc128, psT, lmout_dram,
                     lm11, s_pre, pre_val, s_f, s_fd, g_extra_wait=None):
    def dve_a(v):
        v.wait_ge(s_pre, pre_val)
        v.reduce_max(sc128[:, 0:1], mxcol[:, 0:ncols], axis=AX.X).then_inc(s_f, 1)
    blk.vector(dve_a)

    def pe_a(t):
        t.wait_ge(s_f, 1)
        t.transpose(psT[:], sc128[:], idf_s[:]).then_inc(s_f, 1)
    blk.tensor(pe_a)

    def dve_b(v):
        v.wait_ge(s_f, 2)
        v.reduce_max(lm11[0:1, 0:1], psT[0:1, 0:128], axis=AX.X).then_inc(s_f, 1)
    blk.vector(dve_b)

    def g_a(g):
        g.wait_ge(s_f, 3)
        g.dma_start(out=lmout_dram, in_=lm11[0:1, 0:1]).then_inc(s_fd, 16)
        g.wait_ge(s_fd, 16)
        if g_extra_wait is not None:
            g_extra_wait(g)
    blk.gpsimd(g_a)


def build_L2():
    nc = bass.Bass("TRN2", target_bir_lowering=False)
    x3bf = nc.dram_tensor("x3bf", [TOK, CM], BF16, kind="ExternalInput").ap()
    lmx8 = nc.dram_tensor("lmx8", [1, 8], F32, kind="ExternalInput").ap()
    wupdq = nc.dram_tensor("wupdq", [ID, CM], BF16, kind="ExternalInput").ap()
    wgatedq = nc.dram_tensor("wgatedq", [ID, CM], BF16, kind="ExternalInput").ap()
    consts, onesb, identf, onesf = common_inputs(nc)

    upT = nc.dram_tensor("upT", [ID, TOK], BF16, kind="ExternalOutput").ap()
    gateT = nc.dram_tensor("gateT", [ID, TOK], BF16, kind="ExternalOutput").ap()
    lmug = nc.dram_tensor("lmug", [2, 1], F32, kind="ExternalOutput").ap()

    xq_dram = nc.dram_tensor("xq_dram", [TOK, CM], BF16, kind="Internal").ap()

    NT = TOK // 128
    NM = ID // 128
    NTC = TOK // 512
    NC8 = CM // 128
    NK = NM * NTC

    with ExitStack() as est:
        e = est.enter_context
        cs = e(nc.sbuf_tensor("cs", [128, NCONST], F32))
        ones_s = e(nc.sbuf_tensor("ones_s", [128, 128], BF16))
        onesf_s = e(nc.sbuf_tensor("onesf_s", [128, 128], F32))
        idf_s = e(nc.sbuf_tensor("idf_s", [128, 128], F32))
        lm8 = e(nc.sbuf_tensor("lm8", [1, 8], F32))
        sx = e(nc.sbuf_tensor("sx", [1, 4], F32))
        sxb = e(nc.sbuf_tensor("sxb", [128, 3], F32))
        x3d = [e(nc.sbuf_tensor(f"x3d{i}", [128, CM], BF16)) for i in range(2)]
        xqi = [e(nc.sbuf_tensor(f"xqi{i}", [128, CM], I32)) for i in range(2)]
        xqf = [e(nc.sbuf_tensor(f"xqf{i}", [128, CM], F32)) for i in range(2)]
        xqd = [e(nc.sbuf_tensor(f"xqd{i}", [128, CM], F32)) for i in range(2)]
        xqb = [e(nc.sbuf_tensor(f"xqb{i}", [128, CM], BF16)) for i in range(2)]
        wT = [e(nc.sbuf_tensor(f"wT{i}", [128, ID], BF16)) for i in range(NC8)]
        gT = [e(nc.sbuf_tensor(f"gT{i}", [128, ID], BF16)) for i in range(NC8)]
        xqT = [e(nc.sbuf_tensor(f"xqT{i}", [128, TOK], BF16)) for i in range(NC8)]
        ev = [e(nc.sbuf_tensor(f"ev{i}", [128, 512], BF16)) for i in range(4)]
        mxcol = e(nc.sbuf_tensor("mxcol", [128, 2 * NK], F32))
        sc128 = e(nc.sbuf_tensor("sc128", [128, 128], F32))
        lm2 = e(nc.sbuf_tensor("lm2", [2, 1], F32))
        ps = [e(nc.psum_tensor(f"ps{i}", [128, 512], F32)) for i in range(4)]
        psb = e(nc.psum_tensor("psb", [128, 128], F32))
        psT = e(nc.psum_tensor("psT", [128, 128], F32))

        s_ld0 = e(nc.semaphore("s_ld0"))
        s_sc = e(nc.semaphore("s_sc"))
        s_bc = e(nc.semaphore("s_bc"))
        s_xl = [e(nc.semaphore(f"s_xl{j}")) for j in range(2)]
        s_xa = e(nc.semaphore("s_xa"))
        s_xv = e(nc.semaphore("s_xv"))
        s_xs = [e(nc.semaphore(f"s_xs{j}")) for j in range(2)]
        s_wl = e(nc.semaphore("s_wl"))
        s_mm = e(nc.semaphore("s_mm"))
        s_ev = e(nc.semaphore("s_ev"))
        s_rd = e(nc.semaphore("s_rd"))
        s_sb = [e(nc.semaphore(f"s_sb{j}")) for j in range(4)]
        s_f = e(nc.semaphore("s_f"))
        s_fd = e(nc.semaphore("s_fd"))
        blk = e(nc.Block())

        def sync_b(s):
            s.dma_start(out=cs[:], in_=consts).then_inc(s_ld0, 16)
            s.dma_start(out=ones_s[:], in_=onesb).then_inc(s_ld0, 16)
            s.dma_start(out=onesf_s[:], in_=onesf).then_inc(s_ld0, 16)
            s.dma_start(out=idf_s[:], in_=identf).then_inc(s_ld0, 16)
            s.dma_start(out=lm8[:], in_=lmx8).then_inc(s_ld0, 16)
            for c in range(NC8):
                s.dma_start(out=wT[c][:], in_=wupdq[:, c * 128:(c + 1) * 128],
                            transpose=True).then_inc(s_wl, 16)
            for c in range(NC8):
                s.dma_start(out=gT[c][:], in_=wgatedq[:, c * 128:(c + 1) * 128],
                            transpose=True).then_inc(s_wl, 16)
            for i in range(NT):
                j = i % 2
                if i >= 2:
                    s.wait_ge(s_xa, 2 * i - 3)
                s.dma_start(out=x3d[j][:], in_=x3bf[i * 128:(i + 1) * 128, :])\
                    .then_inc(s_xl[j], 16)
            for j in range(2):
                s.wait_ge(s_xs[j], 16 * (NT // 2))
            for c in range(NC8):
                s.dma_start(out=xqT[c][:], in_=xq_dram[:, c * 128:(c + 1) * 128],
                            transpose=True).then_inc(s_wl, 16)
        blk.sync(sync_b)

        def dve_pre(v):
            v.wait_ge(s_ld0, 80)
        blk.vector(dve_pre)
        _scale_prep(nc, blk, cs, onesf_s, lm8, sx, sxb, psb, s_sc, s_bc, 3)

        def act_xq(a):
            a.wait_ge(s_bc, 2)
            for i in range(NT):
                j = i % 2
                a.wait_ge(s_xl[j], 16 * (i // 2 + 1))
                if i >= 2:
                    a.wait_ge(s_xv, 4 * (i - 2) + 3)
                a.activation(xqf[j][:], x3d[j][:], AF.Copy,
                             scale=sxb[:, 1:2]).then_inc(s_xa, 1)
                a.wait_ge(s_xa, 2 * i + 1)
                a.activation(xqi[j][:], xqf[j][:], AF.Copy).then_inc(s_xa, 1)
        blk.scalar(act_xq)

        def dve_xq(v):
            for i in range(NT):
                j = i % 2
                v.wait_ge(s_xa, 2 * i + 2)
                if i >= 2:
                    v.wait_ge(s_xs[j], 16 * (i // 2))
                dve_round4(v, xqd[j][:], xqf[j][:], xqi[j][:], xqb[j][:],
                           sxb, s_xv, 4 * i)
        blk.vector(dve_xq)

        def g_xq(g):
            for i in range(NT):
                j = i % 2
                g.wait_ge(s_xv, 4 * (i + 1))
                g.dma_start(out=xq_dram[i * 128:(i + 1) * 128, :],
                            in_=xqb[j][:]).then_inc(s_xs[j], 16)
        blk.gpsimd(g_xq)

        def pe_mm(t):
            t.wait_ge(s_wl, 16 * 3 * NC8)
            k = 0
            for wsel in range(2):
                W = wT if wsel == 0 else gT
                for m in range(NM):
                    for tc in range(NTC):
                        pk = k % 4
                        if k >= 4:
                            t.wait_ge(s_ev, k - 3)
                        mm = None
                        for c in range(NC8):
                            mm = t.matmul(ps[pk][:],
                                          W[c][:, m * 128:(m + 1) * 128],
                                          xqT[c][:, tc * 512:(tc + 1) * 512],
                                          start=(c == 0), stop=(c == NC8 - 1))
                        mm.then_inc(s_mm, 1)
                        k += 1
        blk.tensor(pe_mm)

        def act_ev(a):
            for k in range(2 * NK):
                a.wait_ge(s_mm, k + 1)
                if k >= 4:
                    a.wait_ge(s_sb[k % 4], 16 * (k // 4))
                a.activation(ev[k % 4][:], ps[k % 4][:], AF.Copy).then_inc(s_ev, 1)
        blk.scalar(act_ev)

        def dve_rd(v):
            for k in range(2 * NK):
                v.wait_ge(s_ev, k + 1)
                v.reduce_max(mxcol[:, k:k + 1], ev[k % 4][:], axis=AX.X,
                             apply_absolute_value=True).then_inc(s_rd, 1)
        blk.vector(dve_rd)

        def g_st(g):
            for k in range(2 * NK):
                g.wait_ge(s_rd, k + 1)
                wsel, r = divmod(k, NK)
                m, tc = divmod(r, NTC)
                dst = upT if wsel == 0 else gateT
                g.dma_start(out=dst[m * 128:(m + 1) * 128,
                                    tc * 512:(tc + 1) * 512],
                            in_=ev[k % 4][:]).then_inc(s_sb[k % 4], 16)
        blk.gpsimd(g_st)

        def dve_fin(v):
            v.wait_ge(s_rd, 2 * NK)
            v.reduce_max(sc128[:, 0:1], mxcol[:, 0:NK], axis=AX.X).then_inc(s_f, 1)
            v.reduce_max(sc128[:, 1:2], mxcol[:, NK:2 * NK], axis=AX.X)\
                .then_inc(s_f, 1)
        blk.vector(dve_fin)

        def pe_fin(t):
            t.wait_ge(s_f, 2)
            t.transpose(psT[:], sc128[:], idf_s[:]).then_inc(s_f, 1)
        blk.tensor(pe_fin)

        def dve_fin2(v):
            v.wait_ge(s_f, 3)
            v.reduce_max(lm2[0:2, 0:1], psT[0:2, 0:128], axis=AX.X).then_inc(s_f, 1)
        blk.vector(dve_fin2)

        def g_fin(g):
            g.wait_ge(s_f, 4)
            g.dma_start(out=lmug, in_=lm2[0:2, 0:1]).then_inc(s_fd, 16)
            g.wait_ge(s_fd, 16)
            for j in range(4):
                g.wait_ge(s_sb[j], 16 * (2 * NK // 4))
        blk.gpsimd(g_fin)

    return nc


def build_L3():
    nc = bass.Bass("TRN2", target_bir_lowering=False)
    gateT = nc.dram_tensor("gateT", [ID, TOK], BF16, kind="ExternalInput").ap()
    lmg8 = nc.dram_tensor("lmg8", [1, 8], F32, kind="ExternalInput").ap()
    consts, onesb, identf, onesf = common_inputs(nc)
    g2T = nc.dram_tensor("g2T", [ID, TOK], BF16, kind="ExternalOutput").ap()
    lmg2 = nc.dram_tensor("lmg2", [1, 1], F32, kind="ExternalOutput").ap()

    NR = ID // 128

    with ExitStack() as est:
        e = est.enter_context
        cs = e(nc.sbuf_tensor("cs", [128, NCONST], F32))
        ones_s = e(nc.sbuf_tensor("ones_s", [128, 128], BF16))
        onesf_s = e(nc.sbuf_tensor("onesf_s", [128, 128], F32))
        idf_s = e(nc.sbuf_tensor("idf_s", [128, 128], F32))
        lm8 = e(nc.sbuf_tensor("lm8", [1, 8], F32))
        srow = e(nc.sbuf_tensor("srow", [1, 4], F32))
        sb = e(nc.sbuf_tensor("sb", [128, 3], F32))
        gt = [e(nc.sbuf_tensor(f"gt{i}", [128, TOK], BF16)) for i in range(2)]
        gqi = [e(nc.sbuf_tensor(f"gqi{i}", [128, TOK], I32)) for i in range(2)]
        gq = [e(nc.sbuf_tensor(f"gq{i}", [128, TOK], F32)) for i in range(2)]
        ygf = [e(nc.sbuf_tensor(f"ygf{i}", [128, TOK], F32)) for i in range(2)]
        gqd = [e(nc.sbuf_tensor(f"gqd{i}", [128, TOK], F32)) for i in range(2)]
        sig = [e(nc.sbuf_tensor(f"sig{i}", [128, TOK], F32)) for i in range(2)]
        yuf = [e(nc.sbuf_tensor(f"yuf{i}", [128, TOK], F32)) for i in range(2)]
        qd = [e(nc.sbuf_tensor(f"qd{i}", [128, TOK], F32)) for i in range(2)]
        qi = [e(nc.sbuf_tensor(f"qi{i}", [128, TOK], I32)) for i in range(2)]
        sfin = [e(nc.sbuf_tensor(f"sfin{i}", [128, TOK], F32)) for i in range(2)]
        g2o = [e(nc.sbuf_tensor(f"g2o{i}", [128, TOK], BF16)) for i in range(2)]
        mxcol = e(nc.sbuf_tensor("mxcol", [128, NR], F32))
        sc128 = e(nc.sbuf_tensor("sc128", [128, 128], F32))
        lm11 = e(nc.sbuf_tensor("lm11", [1, 1], F32))
        psb = e(nc.psum_tensor("psb", [128, 128], F32))
        psT = e(nc.psum_tensor("psT", [128, 128], F32))

        s_ld0 = e(nc.semaphore("s_ld0"))
        s_sc = e(nc.semaphore("s_sc"))
        s_bc = e(nc.semaphore("s_bc"))
        s_l = [e(nc.semaphore(f"s_l{j}")) for j in range(2)]
        s_a1 = e(nc.semaphore("s_a1"))
        s_v1 = e(nc.semaphore("s_v1"))
        s_a2 = e(nc.semaphore("s_a2"))
        s_v2 = e(nc.semaphore("s_v2"))
        s_v3 = e(nc.semaphore("s_v3"))
        s_v4 = e(nc.semaphore("s_v4"))
        s_rd = e(nc.semaphore("s_rd"))
        s_sb = [e(nc.semaphore(f"s_sb{j}")) for j in range(2)]
        s_f = e(nc.semaphore("s_f"))
        s_fd = e(nc.semaphore("s_fd"))
        blk = e(nc.Block())

        def sync_b(s):
            s.dma_start(out=cs[:], in_=consts).then_inc(s_ld0, 16)
            s.dma_start(out=ones_s[:], in_=onesb).then_inc(s_ld0, 16)
            s.dma_start(out=onesf_s[:], in_=onesf).then_inc(s_ld0, 16)
            s.dma_start(out=idf_s[:], in_=identf).then_inc(s_ld0, 16)
            s.dma_start(out=lm8[:], in_=lmg8).then_inc(s_ld0, 16)
            for i in range(NR):
                j = i % 2
                if i >= 2:
                    s.wait_ge(s_a1, 2 * i - 3)
                s.dma_start(out=gt[j][:], in_=gateT[i * 128:(i + 1) * 128, :])\
                    .then_inc(s_l[j], 16)
        blk.sync(sync_b)

        def dve_pre(v):
            v.wait_ge(s_ld0, 80)
        blk.vector(dve_pre)
        _scale_prep(nc, blk, cs, onesf_s, lm8, srow, sb, psb, s_sc, s_bc, 3)

        def act_all(a):
            a.wait_ge(s_bc, 2)
            for i in range(NR + 1):
                if i < NR:
                    j = i % 2
                    a.wait_ge(s_l[j], 16 * (i // 2 + 1))
                    if i >= 2:
                        a.wait_ge(s_v1, 4 * (i - 2) + 3)
                    a.activation(ygf[j][:], gt[j][:], AF.Copy,
                                 scale=sb[:, 1:2]).then_inc(s_a1, 1)
                    a.wait_ge(s_a1, 2 * i + 1)
                    a.activation(gqi[j][:], ygf[j][:], AF.Copy)\
                        .then_inc(s_a1, 1)
                if i >= 1:
                    k = i - 1
                    jk = k % 2
                    a.wait_ge(s_v1, 4 * k + 4)
                    if k >= 2:
                        a.wait_ge(s_v2, 4 * (k - 2) + 3)
                    a.activation(sig[jk][:], gq[jk][:], AF.Sigmoid)\
                        .then_inc(s_a2, 1)
                    a.wait_ge(s_a2, 3 * k + 1)
                    a.activation(yuf[jk][:], sig[jk][:], AF.Copy,
                                 scale=cc(cs, "c65536")).then_inc(s_a2, 1)
                    a.wait_ge(s_a2, 3 * k + 2)
                    a.activation(qi[jk][:], yuf[jk][:], AF.Copy)\
                        .then_inc(s_a2, 1)
        blk.scalar(act_all)

        def dve_all(v):
            for i in range(NR + 1):
                if i < NR:
                    j = i % 2
                    v.wait_ge(s_a1, 2 * i + 2)
                    if i >= 2:
                        v.wait_ge(s_a2, 3 * (i - 2) + 1)
                        v.wait_ge(s_v4, i - 1)
                    dve_round4(v, gqd[j][:], ygf[j][:], gqi[j][:], gq[j][:],
                               sb, s_v1, 4 * i)
                if i >= 1:
                    k = i - 1
                    jk = k % 2
                    v.wait_ge(s_a2, 3 * k + 3)
                    v.scalar_tensor_tensor(qd[jk][:], qi[jk][:], -1.0,
                                           yuf[jk][:], ALU.mult, ALU.add)\
                        .then_inc(s_v2, 1)
                    v.wait_ge(s_v2, 4 * k + 1)
                    v.scalar_tensor_tensor(yuf[jk][:], qd[jk][:], 0.5,
                                           qi[jk][:], ALU.is_ge, ALU.add)\
                        .then_inc(s_v2, 1)
                    v.wait_ge(s_v2, 4 * k + 2)
                    v.scalar_tensor_tensor(qd[jk][:], qd[jk][:], -0.5,
                                           yuf[jk][:], ALU.is_ge, ALU.add)\
                        .then_inc(s_v2, 1)
                    v.wait_ge(s_v2, 4 * k + 3)
                    v.tensor_scalar(sfin[jk][:], qd[jk][:], cc(cs, "c65536"),
                                    cc(cs, "sc216"), ALU.min, ALU.mult)\
                        .then_inc(s_v2, 1)
                    v.wait_ge(s_v2, 4 * k + 4)
                    if k >= 2:
                        v.wait_ge(s_sb[jk], 16 * (k // 2))
                    v.scalar_tensor_tensor(g2o[jk][:], sfin[jk][:],
                                           cc(cs, "nsc216"), gq[jk][:],
                                           ALU.add, ALU.mult).then_inc(s_v4, 1)
                    v.wait_ge(s_v4, k + 1)
                    v.reduce_max(mxcol[:, k:k + 1], g2o[jk][:], axis=AX.X,
                                 apply_absolute_value=True).then_inc(s_rd, 1)
        blk.vector(dve_all)

        def g_st(g):
            for i in range(NR):
                j = i % 2
                g.wait_ge(s_rd, i + 1)
                g.dma_start(out=g2T[i * 128:(i + 1) * 128, :],
                            in_=g2o[j][:]).then_inc(s_sb[j], 16)
        blk.gpsimd(g_st)

        def gw(g):
            for j in range(2):
                g.wait_ge(s_sb[j], 16 * (NR // 2))
        _xpart_max_store(nc, blk, idf_s, mxcol, NR, sc128, psT, lmg2, lm11,
                         s_rd, NR, s_f, s_fd, g_extra_wait=gw)

    return nc


def build_L4():
    nc = bass.Bass("TRN2", target_bir_lowering=False)
    g2Tin = nc.dram_tensor("g2Tin", [ID, TOK], BF16, kind="ExternalInput").ap()
    upTin = nc.dram_tensor("upTin", [ID, TOK], BF16, kind="ExternalInput").ap()
    lmg28 = nc.dram_tensor("lmg28", [1, 8], F32, kind="ExternalInput").ap()
    lmu8 = nc.dram_tensor("lmu8", [1, 8], F32, kind="ExternalInput").ap()
    consts, onesb, identf, onesf = common_inputs(nc)
    oT = nc.dram_tensor("oT", [ID, TOK], BF16, kind="ExternalOutput").ap()
    lmo = nc.dram_tensor("lmo", [1, 1], F32, kind="ExternalOutput").ap()

    NR = ID // 128

    with ExitStack() as est:
        e = est.enter_context
        cs = e(nc.sbuf_tensor("cs", [128, NCONST], F32))
        ones_s = e(nc.sbuf_tensor("ones_s", [128, 128], BF16))
        onesf_s = e(nc.sbuf_tensor("onesf_s", [128, 128], F32))
        idf_s = e(nc.sbuf_tensor("idf_s", [128, 128], F32))
        lmg = e(nc.sbuf_tensor("lmg", [1, 8], F32))
        lmu = e(nc.sbuf_tensor("lmu", [1, 8], F32))
        srow = e(nc.sbuf_tensor("srow", [1, 4], F32))
        srow2 = e(nc.sbuf_tensor("srow2", [1, 4], F32))
        sbg = e(nc.sbuf_tensor("sbg", [128, 3], F32))
        sbu = e(nc.sbuf_tensor("sbu", [128, 3], F32))
        g2t = [e(nc.sbuf_tensor(f"g2t{i}", [128, TOK], BF16)) for i in range(2)]
        upt = [e(nc.sbuf_tensor(f"upt{i}", [128, TOK], BF16)) for i in range(2)]
        g2i = [e(nc.sbuf_tensor(f"g2i{i}", [128, TOK], I32)) for i in range(2)]
        upi = [e(nc.sbuf_tensor(f"upi{i}", [128, TOK], I32)) for i in range(2)]
        g2f = [e(nc.sbuf_tensor(f"g2f{i}", [128, TOK], F32)) for i in range(2)]
        upf = [e(nc.sbuf_tensor(f"upf{i}", [128, TOK], F32)) for i in range(2)]
        qdd = e(nc.sbuf_tensor("qdd", [128, TOK], F32))
        g2q = [e(nc.sbuf_tensor(f"g2q{i}", [128, TOK], F32)) for i in range(2)]
        upq = [e(nc.sbuf_tensor(f"upq{i}", [128, TOK], F32)) for i in range(2)]
        oo = [e(nc.sbuf_tensor(f"oo{i}", [128, TOK], BF16)) for i in range(2)]
        mxcol = e(nc.sbuf_tensor("mxcol", [128, NR], F32))
        sc128 = e(nc.sbuf_tensor("sc128", [128, 128], F32))
        lm11 = e(nc.sbuf_tensor("lm11", [1, 1], F32))
        psb = e(nc.psum_tensor("psb", [128, 128], F32))
        psb2 = e(nc.psum_tensor("psb2", [128, 128], F32))
        psT = e(nc.psum_tensor("psT", [128, 128], F32))

        s_ld0 = e(nc.semaphore("s_ld0"))
        s_sc = e(nc.semaphore("s_sc"))
        s_bc = e(nc.semaphore("s_bc"))
        s_sc2 = e(nc.semaphore("s_sc2"))
        s_bc2 = e(nc.semaphore("s_bc2"))
        s_l = [e(nc.semaphore(f"s_l{j}")) for j in range(2)]
        s_a1 = e(nc.semaphore("s_a1"))
        s_v1 = e(nc.semaphore("s_v1"))
        s_v2 = e(nc.semaphore("s_v2"))
        s_rd = e(nc.semaphore("s_rd"))
        s_sb = [e(nc.semaphore(f"s_sb{j}")) for j in range(2)]
        s_f = e(nc.semaphore("s_f"))
        s_fd = e(nc.semaphore("s_fd"))
        blk = e(nc.Block())

        def sync_b(s):
            s.dma_start(out=cs[:], in_=consts).then_inc(s_ld0, 16)
            s.dma_start(out=ones_s[:], in_=onesb).then_inc(s_ld0, 16)
            s.dma_start(out=onesf_s[:], in_=onesf).then_inc(s_ld0, 16)
            s.dma_start(out=idf_s[:], in_=identf).then_inc(s_ld0, 16)
            s.dma_start(out=lmg[:], in_=lmg28).then_inc(s_ld0, 16)
            s.dma_start(out=lmu[:], in_=lmu8).then_inc(s_ld0, 16)
            for i in range(NR):
                j = i % 2
                if i >= 2:
                    s.wait_ge(s_a1, 4 * i - 5)
                s.dma_start(out=g2t[j][:],
                            in_=g2Tin[i * 128:(i + 1) * 128, :]).then_inc(s_l[j], 16)
                s.dma_start(out=upt[j][:],
                            in_=upTin[i * 128:(i + 1) * 128, :]).then_inc(s_l[j], 16)
        blk.sync(sync_b)

        def dve_pre(v):
            v.wait_ge(s_ld0, 96)
        blk.vector(dve_pre)
        _scale_prep(nc, blk, cs, onesf_s, lmg, srow, sbg, psb, s_sc, s_bc, 3)
        _scale_prep(nc, blk, cs, onesf_s, lmu, srow2, sbu, psb2, s_sc2, s_bc2, 3)

        def act1(a):
            a.wait_ge(s_bc, 2)
            a.wait_ge(s_bc2, 2)
            for i in range(NR):
                j = i % 2
                a.wait_ge(s_l[j], 32 * (i // 2 + 1))
                if i >= 2:
                    a.wait_ge(s_v1, 8 * (i - 2) + 7)
                a.activation(g2f[j][:], g2t[j][:], AF.Copy,
                             scale=sbg[:, 1:2]).then_inc(s_a1, 1)
                a.wait_ge(s_a1, 4 * i + 1)
                a.activation(g2i[j][:], g2f[j][:], AF.Copy).then_inc(s_a1, 1)
                a.wait_ge(s_a1, 4 * i + 2)
                a.activation(upf[j][:], upt[j][:], AF.Copy,
                             scale=sbu[:, 1:2]).then_inc(s_a1, 1)
                a.wait_ge(s_a1, 4 * i + 3)
                a.activation(upi[j][:], upf[j][:], AF.Copy).then_inc(s_a1, 1)
        blk.scalar(act1)

        def dve1(v):
            for i in range(NR):
                j = i % 2
                v.wait_ge(s_a1, 4 * i + 2)
                if i >= 1:
                    v.wait_ge(s_v1, 8 * i)
                if i >= 2:
                    v.wait_ge(s_v2, i - 1)
                dve_round4(v, qdd[:], g2f[j][:], g2i[j][:], g2q[j][:],
                           sbg, s_v1, 8 * i)
                v.wait_ge(s_a1, 4 * i + 4)
                v.wait_ge(s_v1, 8 * i + 4)
                dve_round4(v, qdd[:], upf[j][:], upi[j][:], upq[j][:],
                           sbu, s_v1, 8 * i + 4)
                v.wait_ge(s_v1, 8 * (i + 1))
                if i >= 2:
                    v.wait_ge(s_sb[j], 16 * (i // 2))
                v.tensor_tensor(oo[j][:], g2q[j][:], upq[j][:],
                                ALU.mult).then_inc(s_v2, 1)
                v.wait_ge(s_v2, i + 1)
                v.reduce_max(mxcol[:, i:i + 1], oo[j][:], axis=AX.X,
                             apply_absolute_value=True).then_inc(s_rd, 1)
        blk.vector(dve1)

        def g_st(g):
            for i in range(NR):
                j = i % 2
                g.wait_ge(s_rd, i + 1)
                g.dma_start(out=oT[i * 128:(i + 1) * 128, :],
                            in_=oo[j][:]).then_inc(s_sb[j], 16)
        blk.gpsimd(g_st)

        def gw(g):
            for j in range(2):
                g.wait_ge(s_sb[j], 16 * (NR // 2))
        _xpart_max_store(nc, blk, idf_s, mxcol, NR, sc128, psT, lmo, lm11,
                         s_rd, NR, s_f, s_fd, g_extra_wait=gw)

    return nc


def build_L5():
    nc = bass.Bass("TRN2", target_bir_lowering=False)
    oTin = nc.dram_tensor("oTin", [ID, TOK], BF16, kind="ExternalInput").ap()
    lmo8 = nc.dram_tensor("lmo8", [1, 8], F32, kind="ExternalInput").ap()
    wdowndq = nc.dram_tensor("wdowndq", [CM, ID], BF16, kind="ExternalInput").ap()
    x2in = nc.dram_tensor("x2in", [TOK, CM], F32, kind="ExternalInput").ap()
    consts, onesb, identf, onesf = common_inputs(nc)
    yout = nc.dram_tensor("yout", [TOK, CM], F32, kind="ExternalOutput").ap()

    NR = ID // 128
    NT = TOK // 128

    with ExitStack() as est:
        e = est.enter_context
        cs = e(nc.sbuf_tensor("cs", [128, NCONST], F32))
        ones_s = e(nc.sbuf_tensor("ones_s", [128, 128], BF16))
        onesf_s = e(nc.sbuf_tensor("onesf_s", [128, 128], F32))
        idf_s = e(nc.sbuf_tensor("idf_s", [128, 128], F32))
        lm8 = e(nc.sbuf_tensor("lm8", [1, 8], F32))
        srow = e(nc.sbuf_tensor("srow", [1, 4], F32))
        sb = e(nc.sbuf_tensor("sb", [128, 3], F32))
        ot = [e(nc.sbuf_tensor(f"ot{i}", [128, TOK], BF16)) for i in range(2)]
        oi = [e(nc.sbuf_tensor(f"oi{i}", [128, TOK], I32)) for i in range(2)]
        of = [e(nc.sbuf_tensor(f"of{i}", [128, TOK], F32)) for i in range(2)]
        od = e(nc.sbuf_tensor("od", [128, TOK], F32))
        oq = [e(nc.sbuf_tensor(f"oq{i}", [128, TOK], BF16)) for i in range(NR)]
        wdT = [e(nc.sbuf_tensor(f"wdT{i}", [128, CM], BF16)) for i in range(NR)]
        x2s = [e(nc.sbuf_tensor(f"x2s{i}", [128, CM], F32)) for i in range(2)]
        stats = e(nc.sbuf_tensor("stats", [128, 16], F32))
        fsb = [of[0][:, 0:CM], of[1][:, 0:CM]]
        yo = [of[0][:, CM:2 * CM], of[1][:, CM:2 * CM]]
        dump = od
        psb = e(nc.psum_tensor("psb", [128, 128], F32))
        pm = [e(nc.psum_tensor(f"pm{i}", [128, CM], F32)) for i in range(2)]

        s_ld0 = e(nc.semaphore("s_ld0"))
        s_sc = e(nc.semaphore("s_sc"))
        s_bc = e(nc.semaphore("s_bc"))
        s_l = [e(nc.semaphore(f"s_l{j}")) for j in range(2)]
        s_wl = e(nc.semaphore("s_wl"))
        s_xl = [e(nc.semaphore(f"s_xl{j}")) for j in range(2)]
        s_a1 = e(nc.semaphore("s_a1"))
        s_a1x = e(nc.semaphore("s_a1x"))
        s_v1 = e(nc.semaphore("s_v1"))
        s_mm = e(nc.semaphore("s_mm"))
        s_v2 = e(nc.semaphore("s_v2"))
        s_a2 = e(nc.semaphore("s_a2"))
        s_v3 = e(nc.semaphore("s_v3"))
        s_sb = [e(nc.semaphore(f"s_sb{j}")) for j in range(2)]
        blk = e(nc.Block())

        def sync_b(s):
            s.dma_start(out=cs[:], in_=consts).then_inc(s_ld0, 16)
            s.dma_start(out=ones_s[:], in_=onesb).then_inc(s_ld0, 16)
            s.dma_start(out=onesf_s[:], in_=onesf).then_inc(s_ld0, 16)
            s.dma_start(out=idf_s[:], in_=identf).then_inc(s_ld0, 16)
            s.dma_start(out=lm8[:], in_=lmo8).then_inc(s_ld0, 16)
            for c in range(NR):
                s.dma_start(out=wdT[c][:], in_=wdowndq[:, c * 128:(c + 1) * 128],
                            transpose=True).then_inc(s_wl, 16)
            for i in range(NR):
                j = i % 2
                if i >= 2:
                    s.wait_ge(s_a1, 2 * i - 3)
                s.dma_start(out=ot[j][:], in_=oTin[i * 128:(i + 1) * 128, :])\
                    .then_inc(s_l[j], 16)
            for tb in range(NT):
                j = tb % 2
                if tb >= 2:
                    s.wait_ge(s_v3, 2 * (tb - 1))
                s.dma_start(out=x2s[j][:],
                            in_=x2in[tb * 128:(tb + 1) * 128, :]).then_inc(s_xl[j], 16)
        blk.sync(sync_b)

        def dve_pre(v):
            v.wait_ge(s_ld0, 80)
        blk.vector(dve_pre)
        _scale_prep(nc, blk, cs, onesf_s, lm8, srow, sb, psb, s_sc, s_bc, 3)

        def act1(a):
            a.wait_ge(s_bc, 2)
            for i in range(NR):
                j = i % 2
                a.wait_ge(s_l[j], 16 * (i // 2 + 1))
                if i >= 2:
                    a.wait_ge(s_v1, 4 * (i - 2) + 3)
                a.activation(of[j][:], ot[j][:], AF.Copy,
                             scale=sb[:, 1:2]).then_inc(s_a1, 1)
                a.wait_ge(s_a1, 2 * i + 1)
                a.activation(oi[j][:], of[j][:], AF.Copy).then_inc(s_a1, 1)
        blk.scalar(act1)

        def dve1(v):
            for i in range(NR):
                j = i % 2
                v.wait_ge(s_a1, 2 * i + 2)
                if i >= 1:
                    v.wait_ge(s_v1, 4 * i)
                dve_round4(v, od[:], of[j][:], oi[j][:], oq[i][:],
                           sb, s_v1, 4 * i)
        blk.vector(dve1)

        def pe_mm(t):
            t.wait_ge(s_wl, 16 * NR)
            t.wait_ge(s_v1, 4 * NR)
            for tb in range(NT):
                if tb >= 2:
                    t.wait_ge(s_a1x, tb - 1)
                mm = None
                for half in range(2):
                    for c in range(NR):
                        mm = t.matmul(pm[tb % 2][:, half * 512:(half + 1) * 512],
                                      oq[c][:, tb * 128:(tb + 1) * 128],
                                      wdT[c][:, half * 512:(half + 1) * 512],
                                      start=(c == 0), stop=(c == NR - 1))
                mm.then_inc(s_mm, 1)
        blk.tensor(pe_mm)

        def act_tail(a):
            for i in range(NT + 1):
                if i < NT:
                    tb = i
                    a.wait_ge(s_mm, tb + 1)
                    if tb >= 2:
                        a.wait_ge(s_v3, 2 * (tb - 1) - 1)
                    a.activation(fsb[tb % 2], pm[tb % 2][:], AF.Copy)\
                        .then_inc(s_a1x, 1)
                if i >= 1:
                    k = i - 1
                    kc = 8 * (k % 2)
                    a.wait_ge(s_v2, 7 * (k + 1))
                    a.activation(stats[:, kc + 6:kc + 7], stats[:, kc + 5:kc + 6],
                                 AF.Ln, bias=cc(cs, "ln_eps")).then_inc(s_a2, 1)
                    a.wait_ge(s_a2, 2 * k + 1)
                    a.activation(stats[:, kc + 7:kc + 8], stats[:, kc + 6:kc + 7],
                                 AF.Exp, scale=cc(cs, "neg_half")).then_inc(s_a2, 1)
        blk.scalar(act_tail)

        def dve_tail(v):
            for i in range(NT + 1):
                if i < NT:
                    tb = i
                    tc = 8 * (tb % 2)
                    base = 7 * tb
                    v.wait_ge(s_a1x, tb + 1)
                    if tb >= 2:
                        v.wait_ge(s_v3, 2 * (tb - 1))
                    v.reduce_sum(stats[:, tc + 0:tc + 1], fsb[tb % 2], axis=AX.X)\
                        .then_inc(s_v2, 1)
                    v.scalar_tensor_tensor(dump[:, 8:8 + CM], fsb[tb % 2], 1.0,
                                           fsb[tb % 2], ALU.bypass, ALU.mult,
                                           accum_out=stats[:, tc + 1:tc + 2])\
                        .then_inc(s_v2, 1)
                    v.wait_ge(s_v2, base + 2)
                    v.tensor_scalar(stats[:, tc + 2:tc + 3], stats[:, tc + 0:tc + 1],
                                    cc(cs, "inv_cm"), None, ALU.mult).then_inc(s_v2, 1)
                    v.wait_ge(s_v2, base + 3)
                    v.tensor_copy(dump[:, 0:8], cs[:, 0:8])
                    v.tensor_scalar(stats[:, tc + 3:tc + 4], stats[:, tc + 2:tc + 3],
                                    stats[:, tc + 2:tc + 3], cc(cs, "neg_one"),
                                    ALU.mult, ALU.mult).then_inc(s_v2, 1)
                    v.wait_ge(s_v2, base + 4)
                    v.tensor_scalar(stats[:, tc + 4:tc + 5], stats[:, tc + 1:tc + 2],
                                    cc(cs, "inv_cm"), None, ALU.mult).then_inc(s_v2, 1)
                    v.wait_ge(s_v2, base + 5)
                    v.tensor_copy(dump[:, 0:8], cs[:, 0:8])
                    v.tensor_tensor(stats[:, tc + 5:tc + 6], stats[:, tc + 4:tc + 5],
                                    stats[:, tc + 3:tc + 4], ALU.add).then_inc(s_v2, 2)
                if i >= 1:
                    k = i - 1
                    jk = k % 2
                    kc = 8 * jk
                    v.wait_ge(s_a2, 2 * (k + 1))
                    v.wait_ge(s_xl[jk], 16 * (k // 2 + 1))
                    if k >= 2:
                        v.wait_ge(s_sb[jk], 16 * (k // 2))
                    v.tensor_scalar(yo[jk], fsb[jk], stats[:, kc + 2:kc + 3],
                                    stats[:, kc + 7:kc + 8], ALU.subtract, ALU.mult)\
                        .then_inc(s_v3, 1)
                    v.wait_ge(s_v3, 2 * k + 1)
                    v.tensor_tensor(yo[jk], yo[jk], x2s[jk][:],
                                    ALU.add).then_inc(s_v3, 1)
        blk.vector(dve_tail)

        def g_st(g):
            for tb in range(NT):
                j = tb % 2
                g.wait_ge(s_v3, 2 * (tb + 1))
                g.dma_start(out=yout[tb * 128:(tb + 1) * 128, :],
                            in_=yo[j]).then_inc(s_sb[j], 16)
            for j in range(2):
                g.wait_ge(s_sb[j], 16 * (NT // 2))
        blk.gpsimd(g_st)

    return nc


def blocks_for_half(h):
    pairs = [(k, 31 - k) for k in range(16)]
    sel = []
    for i, (a, b) in enumerate(pairs):
        if i % 2 == h:
            sel += [a, b]
    return sorted(sel)


def core_rows(c):
    h = c % 2
    rows = []
    for blkid in blocks_for_half(h):
        rows += list(range(blkid * 128, (blkid + 1) * 128))
    return rows


"""Builder for launch L1 of the sandwich kernel (per-core, parity h).

LN1, QKV, forgetting-attention (psum-resident FF cumsum, window-min bisection
pruning threshold, E=exp(-FF)*keep), bounded-softmax, PV, proj, LN2+residual,
LN3, FFN-weight dequant slices.

Host-transposed weight layouts (pure layout): watT = W_attn.T [1024,3072],
wpjT = W_proj.T [1024,1024]. headsel [128,2] f32: col0 = 1 on partitions
0..63, col1 = 1 on 64..127.
"""



TOK = 2048
CM = 1024
ID = 3072
T = 4096
NPAIR = 8


def ff_sched(h, pass_b):
    lo, hi = (4, 8) if pass_b else (0, 4)
    istart = 16 if pass_b else 0
    out = []
    for I in range(istart, 32):
        w = 128 * (I + 1)
        for jc in range(lo, min(hi, (w + 511) // 512)):
            nj = min(512, w - 512 * jc)
            out.append((I, jc, nj))
    return out


def build_L1(h, dbg=False):
    my_blocks = blocks_for_half(h)
    my_pos = {blk: i for i, blk in enumerate(my_blocks)}
    mine = set(my_blocks)
    icq = [my_blocks[4 * k:4 * k + 4] for k in range(4)]
    JCs = [max(bl) + 1 for bl in icq]
    myB = [b for b in my_blocks if b >= 16]
    myBpos = {b: i for i, b in enumerate(myB)}

    SCH = [(I, jc, nj, False) for (I, jc, nj) in ff_sched(h, False)] + \
          [(I, jc, nj, True) for (I, jc, nj) in ff_sched(h, True)]
    first_use, last_use = {}, {}
    for idx, (I, jc, nj, pb) in enumerate(SCH):
        key = (pb, jc)
        if key not in first_use:
            first_use[key] = idx
        last_use[key] = idx

    nc = bass.Bass("TRN2", target_bir_lowering=False)
    xbD = nc.dram_tensor("xb", [T, CM], F32, kind="ExternalInput").ap()
    watT = nc.dram_tensor("watT", [CM, 3 * CM], F32, kind="ExternalInput").ap()
    battn24 = nc.dram_tensor("battn24", [24, 128], F32, kind="ExternalInput").ap()
    wpjT = nc.dram_tensor("wpjT", [CM, CM], F32, kind="ExternalInput").ap()
    bvbD = nc.dram_tensor("bvb", [128, CM], F32, kind="ExternalInput").ap()
    bprojbD = nc.dram_tensor("bprojb", [128, CM], F32, kind="ExternalInput").ap()
    wup_sl = nc.dram_tensor("wup_sl", [384, CM], F32, kind="ExternalInput").ap()
    wgate_sl = nc.dram_tensor("wgate_sl", [384, CM], F32, kind="ExternalInput").ap()
    wdown_sl = nc.dram_tensor("wdown_sl", [128, ID], F32, kind="ExternalInput").ap()
    constsD = nc.dram_tensor("consts", [128, NCONST], F32, kind="ExternalInput").ap()
    onesbD = nc.dram_tensor("onesb", [128, 128], BF16, kind="ExternalInput").ap()
    identfD = nc.dram_tensor("identf", [128, 128], F32, kind="ExternalInput").ap()
    onesfD = nc.dram_tensor("onesf", [128, 128], F32, kind="ExternalInput").ap()
    cumTD = nc.dram_tensor("cumT", [128, 128], BF16, kind="ExternalInput").ap()
    lowincD = nc.dram_tensor("lowinc", [128, 128], BF16, kind="ExternalInput").ap()
    ltstrD = nc.dram_tensor("ltstr", [128, 128], BF16, kind="ExternalInput").ap()
    ltincD = nc.dram_tensor("ltinc", [128, 128], BF16, kind="ExternalInput").ap()
    headselD = nc.dram_tensor("headsel", [128, 2], BF16, kind="ExternalInput").ap()
    iotaD = nc.dram_tensor("iota32", [128, 32], F32, kind="ExternalInput").ap()

    x2o = nc.dram_tensor("x2o", [TOK, CM], F32, kind="ExternalOutput").ap()
    x3o = nc.dram_tensor("x3o", [TOK, CM], BF16, kind="ExternalOutput").ap()
    lmxo = nc.dram_tensor("lmxo", [1, 1], F32, kind="ExternalOutput").ap()
    wupdq_sl = nc.dram_tensor("wupdq_sl", [384, CM], BF16, kind="ExternalOutput").ap()
    wgatedq_sl = nc.dram_tensor("wgatedq_sl", [384, CM], BF16, kind="ExternalOutput").ap()
    wdowndq_sl = nc.dram_tensor("wdowndq_sl", [128, ID], BF16, kind="ExternalOutput").ap()
    if dbg:
        dbg_x1 = nc.dram_tensor("dbg_x1", [T, CM], BF16, kind="ExternalOutput").ap()
        dbg_E = nc.dram_tensor("dbg_E", [TOK, T], BF16, kind="ExternalOutput").ap()
        dbg_y = nc.dram_tensor("dbg_y", [TOK, CM], BF16, kind="ExternalOutput").ap()
        dbg_kT = nc.dram_tensor("dbg_kT", [CM, T], BF16, kind="ExternalOutput").ap()
        dbg_m = nc.dram_tensor("dbg_m", [1, 1], F32, kind="ExternalOutput").ap()

    x1d = nc.dram_tensor("x1d", [T, CM], BF16, kind="Internal").ap()
    qTd = nc.dram_tensor("qTd", [CM, T], BF16, kind="Internal").ap()
    kTd = nc.dram_tensor("kTd", [CM, T], BF16, kind="Internal").ap()
    Ed = nc.dram_tensor("Ed", [TOK, T], BF16, kind="Internal").ap()
    ffAd = nc.dram_tensor("ffAd", [8 * 128, TOK], F32, kind="Internal").ap()
    ydD = nc.dram_tensor("yd", [TOK, CM], BF16, kind="Internal").ap()

    est = ExitStack()
    with est:
        e = est.enter_context

        def sem(name):
            return e(nc.semaphore(name))

        cs = e(nc.sbuf_tensor("cs", [128, NCONST], F32))
        ones_s = e(nc.sbuf_tensor("ones_s", [128, 128], BF16))
        onesf_s = e(nc.sbuf_tensor("onesf_s", [128, 128], F32))
        idf_s = e(nc.sbuf_tensor("idf_s", [128, 128], F32))
        cum_s = e(nc.sbuf_tensor("cum_s", [128, 128], BF16))
        low_s = e(nc.sbuf_tensor("low_s", [128, 128], BF16))
        lts_s = e(nc.sbuf_tensor("lts_s", [128, 128], BF16))
        lti_s = e(nc.sbuf_tensor("lti_s", [128, 128], BF16))
        hsel_s = e(nc.sbuf_tensor("hsel_s", [128, 2], BF16))
        iota_s = e(nc.sbuf_tensor("iota_s", [128, 32], F32))
        bT_s = e(nc.sbuf_tensor("bT_s", [128, 24], F32))
        bvb_s = e(nc.sbuf_tensor("bvb_s", [128, CM], F32))
        bpb_s = e(nc.sbuf_tensor("bpb_s", [128, CM], F32))
        ebias = e(nc.sbuf_tensor("ebias", [128, 1], F32))
        mbc = e(nc.sbuf_tensor("mbc", [128, 144], F32))
        sc128 = e(nc.sbuf_tensor("sc128", [128, 128], F32))
        mrow = e(nc.sbuf_tensor("mrow", [2, 8], F32))
        ldump = e(nc.sbuf_tensor("ldump", [128, 8 + CM], F32))
        zer_s = e(nc.sbuf_tensor("zer_s", [128, 512], BF16))
        lm11 = e(nc.sbuf_tensor("lm11", [1, 1], F32))

        s_c0 = sem("s_c0")
        s_bt = sem("s_bt")
        s_dql = [sem(f"s_dql{j}") for j in range(2)]
        s_dqw = sem("s_dqw"); s_dqc = sem("s_dqc"); s_dqi = sem("s_dqi")
        s_dqr = sem("s_dqr"); s_dqb = sem("s_dqb")
        s_dqs = [sem(f"s_dqs{j}") for j in range(2)]
        s_x0 = [sem(f"s_x0{j}") for j in range(2)]
        s_ssq = sem("s_ssq")
        s_rstd = sem("s_rstd"); s_app = sem("s_app")
        s_l1s = [sem(f"s_l1s{j}") for j in range(2)]
        s_wl = [sem(f"s_wl{j}") for j in range(2)]
        s_wc = sem("s_wc")
        s_xT = sem("s_xT")
        s_kv = sem("s_kv")
        s_kve = [sem(f"s_kve{j}") for j in range(2)]
        s_krd = sem("s_krd"); s_qrd = sem("s_qrd")
        s_kst = [sem(f"s_kst{j}") for j in range(2)]
        s_qst = [sem(f"s_qst{j}") for j in range(2)]
        s_mb1 = sem("s_mb1"); s_mb2 = sem("s_mb2"); s_mb3 = sem("s_mb3")
        s_mb4 = sem("s_mb4"); s_mb5 = sem("s_mb5"); s_mb6 = sem("s_mb6")
        s_mb7 = sem("s_mb7"); s_mb8 = sem("s_mb8")
        s_q0l = sem("s_q0l"); s_k0l = sem("s_k0l")
        s_s0w = [sem(f"s_s0w{j}") for j in range(2)]
        s_s0r = [sem(f"s_s0r{j}") for j in range(2)]
        s_sm = sem("s_sm")
        s_ff1 = sem("s_ff1"); s_ffe = sem("s_ffe")
        s_cs = sem("s_cs"); s_cv = sem("s_cv")
        s_wmn = sem("s_wmn"); s_bis = sem("s_bis"); s_th = sem("s_th")
        s_ein = sem("s_ein"); s_ee0 = sem("s_ee0"); s_eml = sem("s_eml")
        s_est = sem("s_est")
        s_ktl = sem("s_ktl")
        s_el = sem("s_el"); s_ql = sem("s_ql")
        s_qkd = sem("s_qkd")
        s_ex = [sem(f"s_ex{j}") for j in range(2)]
        s_pm = [sem(f"s_pm{j}") for j in range(2)]
        s_pv = [sem(f"s_pv{j}") for j in range(2)]
        s_ye = [sem(f"s_ye{j}") for j in range(2)]
        s_yn = [sem(f"s_yn{j}") for j in range(2)]
        s_ys = [sem(f"s_ys{j}") for j in range(4)]
        s_ytl = [sem(f"s_ytl{j}") for j in range(2)]
        s_pj = sem("s_pj"); s_bia = sem("s_bia")
        s_xol = [sem(f"s_xol{j}") for j in range(2)]
        s_sq2 = sem("s_sq2")
        s_rs2 = sem("s_rs2"); s_re2 = sem("s_re2")
        s_sq3 = sem("s_sq3")
        s_rs3 = sem("s_rs3"); s_ap3 = sem("s_ap3"); s_mx3 = sem("s_mx3")
        s_o2 = [sem(f"s_o2{j}") for j in range(2)]
        s_o3 = [sem(f"s_o3{j}") for j in range(2)]
        s_fx = sem("s_fx"); s_fxd = sem("s_fxd")

        blk = e(nc.Block())

        def _stats(v, st, dst, src_t, base, s_cnt):
            v.reduce_sum(st[:, dst + 0:dst + 1], src_t[:], axis=AX.X)\
                .then_inc(s_cnt, 1)
            v.scalar_tensor_tensor(ldump[:, 8:8 + CM], src_t[:], 1.0, src_t[:],
                                   ALU.bypass, ALU.mult,
                                   accum_out=st[:, dst + 1:dst + 2])\
                .then_inc(s_cnt, 1)
            v.wait_ge(s_cnt, base + 2)
            v.tensor_scalar(st[:, dst + 2:dst + 3], st[:, dst + 0:dst + 1],
                            cc(cs, "inv_cm"), None, ALU.mult).then_inc(s_cnt, 1)
            v.wait_ge(s_cnt, base + 3)
            v.tensor_copy(ldump[:, 0:8], cs[:, 0:8])
            v.tensor_scalar(st[:, dst + 3:dst + 4], st[:, dst + 2:dst + 3],
                            st[:, dst + 2:dst + 3], cc(cs, "neg_one"),
                            ALU.mult, ALU.mult).then_inc(s_cnt, 1)
            v.wait_ge(s_cnt, base + 4)
            v.tensor_scalar(st[:, dst + 4:dst + 5], st[:, dst + 1:dst + 2],
                            cc(cs, "inv_cm"), None, ALU.mult).then_inc(s_cnt, 1)
            v.wait_ge(s_cnt, base + 5)
            v.tensor_copy(ldump[:, 0:8], cs[:, 0:8])
            v.tensor_tensor(st[:, dst + 5:dst + 6], st[:, dst + 4:dst + 5],
                            st[:, dst + 3:dst + 4], ALU.add).then_inc(s_cnt, 2)

        with ExitStack() as sc0:
            e0 = sc0.enter_context
            b24_s = e0(nc.sbuf_tensor("b24_s", [24, 128], F32))
            dq_f = [e0(nc.sbuf_tensor(f"dq_f{j}", [128, ID], F32)) for j in range(2)]
            dq_sc = e0(nc.sbuf_tensor("dq_sc", [128, ID // 32], F32))
            dq_si = e0(nc.sbuf_tensor("dq_si", [128, ID // 32], F32))
            dq_i = e0(nc.sbuf_tensor("dq_i", [128, ID], I32))
            dq_d = e0(nc.sbuf_tensor("dq_d", [128, ID], F32))
            dq_ns = e0(nc.sbuf_tensor("dq_ns", [128, ID // 32], F32))
            dq_b = [e0(nc.sbuf_tensor(f"dq_b{j}", [128, ID], BF16)) for j in range(2)]
            xt = [e0(nc.sbuf_tensor(f"xt{j}", [128, CM], F32)) for j in range(2)]
            x1t = [e0(nc.sbuf_tensor(f"x1t{j}", [128, CM], BF16)) for j in range(2)]
            lst = e0(nc.sbuf_tensor("lst", [128, 16], F32))
            px = e0(nc.psum_tensor("px", [128, 512], F32))

            def sy01(s):
                for ap_, dst in ((constsD, cs), (onesbD, ones_s), (onesfD, onesf_s),
                                 (identfD, idf_s), (cumTD, cum_s), (lowincD, low_s),
                                 (ltstrD, lts_s), (ltincD, lti_s), (headselD, hsel_s),
                                 (iotaD, iota_s), (bvbD, bvb_s), (bprojbD, bpb_s),
                                 (battn24, b24_s)):
                    s.dma_start(out=dst[:], in_=ap_).then_inc(s_c0, 16)
                for i in range(7):
                    j = i % 2
                    if i >= 2:
                        s.wait_ge(s_dqr, 6 * i - 7)
                    if i < 3:
                        s.dma_start(out=dq_f[j][:, 0:CM],
                                    in_=wup_sl[i * 128:(i + 1) * 128, :])\
                            .then_inc(s_dql[j], 16)
                    elif i < 6:
                        s.dma_start(out=dq_f[j][:, 0:CM],
                                    in_=wgate_sl[(i - 3) * 128:(i - 2) * 128, :])\
                            .then_inc(s_dql[j], 16)
                    else:
                        s.dma_start(out=dq_f[j][:], in_=wdown_sl)\
                            .then_inc(s_dql[j], 16)
                for i in range(32):
                    j = i % 2
                    if i >= 2:
                        s.wait_ge(s_app, i - 1)
                    s.dma_start(out=xt[j][:], in_=xbD[i * 128:(i + 1) * 128, :])\
                        .then_inc(s_x0[j], 16)
            blk.sync(sy01)

            def pe01(t):
                t.wait_ge(s_c0, 16 * 13)
                t.transpose(px[:, 0:24], b24_s[:], idf_s[0:24, 0:24])\
                    .then_inc(s_bt, 1)
            blk.tensor(pe01)

            def act01(a):
                a.wait_ge(s_bt, 1)
                a.copy(bT_s[:], px[:, 0:24]).then_inc(s_mb8, 1)
                for k in range(32):
                    kc = 8 * (k % 2)
                    a.wait_ge(s_ssq, 7 * (k + 1))
                    a.activation(lst[:, kc + 6:kc + 7], lst[:, kc + 5:kc + 6],
                                 AF.Ln, bias=cc(cs, "ln_eps")).then_inc(s_rstd, 1)
                    a.wait_ge(s_rstd, 2 * k + 1)
                    a.activation(lst[:, kc + 7:kc + 8], lst[:, kc + 6:kc + 7],
                                 AF.Exp, scale=cc(cs, "neg_half")).then_inc(s_rstd, 1)
            blk.scalar(act01)

            def dve01(v):
                v.wait_ge(s_c0, 16 * 13)
                v.memset(zer_s[:], 0.0)
                for i in range(7):
                    j = i % 2
                    w = CM if i < 6 else ID
                    nwin = w // 32
                    v.wait_ge(s_dql[j], 16 * (i // 2 + 1))
                    if i >= 1:
                        v.wait_ge(s_dqb, i)
                    v.reduce_max(dq_sc[:, 0:nwin],
                                 dq_f[j][:, 0:w].rearrange("p (a b) -> p a b", b=32),
                                 axis=AX.X, apply_absolute_value=True)\
                        .then_inc(s_dqw, 1)
                    v.wait_ge(s_dqw, i + 1)
                    v.tensor_copy(ldump[:, 0:8], cs[:, 0:8])
                    v.tensor_scalar(dq_sc[:, 0:nwin], dq_sc[:, 0:nwin],
                                    cc(cs, "inv7"), cc(cs, "eps12"),
                                    ALU.mult, ALU.max).then_inc(s_dqc, 1)
                    v.wait_ge(s_dqc, i + 1)
                    v.tensor_copy(ldump[:, 0:8], cs[:, 0:8])
                    v.tensor_scalar(dq_ns[:, 0:nwin], dq_sc[:, 0:nwin], -1.0,
                                    None, ALU.mult)
                    v.reciprocal(dq_si[:, 0:nwin], dq_sc[:, 0:nwin])\
                        .then_inc(s_dqi, 1)
                    v.wait_ge(s_dqi, i + 1)
                    v.tensor_copy(ldump[:, 0:8], cs[:, 0:8])
                    si3 = dq_si[:, 0:nwin].rearrange("p (a o) -> p a o", o=1)\
                        .broadcast_to([128, nwin, 32])
                    v.tensor_tensor(dq_f[j][:, 0:w].rearrange("p (a b) -> p a b",
                                                              b=32),
                                    dq_f[j][:, 0:w].rearrange("p (a b) -> p a b",
                                                              b=32),
                                    si3, ALU.mult).then_inc(s_dqr, 1)
                    v.wait_ge(s_dqr, 6 * i + 1)
                    v.tensor_copy(ldump[:, 0:8], cs[:, 0:8])
                    v.tensor_scalar(dq_i[:, 0:w], dq_f[j][:, 0:w], 1.0, None,
                                    ALU.mult).then_inc(s_dqr, 1)
                    v.wait_ge(s_dqr, 6 * i + 2)
                    v.tensor_copy(ldump[:, 0:8], cs[:, 0:8])
                    v.scalar_tensor_tensor(dq_d[:, 0:w], dq_i[:, 0:w], -1.0,
                                           dq_f[j][:, 0:w], ALU.mult, ALU.add)\
                        .then_inc(s_dqr, 1)
                    v.wait_ge(s_dqr, 6 * i + 3)
                    v.tensor_copy(ldump[:, 0:8], cs[:, 0:8])
                    v.scalar_tensor_tensor(dq_f[j][:, 0:w], dq_d[:, 0:w], 0.5,
                                           dq_i[:, 0:w], ALU.is_ge, ALU.add)\
                        .then_inc(s_dqr, 1)
                    v.wait_ge(s_dqr, 6 * i + 4)
                    v.tensor_copy(ldump[:, 0:8], cs[:, 0:8])
                    v.scalar_tensor_tensor(dq_d[:, 0:w], dq_d[:, 0:w], -0.5,
                                           dq_f[j][:, 0:w], ALU.is_ge, ALU.add)\
                        .then_inc(s_dqr, 1)
                    v.wait_ge(s_dqr, 6 * i + 5)
                    if i >= 2:
                        v.wait_ge(s_dqs[j], 16 * (i // 2))
                    v.tensor_copy(ldump[:, 0:8], cs[:, 0:8])
                    sc3 = dq_sc[:, 0:nwin].rearrange("p (a o) -> p a o", o=1)\
                        .broadcast_to([128, nwin, 32])
                    nsc3 = dq_ns[:, 0:nwin].rearrange("p (a o) -> p a o", o=1)\
                        .broadcast_to([128, nwin, 32])
                    v.tensor_tensor(dq_d[:, 0:w].rearrange("p (a b) -> p a b",
                                                           b=32),
                                    dq_d[:, 0:w].rearrange("p (a b) -> p a b",
                                                           b=32),
                                    sc3, ALU.mult).then_inc(s_dqr, 1)
                    v.wait_ge(s_dqr, 6 * i + 6)
                    v.tensor_copy(ldump[:, 0:8], cs[:, 0:8])
                    v.tensor_tensor(dq_b[j][:, 0:w].rearrange("p (a b) -> p a b",
                                                              b=32),
                                    dq_d[:, 0:w].rearrange("p (a b) -> p a b",
                                                           b=32),
                                    nsc3, ALU.add).then_inc(s_dqb, 1)
                for i in range(33):
                    if i < 32:
                        tb = i
                        j = tb % 2
                        tc = 8 * j
                        v.wait_ge(s_x0[j], 16 * (tb // 2 + 1))
                        if tb >= 2:
                            v.wait_ge(s_app, tb - 1)
                        _stats(v, lst, tc, xt[j], 7 * tb, s_ssq)
                    if i >= 1:
                        k = i - 1
                        jk = k % 2
                        kc = 8 * jk
                        v.wait_ge(s_rstd, 2 * (k + 1))
                        if k >= 2:
                            v.wait_ge(s_l1s[jk], 16 * (k // 2))
                        v.tensor_scalar(x1t[jk][:], xt[jk][:], lst[:, kc + 2:kc + 3],
                                        lst[:, kc + 7:kc + 8], ALU.subtract,
                                        ALU.mult).then_inc(s_app, 1)
            blk.vector(dve01)

            def g01(g):
                for i in range(7):
                    j = i % 2
                    g.wait_ge(s_dqb, i + 1)
                    if i < 3:
                        g.dma_start(out=wupdq_sl[i * 128:(i + 1) * 128, :],
                                    in_=dq_b[j][:, 0:CM]).then_inc(s_dqs[j], 16)
                    elif i < 6:
                        g.dma_start(out=wgatedq_sl[(i - 3) * 128:(i - 2) * 128, :],
                                    in_=dq_b[j][:, 0:CM]).then_inc(s_dqs[j], 16)
                    else:
                        g.dma_start(out=wdowndq_sl, in_=dq_b[j][:])\
                            .then_inc(s_dqs[j], 16)
                for k in range(32):
                    jk = k % 2
                    g.wait_ge(s_app, k + 1)
                    g.dma_start(out=x1d[k * 128:(k + 1) * 128, :],
                                in_=x1t[jk][:]).then_inc(s_l1s[jk], 16)
            blk.gpsimd(g01)

        with ExitStack() as scA:
            eA = scA.enter_context
            vaug = eA(nc.sbuf_tensor("vaug", [128, 32 * 1040], BF16))

            with ExitStack() as sc2:
                e2 = sc2.enter_context
                waT = [e2(nc.sbuf_tensor(f"waT{c}", [128, 3 * CM], BF16))
                       for c in range(8)]
                wfl = [e2(nc.sbuf_tensor(f"wfl{j}", [128, CM], F32))
                       for j in range(2)]
                x1T = [e2(nc.sbuf_tensor(f"x1T{c}", [128, T], BF16))
                       for c in range(8)]
                kev = [e2(nc.sbuf_tensor(f"kev{j}", [128, 512], BF16))
                       for j in range(2)]
                sqb = [e2(nc.sbuf_tensor(f"sqb{j}", [128, 512], BF16))
                       for j in range(2)]
                qev = [e2(nc.sbuf_tensor(f"qev{j}", [128, 512], BF16))
                       for j in range(2)]
                pq = [e2(nc.psum_tensor(f"pq{j}", [128, 512], F32))
                      for j in range(2)]
                pxb = e2(nc.psum_tensor("pxb", [128, 512], F32))
                psq = [e2(nc.psum_tensor(f"psq{j}", [2, 512], F32))
                       for j in range(2)]

                TILES = ([("k", p, tc4) for p in range(NPAIR) for tc4 in range(8)]
                         + [("v", tb, hf) for tb in range(32) for hf in range(2)]
                         + [("q", p, tc4) for p in range(NPAIR) for tc4 in range(8)])

                def sy2(s):
                    s.wait_ge(s_dqs[0], 16 * 4)
                    s.wait_ge(s_dqs[1], 16 * 3)
                    s.wait_ge(s_l1s[0], 16 * 16)
                    s.wait_ge(s_l1s[1], 16 * 16)
                    for u in range(24):
                        c, part = u // 3, u % 3
                        j = u % 2
                        if u >= 2:
                            s.wait_ge(s_wc, u - 1)
                        s.dma_start(out=wfl[j][:],
                                    in_=watT[c * 128:(c + 1) * 128,
                                             part * CM:(part + 1) * CM])\
                            .then_inc(s_wl[j], 16)
                    for j in range(2):
                        s.wait_ge(s_l1s[j], 16 * 16)
                    for c in range(8):
                        s.dma_start(out=x1T[c][:],
                                    in_=x1d[:, c * 128:(c + 1) * 128],
                                    transpose=True).then_inc(s_xT, 16)
                blk.sync(sy2)

                def pe2(t):
                    t.wait_ge(s_mb8, 1)
                    t.wait_ge(s_wc, 24)
                    t.wait_ge(s_xT, 16 * 8)
                    for k, tile in enumerate(TILES):
                        kind = tile[0]
                        bank = k % 2
                        if k >= 2:
                            t.wait_ge(s_kve[bank], k // 2)
                        if kind in ("k", "q"):
                            _, p, tc4 = tile
                            off = CM if kind == "k" else 0
                            tkq = p * 8 + tc4 + (0 if kind == "k" else 64)
                            mm = None
                            for c in range(8):
                                mm = t.matmul(
                                    pq[bank][:],
                                    waT[c][:, off + 128 * p:off + 128 * p + 128],
                                    x1T[c][:, tc4 * 512:(tc4 + 1) * 512],
                                    start=(c == 0), stop=(c == 7))
                            mm.then_inc(s_kv, 1)
                            t.wait_ge(s_mb3, tkq + 1)
                            if tkq >= 2:
                                if tkq - 2 < 64:
                                    t.wait_ge(s_krd, tkq - 1)
                                else:
                                    t.wait_ge(s_qrd, tkq - 65)
                            t.matmul(psq[tkq % 2][0:2, :], hsel_s[:],
                                     sqb[tkq % 2][:], start=True, stop=True)\
                                .then_inc(s_mb5, 1)
                        else:
                            _, tb, hf = tile
                            mm = None
                            for c in range(8):
                                mm = t.matmul(
                                    pq[bank][:],
                                    x1T[c][:, tb * 128:(tb + 1) * 128],
                                    waT[c][:, 2 * CM + 512 * hf:2 * CM + 512 * hf + 512],
                                    start=(c == 0), stop=(c == 7))
                            mm.then_inc(s_kv, 1)
                    t.wait_ge(s_mb2, 2)
                    t.transpose(pxb[:, 64:192], sc128[:], idf_s[:])\
                        .then_inc(s_mb6, 1)
                    t.wait_ge(s_mb7, 2)
                    t.matmul(pxb[:, 256:257], onesf_s[0:1, 0:128], mrow[0:1, 5:6],
                             start=True, stop=True).then_inc(s_mb8, 1)
                blk.tensor(pe2)

                def act2(a):
                    for k, tile in enumerate(TILES):
                        kind = tile[0]
                        bank = k % 2
                        if kind == "v":
                            continue
                        a.wait_ge(s_kv, k + 1)
                        if kind == "k":
                            _, p, tc4 = tile
                            ki = p * 8 + tc4
                            if ki >= 2:
                                a.wait_ge(s_kst[ki % 2], 16 * (ki // 2))
                                a.wait_ge(s_mb3, ki - 1)
                            a.activation(kev[ki % 2][:], pq[bank][:], AF.Identity,
                                         bias=bT_s[:, 8 + p:9 + p])\
                                .then_inc(s_kve[bank], 1)
                        else:
                            _, p, tc4 = tile
                            qi_ = p * 8 + tc4
                            if qi_ >= 2:
                                a.wait_ge(s_qst[qi_ % 2], 16 * (qi_ // 2))
                                a.wait_ge(s_mb3, 64 + qi_ - 1)
                            a.activation(qev[qi_ % 2][:], pq[bank][:], AF.Identity,
                                         bias=bT_s[:, p:p + 1])\
                                .then_inc(s_kve[bank], 1)
                    a.wait_ge(s_mb4, 1)
                    a.activation(mrow[0:1, 4:5], mrow[0:1, 3:4], AF.Sqrt)\
                        .then_inc(s_mb7, 1)
                    a.wait_ge(s_mb8, 2)
                    a.copy(ebias[:], pxb[:, 256:257]).then_inc(s_mb8, 1)
                blk.scalar(act2)

                def dve2_(v):
                    for u in range(24):
                        c, part = u // 3, u % 3
                        j = u % 2
                        v.wait_ge(s_wl[j], 16 * (u // 2 + 1))
                        v.tensor_copy(waT[c][:, part * CM:(part + 1) * CM],
                                      wfl[j][:]).then_inc(s_wc, 1)
                    for k, tile in enumerate(TILES):
                        kind = tile[0]
                        bank = k % 2
                        if kind == "v":
                            _, tb, hf = tile
                            v.wait_ge(s_kv, k + 1)
                            base = tb * 1040 + hf * 8 * 65
                            dst = vaug[:, base:base + 8 * 65]\
                                .rearrange("p (hh d) -> p hh d", d=65)[:, :, 0:64]
                            v.tensor_copy(
                                dst, pq[bank][:].rearrange("p (hh d) -> p hh d",
                                                           d=64))\
                                .then_inc(s_kve[bank], 1)
                        else:
                            _, p, tc4 = tile
                            ev_ = kev if kind == "k" else qev
                            ti_ = p * 8 + tc4
                            tkq = ti_ + (0 if kind == "k" else 64)
                            v.wait_ge(s_kve[bank], k // 2 + 1)
                            if tkq >= 2:
                                v.wait_ge(s_mb5, tkq - 1)
                            v.tensor_tensor(sqb[tkq % 2][:], ev_[ti_ % 2][:],
                                            ev_[ti_ % 2][:], ALU.mult)\
                                .then_inc(s_mb3, 1)
                            v.wait_ge(s_mb5, tkq + 1)
                            v.reduce_max(mbc[0:2, tkq:tkq + 1],
                                         psq[tkq % 2][0:2, :], axis=AX.X)\
                                .then_inc(s_krd if kind == "k" else s_qrd, 1)
                    for tb in range(32):
                        dst = vaug[:, tb * 1040:(tb + 1) * 1040]\
                            .rearrange("p (hh d) -> p hh d", d=65)[:, :, 64:65]
                        v.memset(dst, 1.0)
                    v.wait_ge(s_krd, 64)
                    v.wait_ge(s_qrd, 64)
                    v.memset(sc128[:], 0.0)
                    v.tensor_copy(ldump[:, 0:8], cs[:, 0:8])
                    v.reduce_max(mbc[0:2, 128:136],
                                 mbc[0:2, 0:64].rearrange("p (a b) -> p a b",
                                                          b=8),
                                 axis=AX.X).then_inc(s_mb1, 1)
                    v.reduce_max(mbc[0:2, 136:144],
                                 mbc[0:2, 64:128].rearrange("p (a b) -> p a b",
                                                            b=8),
                                 axis=AX.X).then_inc(s_mb1, 1)
                    v.wait_ge(s_mb1, 2)
                    v.tensor_copy(ldump[:, 0:8], cs[:, 0:8])
                    v.tensor_tensor(sc128[0:2, 8:16], mbc[0:2, 128:136],
                                    mbc[0:2, 136:144], ALU.mult)\
                        .then_inc(s_mb2, 1)
                    v.wait_ge(s_mb2, 1)
                    v.tensor_copy(ldump[:, 0:8], cs[:, 0:8])
                    v.reduce_max(sc128[0:2, 0:1], sc128[0:2, 8:16], axis=AX.X)\
                        .then_inc(s_mb2, 1)
                    v.wait_ge(s_mb6, 1)
                    v.tensor_copy(ldump[:, 0:8], cs[:, 0:8])
                    v.reduce_max(mrow[0:1, 3:4], pxb[0:1, 64:66], axis=AX.X)\
                        .then_inc(s_mb4, 1)
                    v.wait_ge(s_mb7, 1)
                    v.tensor_copy(ldump[:, 0:8], cs[:, 0:8])
                    v.tensor_scalar(mrow[0:1, 5:6], mrow[0:1, 4:5],
                                    cc(cs, "att_scale")[0:1, :],
                                    cc(cs, "neg_one")[0:1, :],
                                    ALU.mult, ALU.mult).then_inc(s_mb7, 1)
                blk.vector(dve2_)

                def g2_(g):
                    for k, tile in enumerate(TILES):
                        kind = tile[0]
                        if kind == "k":
                            _, p, tc4 = tile
                            ki = p * 8 + tc4
                            g.wait_ge(s_mb3, ki + 1)
                            g.dma_start(out=kTd[p * 128:(p + 1) * 128,
                                                tc4 * 512:(tc4 + 1) * 512],
                                        in_=kev[ki % 2][:]).then_inc(s_kst[ki % 2], 16)
                        elif kind == "q":
                            _, p, tc4 = tile
                            qi_ = p * 8 + tc4
                            g.wait_ge(s_mb3, 64 + qi_ + 1)
                            g.dma_start(out=qTd[p * 128:(p + 1) * 128,
                                                tc4 * 512:(tc4 + 1) * 512],
                                        in_=qev[qi_ % 2][:]).then_inc(s_qst[qi_ % 2], 16)
                    if dbg:
                        for j in range(2):
                            g.wait_ge(s_kst[j], 16 * 32)
                        g.dma_start(out=dbg_kT, in_=kTd).then_inc(s_fxd, 16)
                        g.wait_ge(s_mb7, 2)
                        g.dma_start(out=dbg_m, in_=mrow[0:1, 4:5]).then_inc(s_fxd, 16)
                blk.gpsimd(g2_)

            with ExitStack() as sc3:
                e3 = sc3.enter_context
                q0s = e3(nc.sbuf_tensor("q0s", [64, T], BF16))
                k0s = e3(nc.sbuf_tensor("k0s", [64, T], BF16))
                sSc = [e3(nc.sbuf_tensor(f"sSc{j}", [128, 512], BF16))
                       for j in range(2)]
                ffsb = e3(nc.sbuf_tensor("ffsb", [128, T], F32))
                carv = e3(nc.sbuf_tensor("carv", [1, T], F32))
                carvb = e3(nc.sbuf_tensor("carvb", [1, T], BF16))
                ffw = e3(nc.sbuf_tensor("ffw", [128, 128], F32))
                bis = e3(nc.sbuf_tensor("bis", [128, 12], F32))
                eind = e3(nc.sbuf_tensor("eind", [128, T], BF16))
                ee0 = e3(nc.sbuf_tensor("ee0", [128, T], BF16))
                eout = e3(nc.sbuf_tensor("eout", [128, T], BF16))
                dgt = e3(nc.sbuf_tensor("dgt", [128, 128], BF16))
                ps0 = [e3(nc.psum_tensor(f"ps0_{j}", [128, 512], F32))
                       for j in range(2)]
                pff = [e3(nc.psum_tensor(f"pff{j}", [128, 512], F32))
                       for j in range(2)]
                pcs = [e3(nc.psum_tensor(f"pcs{j}", [1, 512], F32))
                       for j in range(2)]

                SCH = []
                for I in range(32):
                    w = 128 * (I + 1)
                    for jc in range((w + 511) // 512):
                        SCH.append((I, jc, min(512, w - 512 * jc)))
                E_before = {}
                for idx, (I, jc, nj) in enumerate(SCH):
                    if I not in E_before:
                        E_before[I] = idx
                done_at = {}
                for idx, (I, jc, nj) in enumerate(SCH):
                    if I in mine:
                        done_at[I] = idx
                ev = []
                nS0 = [0, 0]; nSR = [0, 0]; nSM = 0
                nF1 = 0; nFE = 0
                nTH = 0; nEST = 0; nEE = 0; nEM = 0; nBIS = 0
                seen_first_evac = set()
                for idx, (I, jc, nj) in enumerate(SCH):
                    slot = idx % 2
                    d = {"slot": slot, "I": I, "jc": jc, "nj": nj}
                    nS0[slot] += 1; d["s0w"] = nS0[slot]
                    d["m0"] = (jc == 0)
                    d["md"] = (jc == I // 4)
                    d["mask"] = d["m0"] or d["md"]
                    d["sm_before"] = nSM
                    nSM += int(d["m0"]) + int(d["md"])
                    d["sm"] = nSM
                    nSR[slot] += 1; d["s0r"] = nSR[slot]
                    if I in mine:
                        d["fslot"] = nF1 % 2
                        nF1 += 1; d["f1"] = nF1
                        nFE += 1; d["fe"] = nFE
                        d["first_evac"] = I not in seen_first_evac
                        seen_first_evac.add(I)
                    d["th_before"] = nTH
                    d["est_before"] = nEST
                    d["ee_before"] = nEE
                    d["em_before"] = nEM
                    d["bis_before"] = nBIS
                    d["done"] = (I in mine) and (idx == done_at[I])
                    if d["done"]:
                        nTH += 1
                        nEE += 1
                        nEM += 4
                        if I > 5:
                            nBIS += 1
                        W = 128 * (I + 1)
                        Wpad = JCs[my_pos[I] // 4] * 128
                        d["n_st"] = 1 + (Wpad - W + 511) // 512
                        nEST += d["n_st"]
                    d["est_after"] = nEST
                    d["th_after"] = nTH
                    d["ee_after"] = nEE
                    d["em_after"] = nEM
                    d["bis_after"] = nBIS
                    ev.append(d)
                NSCH = len(SCH)
                EST_TOTAL = nEST
                FE_TOTAL = nFE
                F1_TOTAL = nF1
                SR_TOT = (nSR[0], nSR[1])

                def sy3(s):
                    for j in range(2):
                        s.wait_ge(s_qst[j], 16 * 32)
                        s.wait_ge(s_kst[j], 16 * 32)
                    s.dma_start(out=q0s[:], in_=qTd[0:64, :]).then_inc(s_q0l, 16)
                    s.dma_start(out=k0s[:], in_=kTd[0:64, :]).then_inc(s_k0l, 16)
                blk.sync(sy3)

                def pe3(t):
                    t.wait_ge(s_kve[0], 96)
                    t.wait_ge(s_kve[1], 96)
                    t.wait_ge(s_krd, 64)
                    t.wait_ge(s_qrd, 64)
                    t.wait_ge(s_mb8, 3)
                    t.wait_ge(s_q0l, 16)
                    t.wait_ge(s_k0l, 16)
                    t.wait_ge(s_cv, 1)
                    for idx, (I, jc, nj) in enumerate(SCH):
                        d = ev[idx]
                        slot = d["slot"]
                        if idx >= 2:
                            t.wait_ge(s_s0r[slot], ev[idx - 2]["s0r"])
                        t.matmul(ps0[slot][:, 0:nj],
                                 q0s[:, I * 128:(I + 1) * 128],
                                 k0s[:, jc * 512:jc * 512 + nj],
                                 start=True, stop=True)\
                            .then_inc(s_s0w[slot], 1)
                        t.wait_ge(s_s0r[slot], d["s0r"])
                        if d["mask"]:
                            t.wait_ge(s_sm, d["sm"])
                        if I in mine:
                            if d["f1"] >= 3:
                                t.wait_ge(s_ffe, d["f1"] - 2)
                            t.wait_ge(s_cv, 1 + 2 * E_before[I])
                            t.matmul(pff[d["fslot"]][:, 0:nj], cum_s[:],
                                     sSc[slot][:, 0:nj],
                                     start=True, stop=False)
                            t.matmul(pff[d["fslot"]][:, 0:nj],
                                     ones_s[0:1, 0:128],
                                     carvb[0:1, jc * 512:jc * 512 + nj],
                                     start=False, stop=True).then_inc(s_ff1, 1)
                        if idx >= 2:
                            t.wait_ge(s_cv, 1 + 2 * (idx - 1))
                        t.matmul(pcs[idx % 2][0:1, 0:nj], ones_s[:, 0:1],
                                 sSc[slot][:, 0:nj],
                                 start=True, stop=True).then_inc(s_cs, 1)
                blk.tensor(pe3)

                def act3(a):
                    for idx, (I, jc, nj) in enumerate(SCH):
                        d = ev[idx]
                        slot = d["slot"]
                        a.wait_ge(s_s0w[slot], d["s0w"])
                        if idx >= 2:
                            a.wait_ge(s_cs, idx - 1)
                        a.activation(sSc[slot][:, 0:nj], ps0[slot][:, 0:nj],
                                     AF.Relu, scale=cc(cs, "att_scale"))\
                            .then_inc(s_s0r[slot], 1)
                        if d["done"]:
                            W = 128 * (I + 1)
                            a.wait_ge(s_th, d["th_after"])
                            if d["ee_before"] >= 1:
                                a.wait_ge(s_eml, d["em_before"])
                            a.activation(ee0[:, 0:W], ffsb[:, 0:W], AF.Exp,
                                         scale=cc(cs, "neg_one"))\
                                .then_inc(s_ee0, 1)
                blk.scalar(act3)

                def dve3_(v):
                    v.memset(carv[:], 0.0)
                    v.memset(carvb[:], 0.0).then_inc(s_cv, 1)
                    for idx, (I, jc, nj) in enumerate(SCH):
                        d = ev[idx]
                        slot = d["slot"]
                        if d["mask"]:
                            v.wait_ge(s_s0r[slot], d["s0r"])
                            n0 = d["sm_before"]
                            if d["m0"]:
                                v.memset(sSc[slot][:, 0:1], 0.0)\
                                    .then_inc(s_sm, 1)
                                n0 += 1
                            if d["md"]:
                                if d["m0"]:
                                    v.wait_ge(s_sm, n0)
                                off = I * 128 - jc * 512
                                v.tensor_tensor(sSc[slot][:, off:off + 128],
                                                sSc[slot][:, off:off + 128],
                                                lts_s[:], ALU.mult)\
                                    .then_inc(s_sm, 1)
                        if I in mine:
                            v.wait_ge(s_ff1, d["f1"])
                            if d.get("first_evac"):
                                v.wait_ge(s_ee0, d["ee_before"])
                                v.wait_ge(s_ein, d["bis_before"])
                            v.tensor_copy(ffsb[:, jc * 512:jc * 512 + nj],
                                          pff[d["fslot"]][:, 0:nj])\
                                .then_inc(s_ffe, 1)
                        v.wait_ge(s_cs, idx + 1)
                        v.tensor_copy(ldump[:, 0:8], cs[:, 0:8])
                        v.tensor_tensor(carv[0:1, jc * 512:jc * 512 + nj],
                                        carv[0:1, jc * 512:jc * 512 + nj],
                                        pcs[idx % 2][0:1, 0:nj], ALU.add)\
                            .then_inc(s_cv, 1)
                        v.wait_ge(s_cv, 2 * idx + 2)
                        v.tensor_copy(carvb[0:1, jc * 512:jc * 512 + nj],
                                      carv[0:1, jc * 512:jc * 512 + nj])\
                            .then_inc(s_cv, 1)
                        if d["done"]:
                            W = 128 * (I + 1)
                            nw = W // 32
                            m = my_pos[I]
                            v.wait_ge(s_ffe, d["fe"])
                            v.tensor_copy(ldump[:, 0:8], cs[:, 0:8])
                            wm = 3 * d["th_before"]
                            v.tensor_scalar(ee0[:, 0:W], ffsb[:, 0:W],
                                            cc(cs, "neg_one"), None, ALU.mult)\
                                .then_inc(s_wmn, 1)
                            v.wait_ge(s_wmn, wm + 1)
                            v.tensor_copy(ldump[:, 0:8], cs[:, 0:8])
                            v.reduce_max(ffw[:, 0:nw],
                                         ee0[:, 0:W].rearrange(
                                             "p (a b) -> p a b", b=32),
                                         axis=AX.X).then_inc(s_wmn, 1)
                            v.wait_ge(s_wmn, wm + 2)
                            v.tensor_copy(ldump[:, 0:8], cs[:, 0:8])
                            v.tensor_scalar(ffw[:, 0:nw], ffw[:, 0:nw],
                                            cc(cs, "neg_one"), None, ALU.mult)\
                                .then_inc(s_wmn, 1)
                            v.wait_ge(s_wmn, wm + 3)
                            if I > 5:
                                bb = 50 * d["bis_before"]
                                v.memset(bis[:, 0:1], 0.0)
                                v.tensor_copy(bis[:, 1:2], cc(cs, "c1024"))\
                                    .then_inc(s_bis, 1)
                                v.wait_ge(s_bis, bb + 1)
                                for it in range(8):
                                    ib = bb + 1 + 6 * it
                                    if it >= 1:
                                        v.wait_ge(s_bis, ib)
                                    v.tensor_tensor(bis[:, 2:3], bis[:, 0:1],
                                                    bis[:, 1:2], ALU.add)\
                                        .then_inc(s_bis, 1)
                                    v.wait_ge(s_bis, ib + 1)
                                    v.tensor_scalar(bis[:, 2:3], bis[:, 2:3],
                                                    cc(cs, "half"), None,
                                                    ALU.mult).then_inc(s_bis, 1)
                                    v.wait_ge(s_bis, ib + 2)
                                    v.tensor_tensor(bis[:, 3:4], bis[:, 2:3],
                                                    bis[:, 0:1], ALU.subtract)
                                    v.tensor_tensor(bis[:, 4:5], bis[:, 2:3],
                                                    bis[:, 1:2], ALU.subtract)
                                    v.tensor_scalar(ldump[:, 8:8 + nw],
                                                    ffw[:, 0:nw], bis[:, 2:3],
                                                    0.0, ALU.is_lt, ALU.add,
                                                    accum_out=bis[:, 5:6])\
                                        .then_inc(s_bis, 1)
                                    v.wait_ge(s_bis, ib + 3)
                                    v.tensor_copy(ldump[:, 0:8], cs[:, 0:8])
                                    v.tensor_scalar(bis[:, 6:7], bis[:, 5:6],
                                                    cc(cs, "win_budget"), None,
                                                    ALU.is_le).then_inc(s_bis, 1)
                                    v.wait_ge(s_bis, ib + 4)
                                    v.tensor_scalar(bis[:, 7:8], bis[:, 6:7],
                                                    cc(cs, "neg_one"), 1.0,
                                                    ALU.mult, ALU.add)\
                                        .then_inc(s_bis, 1)
                                    v.wait_ge(s_bis, ib + 5)
                                    v.scalar_tensor_tensor(bis[:, 0:1],
                                                           bis[:, 3:4],
                                                           bis[:, 6:7],
                                                           bis[:, 0:1],
                                                           ALU.mult, ALU.add)
                                    v.scalar_tensor_tensor(bis[:, 1:2],
                                                           bis[:, 4:5],
                                                           bis[:, 7:8],
                                                           bis[:, 1:2],
                                                           ALU.mult, ALU.add)\
                                        .then_inc(s_bis, 1)
                                v.wait_ge(s_bis, bb + 49)
                                v.tensor_scalar(bis[:, 8:9], iota_s[:, I:I + 1],
                                                cc(cs, "c817"), None, ALU.is_le)\
                                    .then_inc(s_bis, 1)
                                v.wait_ge(s_bis, bb + 50)
                                v.scalar_tensor_tensor(bis[:, 9:10], bis[:, 8:9],
                                                       cc(cs, "big"),
                                                       bis[:, 0:1],
                                                       ALU.mult, ALU.add)\
                                    .then_inc(s_th, 1)
                                v.wait_ge(s_th, d["th_before"] + 1)
                                v.tensor_copy(ldump[:, 0:8], cs[:, 0:8])
                                v.tensor_scalar(eind[:, 0:W], ffsb[:, 0:W],
                                                bis[:, 9:10], None, ALU.is_lt)\
                                    .then_inc(s_ein, 1)
                            else:
                                v.tensor_copy(ldump[:, 0:8], cs[:, 0:8])\
                                    .then_inc(s_th, 1)
                            v.wait_ge(s_ee0, d["ee_before"] + 1)
                            v.wait_ge(s_est, 16 * d["est_before"])
                            if I > 5:
                                v.wait_ge(s_ein, d["bis_before"] + 1)
                                v.tensor_tensor(eout[:, 0:W], ee0[:, 0:W],
                                                eind[:, 0:W], ALU.mult)\
                                    .then_inc(s_eml, 1)
                            else:
                                v.tensor_copy(eout[:, 0:W], ee0[:, 0:W])\
                                    .then_inc(s_eml, 1)
                            v.wait_ge(s_eml, d["em_before"] + 1)
                            v.tensor_copy(ldump[:, 0:8], cs[:, 0:8])
                            v.tensor_tensor(dgt[:],
                                            eout[:, I * 128:(I + 1) * 128],
                                            lti_s[:], ALU.mult)\
                                .then_inc(s_eml, 1)
                            v.wait_ge(s_eml, d["em_before"] + 2)
                            v.tensor_copy(ldump[:, 0:8], cs[:, 0:8])
                            v.tensor_copy(eout[:, I * 128:(I + 1) * 128],
                                          dgt[:]).then_inc(s_eml, 2)
                blk.vector(dve3_)

                def g3(g):
                    for idx, (I, jc, nj) in enumerate(SCH):
                        d = ev[idx]
                        if d["done"]:
                            W = 128 * (I + 1)
                            m = my_pos[I]
                            Wpad = JCs[m // 4] * 128
                            g.wait_ge(s_eml, d["em_before"] + 4)
                            g.dma_start(out=Ed[m * 128:(m + 1) * 128, 0:W],
                                        in_=eout[:, 0:W]).then_inc(s_est, 16)
                            o = W
                            while o < Wpad:
                                wz = min(512, Wpad - o)
                                g.dma_start(out=Ed[m * 128:(m + 1) * 128,
                                                   o:o + wz],
                                            in_=zer_s[:, 0:wz])\
                                    .then_inc(s_est, 16)
                                o += wz
                blk.gpsimd(g3)

            with ExitStack() as sc4:
                e4 = sc4.enter_context
                kTb = [e4(nc.sbuf_tensor(f"kTb{p}", [128, T], BF16))
                       for p in range(NPAIR)]
                ebuf = e4(nc.sbuf_tensor("ebuf", [128, 32 * 512], BF16))
                qbuf = [e4(nc.sbuf_tensor(f"qbuf{p}", [128, 512], BF16))
                        for p in range(NPAIR)]
                p2 = [e4(nc.sbuf_tensor(f"p2_{j}", [128, 1024], BF16))
                      for j in range(2)]
                pmul = [e4(nc.sbuf_tensor(f"pmul{j}", [128, 1024], BF16))
                        for j in range(2)]
                ysb = [e4(nc.sbuf_tensor(f"ysb{j}", [128, 520], F32))
                       for j in range(2)]
                rz = [e4(nc.sbuf_tensor(f"rz{j}", [128, 8], F32)) for j in range(2)]
                ybf = [e4(nc.sbuf_tensor(f"ybf{s}", [128, CM], BF16))
                       for s in range(4)]
                pqk = [e4(nc.psum_tensor(f"pqk{j}", [128, 1024], F32))
                       for j in range(2)]
                yb = [e4(nc.psum_tensor(f"yb{j}", [128, 512], F32))
                      for j in range(4)]
                s_pvt = [sem(f"s_pvt{j}") for j in range(2)]

                QKT = []
                for ic in range(4):
                    for p in range(NPAIR):
                        for jc in range(JCs[ic]):
                            QKT.append((ic, p, jc))
                NQK = len(QKT)
                EL_TOT, QL_TOT = [], []
                elc = qlc = 0
                for ic in range(4):
                    elc += JCs[ic]
                    qlc += NPAIR * 4
                    EL_TOT.append(elc)
                    QL_TOT.append(qlc)
                PM_CNT = []
                PM_TOT = [[0] * 4, [0] * 4]
                pmc = [0, 0]
                GATE = []
                pv_c = [0, 0]
                pvt_c = [0, 0]
                for i, (ic, p, jc) in enumerate(QKT):
                    pmc[i % 2] += 1
                    PM_CNT.append(pmc[i % 2])
                    PM_TOT[0][ic] = pmc[0]
                    PM_TOT[1][ic] = pmc[1]
                    if jc == JCs[ic] - 1:
                        pv_c[p % 2] += 1
                        GATE.append(("pv", p % 2, pv_c[p % 2]))
                    else:
                        pvt_c[i % 2] += 1
                        GATE.append(("pvt", i % 2, pvt_c[i % 2]))

                def sy4(s):
                    s.wait_ge(s_est, 16 * EST_TOTAL)
                    s.wait_ge(s_cv, 1 + 2 * NSCH)
                    s.wait_ge(s_cs, NSCH)
                    s.wait_ge(s_s0r[0], SR_TOT[0])
                    s.wait_ge(s_s0r[1], SR_TOT[1])
                    for j in range(2):
                        s.wait_ge(s_kst[j], 16 * 32)
                    for p in range(NPAIR):
                        s.dma_start(out=kTb[p][:],
                                    in_=kTd[p * 128:(p + 1) * 128, :])\
                            .then_inc(s_ktl, 16)
                    for ic in range(4):
                        if ic == 0:
                            s.wait_ge(s_est, 16 * EST_TOTAL)
                        else:
                            s.wait_ge(s_pm[0], PM_TOT[0][ic - 1])
                            s.wait_ge(s_pm[1], PM_TOT[1][ic - 1])
                        for jc in range(JCs[ic]):
                            s.dma_start(out=ebuf[:, jc * 512:(jc + 1) * 512],
                                        in_=Ed[ic * 512:(ic + 1) * 512,
                                               jc * 128:(jc + 1) * 128],
                                        transpose=True).then_inc(s_el, 16)
                        for p in range(NPAIR):
                            for z, b_ in enumerate(icq[ic]):
                                s.dma_start(
                                    out=qbuf[p][:, z * 128:(z + 1) * 128],
                                    in_=qTd[p * 128:(p + 1) * 128,
                                            b_ * 128:(b_ + 1) * 128])\
                                    .then_inc(s_ql, 16)
                blk.sync(sy4)

                def pe4(t):
                    t.wait_ge(s_ffe, FE_TOTAL)
                    t.wait_ge(s_cv, 1 + 2 * NSCH)
                    t.wait_ge(s_s0r[0], SR_TOT[0])
                    t.wait_ge(s_s0r[1], SR_TOT[1])
                    t.wait_ge(s_ktl, 16 * NPAIR)
                    for i, (ic, p, jc) in enumerate(QKT):
                        bank = i % 2
                        par = p % 2
                        gp = ic * NPAIR + p
                        if jc == 0:
                            t.wait_ge(s_el, 16 * EL_TOT[ic])
                            t.wait_ge(s_ql, 16 * QL_TOT[ic])
                        if i >= 2:
                            t.wait_ge(s_ex[bank], i // 2)
                        t.matmul(pqk[bank][:, 0:512],
                                 kTb[p][0:64, jc * 128:(jc + 1) * 128],
                                 qbuf[p][0:64, :], start=True, stop=True,
                                 tile_position=(0, 0))
                        t.matmul(pqk[bank][:, 512:1024],
                                 kTb[p][64:128, jc * 128:(jc + 1) * 128],
                                 qbuf[p][64:128, :], start=True, stop=True,
                                 tile_position=(64, 0)).then_inc(s_qkd, 1)
                        t.wait_ge(s_pm[bank], PM_CNT[i])
                        if jc == 0 and gp >= 2:
                            t.wait_ge(s_ye[par], gp // 2)
                        last_tile = (jc == JCs[ic] - 1)
                        for hh in range(2):
                            bk = par * 2 + hh
                            mm = None
                            for sub in range(4):
                                mm = t.matmul(
                                    yb[bk][:, 65 * sub:65 * sub + 65],
                                    pmul[bank][:, hh * 512 + sub * 128:
                                               hh * 512 + (sub + 1) * 128],
                                    vaug[:, jc * 1040 + (2 * p + hh) * 65:
                                         jc * 1040 + (2 * p + hh) * 65 + 65],
                                    start=(jc == 0 and sub == 0),
                                    stop=(last_tile and sub == 3))
                            if hh == 1:
                                if last_tile:
                                    mm.then_inc(s_pv[par], 1)
                                else:
                                    mm.then_inc(s_pvt[bank], 1)
                blk.tensor(pe4)

                def act4(a):
                    a.wait_ge(s_mb8, 3)
                    for i, (ic, p, jc) in enumerate(QKT):
                        bank = i % 2
                        par = p % 2
                        gp = ic * NPAIR + p
                        a.wait_ge(s_qkd, i + 1)
                        if i >= 2:
                            a.wait_ge(s_pm[bank], PM_CNT[i] - 1)
                        a.activation(p2[bank][:], pqk[bank][:], AF.Exp,
                                     bias=ebias[:, 0:1],
                                     scale=cc(cs, "att_scale"))\
                            .then_inc(s_ex[bank], 1)
                        if jc == JCs[ic] - 1:
                            a.wait_ge(s_pv[par], gp // 2 + 1)
                            if gp >= 2:
                                a.wait_ge(s_yn[par], 2 * (gp // 2))
                            a.copy(ysb[par][:, 0:260], yb[par * 2][:, 0:260])
                            a.copy(ysb[par][:, 260:520],
                                   yb[par * 2 + 1][:, 0:260]).then_inc(s_ye[par], 1)
                blk.scalar(act4)

                def dve4(v):
                    for i, (ic, p, jc) in enumerate(QKT):
                        bank = i % 2
                        par = p % 2
                        gp = ic * NPAIR + p
                        v.wait_ge(s_ex[bank], i // 2 + 1)
                        if i >= 2:
                            gk, gj, gc = GATE[i - 2]
                            v.wait_ge(s_pv[gj] if gk == "pv" else s_pvt[gj], gc)
                        v.tensor_tensor(pmul[bank][:, 0:512],
                                        p2[bank][:, 0:512],
                                        ebuf[:, jc * 512:(jc + 1) * 512],
                                        ALU.mult)
                        v.tensor_tensor(pmul[bank][:, 512:1024],
                                        p2[bank][:, 512:1024],
                                        ebuf[:, jc * 512:(jc + 1) * 512],
                                        ALU.mult).then_inc(s_pm[bank], 1)
                        if jc == JCs[ic] - 1:
                            v.wait_ge(s_ye[par], gp // 2 + 1)
                            if p == 0 and ic >= 1:
                                for sbi in range(4):
                                    v.wait_ge(s_ys[sbi], 16 * ic)
                            zv = ysb[par][:, 0:520].rearrange(
                                "p (hh d) -> p hh d", d=65)[:, :, 64:65]
                            v.reciprocal(rz[par][:, 0:8].rearrange(
                                "p (a o) -> p a o", o=1), zv)
                            v.tensor_copy(ldump[:, 0:8], cs[:, 0:8])
                            v.tensor_copy(ldump[:, 8:16], cs[:, 0:8])\
                                .then_inc(s_yn[par], 1)
                            v.wait_ge(s_yn[par], 2 * (gp // 2) + 1)
                            for hh in range(2):
                                head = 2 * p + hh
                                for sbi in range(4):
                                    ins_ = v.scalar_tensor_tensor(
                                        ybf[sbi][:, head * 64:(head + 1) * 64],
                                        ysb[par][:, hh * 260 + sbi * 65:
                                                 hh * 260 + sbi * 65 + 64],
                                        rz[par][:, hh * 4 + sbi:hh * 4 + sbi + 1],
                                        bvb_s[:, head * 64:(head + 1) * 64],
                                        ALU.mult, ALU.add)
                            ins_.then_inc(s_yn[par], 1)
                blk.vector(dve4)

                def g4(g):
                    for ic in range(4):
                        g.wait_ge(s_yn[0], 8 * (ic + 1))
                        g.wait_ge(s_yn[1], 8 * (ic + 1))
                        for sbi in range(4):
                            g.dma_start(
                                out=ydD[(ic * 4 + sbi) * 128:
                                        (ic * 4 + sbi + 1) * 128, :],
                                in_=ybf[sbi][:]).then_inc(s_ys[sbi], 16)
                    if dbg:
                        for icq_ in range(4):
                            wq = JCs[icq_] * 128
                            g.dma_start(
                                out=dbg_E[icq_ * 512:(icq_ + 1) * 512, 0:wq],
                                in_=Ed[icq_ * 512:(icq_ + 1) * 512, 0:wq])\
                                .then_inc(s_fxd, 16)
                        for sbi in range(4):
                            g.wait_ge(s_ys[sbi], 16 * 4)
                        g.dma_start(out=dbg_y, in_=ydD).then_inc(s_fxd, 16)
                blk.gpsimd(g4)

        with ExitStack() as sc5:
            e5 = sc5.enter_context
            wpf = [e5(nc.sbuf_tensor(f"wpf{j}", [128, CM], F32)) for j in range(2)]
            wpT = [e5(nc.sbuf_tensor(f"wpT{c}", [128, CM], BF16)) for c in range(8)]
            yT = [e5(nc.sbuf_tensor(f"yT{c}", [128, TOK], BF16)) for c in range(8)]
            fsb = [e5(nc.sbuf_tensor(f"fsb{j}", [128, CM], F32)) for j in range(2)]
            xot = [e5(nc.sbuf_tensor(f"xot{j}", [128, CM], F32)) for j in range(2)]
            x2t = [e5(nc.sbuf_tensor(f"x2t{j}", [128, CM], F32)) for j in range(2)]
            x3t = [e5(nc.sbuf_tensor(f"x3t{j}", [128, CM], BF16)) for j in range(2)]
            lst2 = e5(nc.sbuf_tensor("lst2", [128, 16], F32))
            lst3 = e5(nc.sbuf_tensor("lst3", [128, 16], F32))
            mxc = e5(nc.sbuf_tensor("mxc", [128, 16], F32))
            pp = [e5(nc.psum_tensor(f"pp{j}", [128, CM], F32)) for j in range(2)]
            s_wpf = [sem(f"s_wpf{j}") for j in range(2)]
            s_wpc = sem("s_wpc")

            def sy5(s):
                for q in range(4):
                    s.wait_ge(s_ys[q], 16 * 4)
                for c in range(8):
                    j = c % 2
                    if c >= 2:
                        s.wait_ge(s_wpc, c - 1)
                    s.dma_start(out=wpf[j][:], in_=wpjT[c * 128:(c + 1) * 128, :])\
                        .then_inc(s_wpf[j], 16)
                for tb in range(16):
                    if tb >= 2:
                        s.wait_ge(s_ytl[tb % 2], 16 * 8 * (tb // 2))
                    for c in range(8):
                        s.dma_start(out=yT[c][:, tb * 128:(tb + 1) * 128],
                                    in_=ydD[tb * 128:(tb + 1) * 128,
                                            c * 128:(c + 1) * 128],
                                    transpose=True).then_inc(s_ytl[tb % 2], 16)
                for tb in range(16):
                    j = tb % 2
                    if tb >= 2:
                        s.wait_ge(s_re2, 2 * tb - 2)
                    gb = my_blocks[tb]
                    s.dma_start(out=xot[j][:], in_=xbD[gb * 128:(gb + 1) * 128, :])\
                        .then_inc(s_xol[j], 16)
            blk.sync(sy5)

            def pe5(t):
                t.wait_ge(s_wpc, 8)
                for tb in range(16):
                    t.wait_ge(s_ytl[tb % 2], 16 * 8 * (tb // 2 + 1))
                    if tb >= 2:
                        t.wait_ge(s_bia, tb - 1)
                    mm = None
                    for half in range(2):
                        for c in range(8):
                            mm = t.matmul(pp[tb % 2][:, half * 512:(half + 1) * 512],
                                          yT[c][:, tb * 128:(tb + 1) * 128],
                                          wpT[c][:, half * 512:(half + 1) * 512],
                                          start=(c == 0), stop=(c == 7))
                    mm.then_inc(s_pj, 1)
            blk.tensor(pe5)

            def act5(a):
                for i in range(18):
                    if 1 <= i <= 16:
                        k = i - 1
                        kc = 8 * (k % 2)
                        a.wait_ge(s_sq2, 7 * (k + 1))
                        a.activation(lst2[:, kc + 6:kc + 7], lst2[:, kc + 5:kc + 6],
                                     AF.Ln, bias=cc(cs, "ln_eps")).then_inc(s_rs2, 1)
                        a.wait_ge(s_rs2, 2 * k + 1)
                        a.activation(lst2[:, kc + 7:kc + 8], lst2[:, kc + 6:kc + 7],
                                     AF.Exp, scale=cc(cs, "neg_half"))\
                            .then_inc(s_rs2, 1)
                    if i >= 2:
                        k = i - 2
                        kc = 8 * (k % 2)
                        a.wait_ge(s_sq3, 7 * (k + 1))
                        a.activation(lst3[:, kc + 6:kc + 7], lst3[:, kc + 5:kc + 6],
                                     AF.Ln, bias=cc(cs, "ln_eps")).then_inc(s_rs3, 1)
                        a.wait_ge(s_rs3, 2 * k + 1)
                        a.activation(lst3[:, kc + 7:kc + 8], lst3[:, kc + 6:kc + 7],
                                     AF.Exp, scale=cc(cs, "neg_half"))\
                            .then_inc(s_rs3, 1)
            blk.scalar(act5)

            def dve5(v):
                for c in range(8):
                    j = c % 2
                    v.wait_ge(s_wpf[j], 16 * (c // 2 + 1))
                    v.tensor_copy(wpT[c][:], wpf[j][:]).then_inc(s_wpc, 1)
                for i in range(18):
                    if i < 16:
                        tb = i
                        j = tb % 2
                        tc = 8 * j
                        v.wait_ge(s_pj, tb + 1)
                        if tb >= 2:
                            v.wait_ge(s_re2, 2 * tb - 3)
                            v.wait_ge(s_rs2, 2 * (tb - 1))
                        v.scalar_tensor_tensor(fsb[j][:], pp[j][:], 1.0,
                                               bpb_s[:], ALU.bypass, ALU.add)\
                            .then_inc(s_bia, 1)
                        v.wait_ge(s_bia, tb + 1)
                        _stats(v, lst2, tc, fsb[j], 7 * tb, s_sq2)
                    if 1 <= i <= 16:
                        k = i - 1
                        jk = k % 2
                        kc = 8 * jk
                        v.wait_ge(s_rs2, 2 * (k + 1))
                        v.wait_ge(s_xol[jk], 16 * (k // 2 + 1))
                        if k >= 2:
                            v.wait_ge(s_o2[jk], 16 * (k // 2))
                            v.wait_ge(s_ap3, k - 1)
                            v.wait_ge(s_rs3, 2 * (k - 1))
                        v.tensor_scalar(x2t[jk][:], fsb[jk][:],
                                        lst2[:, kc + 2:kc + 3],
                                        lst2[:, kc + 7:kc + 8], ALU.subtract,
                                        ALU.mult).then_inc(s_re2, 1)
                        v.wait_ge(s_re2, 2 * k + 1)
                        v.tensor_tensor(x2t[jk][:], x2t[jk][:], xot[jk][:],
                                        ALU.add).then_inc(s_re2, 1)
                        v.wait_ge(s_re2, 2 * k + 2)
                        _stats(v, lst3, kc, x2t[jk], 7 * k, s_sq3)
                    if i >= 2:
                        k = i - 2
                        jk = k % 2
                        kc = 8 * jk
                        v.wait_ge(s_rs3, 2 * (k + 1))
                        if k >= 2:
                            v.wait_ge(s_o3[jk], 16 * (k // 2))
                        v.tensor_scalar(x3t[jk][:], x2t[jk][:],
                                        lst3[:, kc + 2:kc + 3],
                                        lst3[:, kc + 7:kc + 8], ALU.subtract,
                                        ALU.mult).then_inc(s_ap3, 1)
                        v.wait_ge(s_ap3, k + 1)
                        v.reduce_max(mxc[:, k:k + 1], x3t[jk][:], axis=AX.X,
                                     apply_absolute_value=True).then_inc(s_mx3, 1)
                v.wait_ge(s_mx3, 16)
                v.tensor_copy(ldump[:, 0:8], cs[:, 0:8])
                v.reduce_max(sc128[:, 0:1], mxc[:], axis=AX.X).then_inc(s_fx, 1)
            blk.vector(dve5)

            def pe5b(t):
                t.wait_ge(s_fx, 1)
                t.transpose(pp[0][:, 0:128], sc128[:, 0:128], idf_s[:])\
                    .then_inc(s_fx, 1)
            blk.tensor(pe5b)

            def dve5b(v):
                v.wait_ge(s_fx, 2)
                v.reduce_max(lm11[0:1, 0:1], pp[0][0:1, 0:128], axis=AX.X)\
                    .then_inc(s_fx, 1)
            blk.vector(dve5b)

            def g5(g):
                for i in range(16):
                    jk = i % 2
                    g.wait_ge(s_re2, 2 * (i + 1))
                    g.dma_start(out=x2o[i * 128:(i + 1) * 128, :],
                                in_=x2t[jk][:]).then_inc(s_o2[jk], 16)
                    g.wait_ge(s_mx3, i + 1)
                    g.dma_start(out=x3o[i * 128:(i + 1) * 128, :],
                                in_=x3t[jk][:]).then_inc(s_o3[jk], 16)
                g.wait_ge(s_fx, 3)
                g.dma_start(out=lmxo, in_=lm11[0:1, 0:1]).then_inc(s_fxd, 16)
                if dbg:
                    g.dma_start(out=dbg_x1, in_=x1d).then_inc(s_fxd, 16)
                for j in range(2):
                    g.wait_ge(s_o2[j], 16 * 8)
                    g.wait_ge(s_o3[j], 16 * 8)
                    g.wait_ge(s_dqs[j], 16 * (4 if j == 0 else 3))
                g.wait_ge(s_fxd, 16 * (1 + (8 if dbg else 0)))
            blk.gpsimd(g5)

    return nc


import ml_dtypes
from concourse.bass_utils import run_bass_kernel_spmd

_BF = ml_dtypes.bfloat16


def _host_consts():
    rr = np.arange(128)
    cumT = (rr[:, None] < rr[None, :]).astype(_BF)
    lowinc = (rr[:, None] >= rr[None, :]).astype(_BF)
    ltstr = (rr[None, :] < rr[:, None]).astype(_BF)
    ltinc = (rr[None, :] <= rr[:, None]).astype(_BF)
    headsel = np.zeros((128, 2), _BF)
    headsel[0:64, 0] = 1.0
    headsel[64:128, 1] = 1.0
    iota32 = (rr[:, None] + 128.0 * np.arange(32)[None, :]).astype(np.float32)
    return cumT, lowinc, ltstr, ltinc, headsel, iota32


def kernel(**inputs):
    x = np.asarray(inputs["L0_attn_x"], dtype=np.float32)
    W_attn = np.asarray(inputs["L0_attn_W_attn"], dtype=np.float32)
    b_attn = np.asarray(inputs["L0_attn_b_attn"], dtype=np.float32)
    W_proj = np.asarray(inputs["L0_attn_W_proj"], dtype=np.float32)
    b_proj = np.asarray(inputs["L0_attn_b_proj"], dtype=np.float32)
    w_up = np.asarray(inputs["L0_ffn_w_up"], dtype=np.float32)
    w_gate = np.asarray(inputs["L0_ffn_w_gate"], dtype=np.float32)
    w_down = np.asarray(inputs["L0_ffn_w_down"], dtype=np.float32)

    cumT, lowinc, ltstr, ltinc, headsel, iota32 = _host_consts()
    common = dict(consts=consts_array(),
                  onesb=np.ones((128, 128), _BF),
                  identf=np.eye(128, dtype=np.float32),
                  onesf=np.ones((128, 128), np.float32))
    l1_common = dict(common,
                     cumT=cumT, lowinc=lowinc, ltstr=ltstr, ltinc=ltinc,
                     headsel=headsel, iota32=iota32,
                     watT=np.ascontiguousarray(W_attn.T),
                     battn24=b_attn.reshape(24, 128),
                     wpjT=np.ascontiguousarray(W_proj.T),
                     bvb=np.tile(b_attn[2048:3072], (128, 1)),
                     bprojb=np.tile(b_proj, (128, 1)))

    res1 = {}
    for h in (0, 1):
        nc1 = build_L1(h)
        ims = []
        for b in range(4):
            c = 2 * b + h
            im = dict(l1_common)
            im["xb"] = x[b]
            im["wup_sl"] = w_up[384 * c:384 * (c + 1)]
            im["wgate_sl"] = w_gate[384 * c:384 * (c + 1)]
            im["wdown_sl"] = w_down[128 * c:128 * (c + 1)]
            ims.append(im)
        r = run_bass_kernel_spmd(nc1, ims, core_ids=list(range(4)))
        for b in range(4):
            res1[2 * b + h] = r.results[b]

    lmx8 = np.array([res1[c]["lmxo"][0, 0] for c in range(8)],
                    dtype=np.float32).reshape(1, 8)
    wupdq = np.concatenate([res1[c]["wupdq_sl"] for c in range(8)], axis=0)
    wgatedq = np.concatenate([res1[c]["wgatedq_sl"] for c in range(8)], axis=0)
    wdowndq = np.concatenate([res1[c]["wdowndq_sl"] for c in range(8)], axis=0)

    nc2 = build_L2()
    im2 = [dict(x3bf=res1[c]["x3o"], lmx8=lmx8, wupdq=wupdq, wgatedq=wgatedq,
                **common) for c in range(8)]
    r2 = run_bass_kernel_spmd(nc2, im2, core_ids=list(range(8)))
    lmu8 = np.array([r2.results[c]["lmug"][0, 0] for c in range(8)],
                    dtype=np.float32).reshape(1, 8)
    lmg8 = np.array([r2.results[c]["lmug"][1, 0] for c in range(8)],
                    dtype=np.float32).reshape(1, 8)

    nc3 = build_L3()
    im3 = [dict(gateT=r2.results[c]["gateT"], lmg8=lmg8, **common)
           for c in range(8)]
    r3 = run_bass_kernel_spmd(nc3, im3, core_ids=list(range(8)))
    lmg28 = np.array([r3.results[c]["lmg2"][0, 0] for c in range(8)],
                     dtype=np.float32).reshape(1, 8)

    nc4 = build_L4()
    im4 = [dict(g2Tin=r3.results[c]["g2T"], upTin=r2.results[c]["upT"],
                lmg28=lmg28, lmu8=lmu8, **common) for c in range(8)]
    r4 = run_bass_kernel_spmd(nc4, im4, core_ids=list(range(8)))
    lmo8 = np.array([r4.results[c]["lmo"][0, 0] for c in range(8)],
                    dtype=np.float32).reshape(1, 8)

    nc5 = build_L5()
    im5 = [dict(oTin=r4.results[c]["oT"], lmo8=lmo8, wdowndq=wdowndq,
                x2in=res1[c]["x2o"], **common) for c in range(8)]
    r5 = run_bass_kernel_spmd(nc5, im5, core_ids=list(range(8)))

    out = np.zeros((4, 4096, 1024), dtype=np.float32)
    for c in range(8):
        out[c // 2, np.array(core_rows(c))] = r5.results[c]["yout"]
    return out


# version 04
# speedup vs baseline: 1.3388x; 1.3388x over previous
from contextlib import ExitStack

import numpy as np
import concourse.bass as bass
import concourse.mybir as mybir

F32 = mybir.dt.float32
BF16 = mybir.dt.bfloat16
I32 = mybir.dt.int32
AF = mybir.ActivationFunctionType
ALU = mybir.AluOpType
AX = mybir.AxisListType

TOK = 2048
CM = 1024
ID = 3072
QMAX = 32767.0

CONST_LIST = [
    ("inv_qmax", 1.0 / QMAX),
    ("eps12", 1e-12),
    ("c65536", 65536.0),
    ("c65535", 65535.0),
    ("sc216", 1.0 / 65536.0),
    ("nsc216", -1.0 / 65536.0),
    ("inv7", 1.0 / 7.0),
    ("ln_eps", 1e-5),
    ("neg_half", -0.5),
    ("inv_cm", 1.0 / CM),
    ("att_scale", 0.125),
    ("big", 30000.0),
    ("win_budget", 25.5),
    ("c64", 64.0),
    ("half", 0.5),
    ("neg_one", -1.0),
    ("neg_big", -30000.0),
    ("c1024", 1024.0),
    ("c817", 817.5),
]
CIX = {k: i for i, (k, _) in enumerate(CONST_LIST)}
NCONST = len(CONST_LIST)


def consts_array():
    v = np.array([val for _, val in CONST_LIST], dtype=np.float32)
    return np.tile(v[None, :], (128, 1))


def cc(cs, name):
    i = CIX[name]
    return cs[:, i:i + 1]


def common_inputs(nc):
    consts = nc.dram_tensor("consts", [128, NCONST], F32, kind="ExternalInput").ap()
    onesb = nc.dram_tensor("onesb", [128, 128], BF16, kind="ExternalInput").ap()
    identf = nc.dram_tensor("identf", [128, 128], F32, kind="ExternalInput").ap()
    onesf = nc.dram_tensor("onesf", [128, 128], F32, kind="ExternalInput").ap()
    return consts, onesb, identf, onesf


def _scale_prep(nc, blk, cs, onesf_s, lmin8, srow, sb, psb, s_sc, s_bc, nsc):
    def dve_scale(v):
        v.reduce_max(srow[0:1, 0:1], lmin8[:], axis=AX.X).then_inc(s_sc, 1)
        v.wait_ge(s_sc, 1)
        v.tensor_copy(srow[0:1, 3:4], cs[0:1, 0:1])
        v.tensor_scalar(srow[0:1, 0:1], srow[0:1, 0:1],
                        cc(cs, "inv_qmax")[0:1, :], cc(cs, "eps12")[0:1, :],
                        ALU.mult, ALU.max).then_inc(s_sc, 1)
        v.wait_ge(s_sc, 2)
        v.tensor_copy(srow[0:1, 3:4], cs[0:1, 0:1])
        v.reciprocal(srow[0:1, 1:2], srow[0:1, 0:1]).then_inc(s_sc, 1)
        v.wait_ge(s_sc, 3)
        v.tensor_copy(srow[0:1, 3:4], cs[0:1, 0:1])
        v.tensor_scalar(srow[0:1, 2:3], srow[0:1, 0:1],
                        cc(cs, "neg_one")[0:1, :], None, ALU.mult)\
            .then_inc(s_sc, 1)
    blk.vector(dve_scale)

    def pe_bc(t):
        t.wait_ge(s_sc, 4)
        t.matmul(psb[:, 0:3], onesf_s[0:1, 0:128], srow[0:1, 0:3],
                 start=True, stop=True).then_inc(s_bc, 1)
    blk.tensor(pe_bc)

    def act_bc(a):
        a.wait_ge(s_bc, 1)
        a.copy(sb[:], psb[:, 0:3]).then_inc(s_bc, 1)
    blk.scalar(act_bc)


def dve_round4(v, dq, yq, qi, out, sb, s_cnt, base):
    v.scalar_tensor_tensor(dq, qi, -1.0, yq, ALU.mult, ALU.add)\
        .then_inc(s_cnt, 1)
    v.wait_ge(s_cnt, base + 1)
    v.scalar_tensor_tensor(yq, dq, 0.5, qi, ALU.is_ge, ALU.add)\
        .then_inc(s_cnt, 1)
    v.wait_ge(s_cnt, base + 2)
    v.scalar_tensor_tensor(dq, dq, -0.5, yq, ALU.is_ge, ALU.add)\
        .then_inc(s_cnt, 1)
    v.wait_ge(s_cnt, base + 3)
    v.tensor_scalar(out, dq, sb[:, 0:1], sb[:, 2:3], ALU.mult, ALU.add)\
        .then_inc(s_cnt, 1)


def _xpart_max_store(nc, blk, idf_s, mxcol, ncols, sc128, psT, lmout_dram,
                     lm11, s_pre, pre_val, s_f, s_fd, g_extra_wait=None):
    def dve_a(v):
        v.wait_ge(s_pre, pre_val)
        v.reduce_max(sc128[:, 0:1], mxcol[:, 0:ncols], axis=AX.X).then_inc(s_f, 1)
    blk.vector(dve_a)

    def pe_a(t):
        t.wait_ge(s_f, 1)
        t.transpose(psT[:], sc128[:], idf_s[:]).then_inc(s_f, 1)
    blk.tensor(pe_a)

    def dve_b(v):
        v.wait_ge(s_f, 2)
        v.reduce_max(lm11[0:1, 0:1], psT[0:1, 0:128], axis=AX.X).then_inc(s_f, 1)
    blk.vector(dve_b)

    def g_a(g):
        g.wait_ge(s_f, 3)
        g.dma_start(out=lmout_dram, in_=lm11[0:1, 0:1]).then_inc(s_fd, 16)
        g.wait_ge(s_fd, 16)
        if g_extra_wait is not None:
            g_extra_wait(g)
    blk.gpsimd(g_a)


def build_L2():
    nc = bass.Bass("TRN2", target_bir_lowering=False)
    x3bf = nc.dram_tensor("x3bf", [TOK, CM], BF16, kind="ExternalInput").ap()
    lmx8 = nc.dram_tensor("lmx8", [1, 8], F32, kind="ExternalInput").ap()
    wupdq = nc.dram_tensor("wupdq", [ID, CM], BF16, kind="ExternalInput").ap()
    wgatedq = nc.dram_tensor("wgatedq", [ID, CM], BF16, kind="ExternalInput").ap()
    consts, onesb, identf, onesf = common_inputs(nc)

    upT = nc.dram_tensor("upT", [ID, TOK], BF16, kind="ExternalOutput").ap()
    gateT = nc.dram_tensor("gateT", [ID, TOK], BF16, kind="ExternalOutput").ap()
    lmug = nc.dram_tensor("lmug", [2, 1], F32, kind="ExternalOutput").ap()

    xq_dram = nc.dram_tensor("xq_dram", [TOK, CM], BF16, kind="Internal").ap()

    NT = TOK // 128
    NM = ID // 128
    NTC = TOK // 512
    NC8 = CM // 128
    NK = NM * NTC

    with ExitStack() as est:
        e = est.enter_context
        cs = e(nc.sbuf_tensor("cs", [128, NCONST], F32))
        ones_s = e(nc.sbuf_tensor("ones_s", [128, 128], BF16))
        onesf_s = e(nc.sbuf_tensor("onesf_s", [128, 128], F32))
        idf_s = e(nc.sbuf_tensor("idf_s", [128, 128], F32))
        lm8 = e(nc.sbuf_tensor("lm8", [1, 8], F32))
        sx = e(nc.sbuf_tensor("sx", [1, 4], F32))
        sxb = e(nc.sbuf_tensor("sxb", [128, 3], F32))
        x3d = [e(nc.sbuf_tensor(f"x3d{i}", [128, CM], BF16)) for i in range(2)]
        xqi = [e(nc.sbuf_tensor(f"xqi{i}", [128, CM], I32)) for i in range(2)]
        xqf = [e(nc.sbuf_tensor(f"xqf{i}", [128, CM], F32)) for i in range(2)]
        xqd = [e(nc.sbuf_tensor(f"xqd{i}", [128, CM], F32)) for i in range(2)]
        xqb = [e(nc.sbuf_tensor(f"xqb{i}", [128, CM], BF16)) for i in range(2)]
        wT = [e(nc.sbuf_tensor(f"wT{i}", [128, ID], BF16)) for i in range(NC8)]
        gT = [e(nc.sbuf_tensor(f"gT{i}", [128, ID], BF16)) for i in range(NC8)]
        xqT = [e(nc.sbuf_tensor(f"xqT{i}", [128, TOK], BF16)) for i in range(NC8)]
        ev = [e(nc.sbuf_tensor(f"ev{i}", [128, 512], BF16)) for i in range(4)]
        mxcol = e(nc.sbuf_tensor("mxcol", [128, 2 * NK], F32))
        sc128 = e(nc.sbuf_tensor("sc128", [128, 128], F32))
        lm2 = e(nc.sbuf_tensor("lm2", [2, 1], F32))
        ps = [e(nc.psum_tensor(f"ps{i}", [128, 512], F32)) for i in range(4)]
        psb = e(nc.psum_tensor("psb", [128, 128], F32))
        psT = e(nc.psum_tensor("psT", [128, 128], F32))

        s_ld0 = e(nc.semaphore("s_ld0"))
        s_sc = e(nc.semaphore("s_sc"))
        s_bc = e(nc.semaphore("s_bc"))
        s_xl = [e(nc.semaphore(f"s_xl{j}")) for j in range(2)]
        s_xa = e(nc.semaphore("s_xa"))
        s_xv = e(nc.semaphore("s_xv"))
        s_xs = [e(nc.semaphore(f"s_xs{j}")) for j in range(2)]
        s_wl = e(nc.semaphore("s_wl"))
        s_mm = e(nc.semaphore("s_mm"))
        s_ev = e(nc.semaphore("s_ev"))
        s_rd = e(nc.semaphore("s_rd"))
        s_sb = [e(nc.semaphore(f"s_sb{j}")) for j in range(4)]
        s_f = e(nc.semaphore("s_f"))
        s_fd = e(nc.semaphore("s_fd"))
        blk = e(nc.Block())

        def sync_b(s):
            s.dma_start(out=cs[:], in_=consts).then_inc(s_ld0, 16)
            s.dma_start(out=ones_s[:], in_=onesb).then_inc(s_ld0, 16)
            s.dma_start(out=onesf_s[:], in_=onesf).then_inc(s_ld0, 16)
            s.dma_start(out=idf_s[:], in_=identf).then_inc(s_ld0, 16)
            s.dma_start(out=lm8[:], in_=lmx8).then_inc(s_ld0, 16)
            for c in range(NC8):
                s.dma_start(out=wT[c][:], in_=wupdq[:, c * 128:(c + 1) * 128],
                            transpose=True).then_inc(s_wl, 16)
            for c in range(NC8):
                s.dma_start(out=gT[c][:], in_=wgatedq[:, c * 128:(c + 1) * 128],
                            transpose=True).then_inc(s_wl, 16)
            for i in range(NT):
                j = i % 2
                if i >= 2:
                    s.wait_ge(s_xa, 2 * i - 3)
                s.dma_start(out=x3d[j][:], in_=x3bf[i * 128:(i + 1) * 128, :])\
                    .then_inc(s_xl[j], 16)
            for j in range(2):
                s.wait_ge(s_xs[j], 16 * (NT // 2))
            for c in range(NC8):
                s.dma_start(out=xqT[c][:], in_=xq_dram[:, c * 128:(c + 1) * 128],
                            transpose=True).then_inc(s_wl, 16)
        blk.sync(sync_b)

        def dve_pre(v):
            v.wait_ge(s_ld0, 80)
        blk.vector(dve_pre)
        _scale_prep(nc, blk, cs, onesf_s, lm8, sx, sxb, psb, s_sc, s_bc, 3)

        def act_xq(a):
            a.wait_ge(s_bc, 2)
            for i in range(NT):
                j = i % 2
                a.wait_ge(s_xl[j], 16 * (i // 2 + 1))
                if i >= 2:
                    a.wait_ge(s_xv, 4 * (i - 2) + 3)
                a.activation(xqf[j][:], x3d[j][:], AF.Copy,
                             scale=sxb[:, 1:2]).then_inc(s_xa, 1)
                a.wait_ge(s_xa, 2 * i + 1)
                a.activation(xqi[j][:], xqf[j][:], AF.Copy).then_inc(s_xa, 1)
        blk.scalar(act_xq)

        def dve_xq(v):
            for i in range(NT):
                j = i % 2
                v.wait_ge(s_xa, 2 * i + 2)
                if i >= 2:
                    v.wait_ge(s_xs[j], 16 * (i // 2))
                dve_round4(v, xqd[j][:], xqf[j][:], xqi[j][:], xqb[j][:],
                           sxb, s_xv, 4 * i)
        blk.vector(dve_xq)

        def g_xq(g):
            for i in range(NT):
                j = i % 2
                g.wait_ge(s_xv, 4 * (i + 1))
                g.dma_start(out=xq_dram[i * 128:(i + 1) * 128, :],
                            in_=xqb[j][:]).then_inc(s_xs[j], 16)
        blk.gpsimd(g_xq)

        def pe_mm(t):
            t.wait_ge(s_wl, 16 * 3 * NC8)
            k = 0
            for wsel in range(2):
                W = wT if wsel == 0 else gT
                for m in range(NM):
                    for tc in range(NTC):
                        pk = k % 4
                        if k >= 4:
                            t.wait_ge(s_ev, k - 3)
                        mm = None
                        for c in range(NC8):
                            mm = t.matmul(ps[pk][:],
                                          W[c][:, m * 128:(m + 1) * 128],
                                          xqT[c][:, tc * 512:(tc + 1) * 512],
                                          start=(c == 0), stop=(c == NC8 - 1))
                        mm.then_inc(s_mm, 1)
                        k += 1
        blk.tensor(pe_mm)

        def act_ev(a):
            for k in range(2 * NK):
                a.wait_ge(s_mm, k + 1)
                if k >= 4:
                    a.wait_ge(s_sb[k % 4], 16 * (k // 4))
                a.activation(ev[k % 4][:], ps[k % 4][:], AF.Copy).then_inc(s_ev, 1)
        blk.scalar(act_ev)

        def dve_rd(v):
            for k in range(2 * NK):
                v.wait_ge(s_ev, k + 1)
                v.reduce_max(mxcol[:, k:k + 1], ev[k % 4][:], axis=AX.X,
                             apply_absolute_value=True).then_inc(s_rd, 1)
        blk.vector(dve_rd)

        def g_st(g):
            for k in range(2 * NK):
                g.wait_ge(s_rd, k + 1)
                wsel, r = divmod(k, NK)
                m, tc = divmod(r, NTC)
                dst = upT if wsel == 0 else gateT
                g.dma_start(out=dst[m * 128:(m + 1) * 128,
                                    tc * 512:(tc + 1) * 512],
                            in_=ev[k % 4][:]).then_inc(s_sb[k % 4], 16)
        blk.gpsimd(g_st)

        def dve_fin(v):
            v.wait_ge(s_rd, 2 * NK)
            v.reduce_max(sc128[:, 0:1], mxcol[:, 0:NK], axis=AX.X).then_inc(s_f, 1)
            v.reduce_max(sc128[:, 1:2], mxcol[:, NK:2 * NK], axis=AX.X)\
                .then_inc(s_f, 1)
        blk.vector(dve_fin)

        def pe_fin(t):
            t.wait_ge(s_f, 2)
            t.transpose(psT[:], sc128[:], idf_s[:]).then_inc(s_f, 1)
        blk.tensor(pe_fin)

        def dve_fin2(v):
            v.wait_ge(s_f, 3)
            v.reduce_max(lm2[0:2, 0:1], psT[0:2, 0:128], axis=AX.X).then_inc(s_f, 1)
        blk.vector(dve_fin2)

        def g_fin(g):
            g.wait_ge(s_f, 4)
            g.dma_start(out=lmug, in_=lm2[0:2, 0:1]).then_inc(s_fd, 16)
            g.wait_ge(s_fd, 16)
            for j in range(4):
                g.wait_ge(s_sb[j], 16 * (2 * NK // 4))
        blk.gpsimd(g_fin)

    return nc


def build_L3():
    nc = bass.Bass("TRN2", target_bir_lowering=False)
    gateT = nc.dram_tensor("gateT", [ID, TOK], BF16, kind="ExternalInput").ap()
    lmg8 = nc.dram_tensor("lmg8", [1, 8], F32, kind="ExternalInput").ap()
    consts, onesb, identf, onesf = common_inputs(nc)
    g2T = nc.dram_tensor("g2T", [ID, TOK], BF16, kind="ExternalOutput").ap()
    lmg2 = nc.dram_tensor("lmg2", [1, 1], F32, kind="ExternalOutput").ap()

    NR = ID // 128

    with ExitStack() as est:
        e = est.enter_context
        cs = e(nc.sbuf_tensor("cs", [128, NCONST], F32))
        ones_s = e(nc.sbuf_tensor("ones_s", [128, 128], BF16))
        onesf_s = e(nc.sbuf_tensor("onesf_s", [128, 128], F32))
        idf_s = e(nc.sbuf_tensor("idf_s", [128, 128], F32))
        lm8 = e(nc.sbuf_tensor("lm8", [1, 8], F32))
        srow = e(nc.sbuf_tensor("srow", [1, 4], F32))
        sb = e(nc.sbuf_tensor("sb", [128, 3], F32))
        gt = [e(nc.sbuf_tensor(f"gt{i}", [128, TOK], BF16)) for i in range(2)]
        gqi = [e(nc.sbuf_tensor(f"gqi{i}", [128, TOK], I32)) for i in range(2)]
        gq = [e(nc.sbuf_tensor(f"gq{i}", [128, TOK], F32)) for i in range(2)]
        ygf = [e(nc.sbuf_tensor(f"ygf{i}", [128, TOK], F32)) for i in range(2)]
        gqd = [e(nc.sbuf_tensor(f"gqd{i}", [128, TOK], F32)) for i in range(2)]
        sig = [e(nc.sbuf_tensor(f"sig{i}", [128, TOK], F32)) for i in range(2)]
        yuf = [e(nc.sbuf_tensor(f"yuf{i}", [128, TOK], F32)) for i in range(2)]
        qd = [e(nc.sbuf_tensor(f"qd{i}", [128, TOK], F32)) for i in range(2)]
        qi = [e(nc.sbuf_tensor(f"qi{i}", [128, TOK], I32)) for i in range(2)]
        sfin = [e(nc.sbuf_tensor(f"sfin{i}", [128, TOK], F32)) for i in range(2)]
        g2o = [e(nc.sbuf_tensor(f"g2o{i}", [128, TOK], BF16)) for i in range(2)]
        mxcol = e(nc.sbuf_tensor("mxcol", [128, NR], F32))
        sc128 = e(nc.sbuf_tensor("sc128", [128, 128], F32))
        lm11 = e(nc.sbuf_tensor("lm11", [1, 1], F32))
        psb = e(nc.psum_tensor("psb", [128, 128], F32))
        psT = e(nc.psum_tensor("psT", [128, 128], F32))

        s_ld0 = e(nc.semaphore("s_ld0"))
        s_sc = e(nc.semaphore("s_sc"))
        s_bc = e(nc.semaphore("s_bc"))
        s_l = [e(nc.semaphore(f"s_l{j}")) for j in range(2)]
        s_a1 = e(nc.semaphore("s_a1"))
        s_v1 = e(nc.semaphore("s_v1"))
        s_a2 = e(nc.semaphore("s_a2"))
        s_v2 = e(nc.semaphore("s_v2"))
        s_v3 = e(nc.semaphore("s_v3"))
        s_v4 = e(nc.semaphore("s_v4"))
        s_rd = e(nc.semaphore("s_rd"))
        s_sb = [e(nc.semaphore(f"s_sb{j}")) for j in range(2)]
        s_f = e(nc.semaphore("s_f"))
        s_fd = e(nc.semaphore("s_fd"))
        blk = e(nc.Block())

        def sync_b(s):
            s.dma_start(out=cs[:], in_=consts).then_inc(s_ld0, 16)
            s.dma_start(out=ones_s[:], in_=onesb).then_inc(s_ld0, 16)
            s.dma_start(out=onesf_s[:], in_=onesf).then_inc(s_ld0, 16)
            s.dma_start(out=idf_s[:], in_=identf).then_inc(s_ld0, 16)
            s.dma_start(out=lm8[:], in_=lmg8).then_inc(s_ld0, 16)
            for i in range(NR):
                j = i % 2
                if i >= 2:
                    s.wait_ge(s_a1, 2 * i - 3)
                s.dma_start(out=gt[j][:], in_=gateT[i * 128:(i + 1) * 128, :])\
                    .then_inc(s_l[j], 16)
        blk.sync(sync_b)

        def dve_pre(v):
            v.wait_ge(s_ld0, 80)
        blk.vector(dve_pre)
        _scale_prep(nc, blk, cs, onesf_s, lm8, srow, sb, psb, s_sc, s_bc, 3)

        def act_all(a):
            a.wait_ge(s_bc, 2)
            for i in range(NR + 1):
                if i < NR:
                    j = i % 2
                    a.wait_ge(s_l[j], 16 * (i // 2 + 1))
                    if i >= 2:
                        a.wait_ge(s_v1, 4 * (i - 2) + 3)
                    a.activation(ygf[j][:], gt[j][:], AF.Copy,
                                 scale=sb[:, 1:2]).then_inc(s_a1, 1)
                    a.wait_ge(s_a1, 2 * i + 1)
                    a.activation(gqi[j][:], ygf[j][:], AF.Copy)\
                        .then_inc(s_a1, 1)
                if i >= 1:
                    k = i - 1
                    jk = k % 2
                    a.wait_ge(s_v1, 4 * k + 4)
                    if k >= 2:
                        a.wait_ge(s_v2, 4 * (k - 2) + 3)
                    a.activation(sig[jk][:], gq[jk][:], AF.Sigmoid)\
                        .then_inc(s_a2, 1)
                    a.wait_ge(s_a2, 3 * k + 1)
                    a.activation(yuf[jk][:], sig[jk][:], AF.Copy,
                                 scale=cc(cs, "c65536")).then_inc(s_a2, 1)
                    a.wait_ge(s_a2, 3 * k + 2)
                    a.activation(qi[jk][:], yuf[jk][:], AF.Copy)\
                        .then_inc(s_a2, 1)
        blk.scalar(act_all)

        def dve_all(v):
            for i in range(NR + 1):
                if i < NR:
                    j = i % 2
                    v.wait_ge(s_a1, 2 * i + 2)
                    if i >= 2:
                        v.wait_ge(s_a2, 3 * (i - 2) + 1)
                        v.wait_ge(s_v4, i - 1)
                    dve_round4(v, gqd[j][:], ygf[j][:], gqi[j][:], gq[j][:],
                               sb, s_v1, 4 * i)
                if i >= 1:
                    k = i - 1
                    jk = k % 2
                    v.wait_ge(s_a2, 3 * k + 3)
                    v.scalar_tensor_tensor(qd[jk][:], qi[jk][:], -1.0,
                                           yuf[jk][:], ALU.mult, ALU.add)\
                        .then_inc(s_v2, 1)
                    v.wait_ge(s_v2, 4 * k + 1)
                    v.scalar_tensor_tensor(yuf[jk][:], qd[jk][:], 0.5,
                                           qi[jk][:], ALU.is_ge, ALU.add)\
                        .then_inc(s_v2, 1)
                    v.wait_ge(s_v2, 4 * k + 2)
                    v.scalar_tensor_tensor(qd[jk][:], qd[jk][:], -0.5,
                                           yuf[jk][:], ALU.is_ge, ALU.add)\
                        .then_inc(s_v2, 1)
                    v.wait_ge(s_v2, 4 * k + 3)
                    v.tensor_scalar(sfin[jk][:], qd[jk][:], cc(cs, "c65536"),
                                    cc(cs, "sc216"), ALU.min, ALU.mult)\
                        .then_inc(s_v2, 1)
                    v.wait_ge(s_v2, 4 * k + 4)
                    if k >= 2:
                        v.wait_ge(s_sb[jk], 16 * (k // 2))
                    v.scalar_tensor_tensor(g2o[jk][:], sfin[jk][:],
                                           cc(cs, "nsc216"), gq[jk][:],
                                           ALU.add, ALU.mult).then_inc(s_v4, 1)
                    v.wait_ge(s_v4, k + 1)
                    v.reduce_max(mxcol[:, k:k + 1], g2o[jk][:], axis=AX.X,
                                 apply_absolute_value=True).then_inc(s_rd, 1)
        blk.vector(dve_all)

        def g_st(g):
            for i in range(NR):
                j = i % 2
                g.wait_ge(s_rd, i + 1)
                g.dma_start(out=g2T[i * 128:(i + 1) * 128, :],
                            in_=g2o[j][:]).then_inc(s_sb[j], 16)
        blk.gpsimd(g_st)

        def gw(g):
            for j in range(2):
                g.wait_ge(s_sb[j], 16 * (NR // 2))
        _xpart_max_store(nc, blk, idf_s, mxcol, NR, sc128, psT, lmg2, lm11,
                         s_rd, NR, s_f, s_fd, g_extra_wait=gw)

    return nc


def build_L4():
    nc = bass.Bass("TRN2", target_bir_lowering=False)
    g2Tin = nc.dram_tensor("g2Tin", [ID, TOK], BF16, kind="ExternalInput").ap()
    upTin = nc.dram_tensor("upTin", [ID, TOK], BF16, kind="ExternalInput").ap()
    lmg28 = nc.dram_tensor("lmg28", [1, 8], F32, kind="ExternalInput").ap()
    lmu8 = nc.dram_tensor("lmu8", [1, 8], F32, kind="ExternalInput").ap()
    consts, onesb, identf, onesf = common_inputs(nc)
    oT = nc.dram_tensor("oT", [ID, TOK], BF16, kind="ExternalOutput").ap()
    lmo = nc.dram_tensor("lmo", [1, 1], F32, kind="ExternalOutput").ap()

    NR = ID // 128

    with ExitStack() as est:
        e = est.enter_context
        cs = e(nc.sbuf_tensor("cs", [128, NCONST], F32))
        ones_s = e(nc.sbuf_tensor("ones_s", [128, 128], BF16))
        onesf_s = e(nc.sbuf_tensor("onesf_s", [128, 128], F32))
        idf_s = e(nc.sbuf_tensor("idf_s", [128, 128], F32))
        lmg = e(nc.sbuf_tensor("lmg", [1, 8], F32))
        lmu = e(nc.sbuf_tensor("lmu", [1, 8], F32))
        srow = e(nc.sbuf_tensor("srow", [1, 4], F32))
        srow2 = e(nc.sbuf_tensor("srow2", [1, 4], F32))
        sbg = e(nc.sbuf_tensor("sbg", [128, 3], F32))
        sbu = e(nc.sbuf_tensor("sbu", [128, 3], F32))
        g2t = [e(nc.sbuf_tensor(f"g2t{i}", [128, TOK], BF16)) for i in range(2)]
        upt = [e(nc.sbuf_tensor(f"upt{i}", [128, TOK], BF16)) for i in range(2)]
        g2i = [e(nc.sbuf_tensor(f"g2i{i}", [128, TOK], I32)) for i in range(2)]
        upi = [e(nc.sbuf_tensor(f"upi{i}", [128, TOK], I32)) for i in range(2)]
        g2f = [e(nc.sbuf_tensor(f"g2f{i}", [128, TOK], F32)) for i in range(2)]
        upf = [e(nc.sbuf_tensor(f"upf{i}", [128, TOK], F32)) for i in range(2)]
        qdd = e(nc.sbuf_tensor("qdd", [128, TOK], F32))
        g2q = [e(nc.sbuf_tensor(f"g2q{i}", [128, TOK], F32)) for i in range(2)]
        upq = [e(nc.sbuf_tensor(f"upq{i}", [128, TOK], F32)) for i in range(2)]
        oo = [e(nc.sbuf_tensor(f"oo{i}", [128, TOK], BF16)) for i in range(2)]
        mxcol = e(nc.sbuf_tensor("mxcol", [128, NR], F32))
        sc128 = e(nc.sbuf_tensor("sc128", [128, 128], F32))
        lm11 = e(nc.sbuf_tensor("lm11", [1, 1], F32))
        psb = e(nc.psum_tensor("psb", [128, 128], F32))
        psb2 = e(nc.psum_tensor("psb2", [128, 128], F32))
        psT = e(nc.psum_tensor("psT", [128, 128], F32))

        s_ld0 = e(nc.semaphore("s_ld0"))
        s_sc = e(nc.semaphore("s_sc"))
        s_bc = e(nc.semaphore("s_bc"))
        s_sc2 = e(nc.semaphore("s_sc2"))
        s_bc2 = e(nc.semaphore("s_bc2"))
        s_l = [e(nc.semaphore(f"s_l{j}")) for j in range(2)]
        s_a1 = e(nc.semaphore("s_a1"))
        s_v1 = e(nc.semaphore("s_v1"))
        s_v2 = e(nc.semaphore("s_v2"))
        s_rd = e(nc.semaphore("s_rd"))
        s_sb = [e(nc.semaphore(f"s_sb{j}")) for j in range(2)]
        s_f = e(nc.semaphore("s_f"))
        s_fd = e(nc.semaphore("s_fd"))
        blk = e(nc.Block())

        def sync_b(s):
            s.dma_start(out=cs[:], in_=consts).then_inc(s_ld0, 16)
            s.dma_start(out=ones_s[:], in_=onesb).then_inc(s_ld0, 16)
            s.dma_start(out=onesf_s[:], in_=onesf).then_inc(s_ld0, 16)
            s.dma_start(out=idf_s[:], in_=identf).then_inc(s_ld0, 16)
            s.dma_start(out=lmg[:], in_=lmg28).then_inc(s_ld0, 16)
            s.dma_start(out=lmu[:], in_=lmu8).then_inc(s_ld0, 16)
            for i in range(NR):
                j = i % 2
                if i >= 2:
                    s.wait_ge(s_a1, 4 * i - 5)
                s.dma_start(out=g2t[j][:],
                            in_=g2Tin[i * 128:(i + 1) * 128, :]).then_inc(s_l[j], 16)
                s.dma_start(out=upt[j][:],
                            in_=upTin[i * 128:(i + 1) * 128, :]).then_inc(s_l[j], 16)
        blk.sync(sync_b)

        def dve_pre(v):
            v.wait_ge(s_ld0, 96)
        blk.vector(dve_pre)
        _scale_prep(nc, blk, cs, onesf_s, lmg, srow, sbg, psb, s_sc, s_bc, 3)
        _scale_prep(nc, blk, cs, onesf_s, lmu, srow2, sbu, psb2, s_sc2, s_bc2, 3)

        def act1(a):
            a.wait_ge(s_bc, 2)
            a.wait_ge(s_bc2, 2)
            for i in range(NR):
                j = i % 2
                a.wait_ge(s_l[j], 32 * (i // 2 + 1))
                if i >= 2:
                    a.wait_ge(s_v1, 8 * (i - 2) + 7)
                a.activation(g2f[j][:], g2t[j][:], AF.Copy,
                             scale=sbg[:, 1:2]).then_inc(s_a1, 1)
                a.wait_ge(s_a1, 4 * i + 1)
                a.activation(g2i[j][:], g2f[j][:], AF.Copy).then_inc(s_a1, 1)
                a.wait_ge(s_a1, 4 * i + 2)
                a.activation(upf[j][:], upt[j][:], AF.Copy,
                             scale=sbu[:, 1:2]).then_inc(s_a1, 1)
                a.wait_ge(s_a1, 4 * i + 3)
                a.activation(upi[j][:], upf[j][:], AF.Copy).then_inc(s_a1, 1)
        blk.scalar(act1)

        def dve1(v):
            for i in range(NR):
                j = i % 2
                v.wait_ge(s_a1, 4 * i + 2)
                if i >= 1:
                    v.wait_ge(s_v1, 8 * i)
                if i >= 2:
                    v.wait_ge(s_v2, i - 1)
                dve_round4(v, qdd[:], g2f[j][:], g2i[j][:], g2q[j][:],
                           sbg, s_v1, 8 * i)
                v.wait_ge(s_a1, 4 * i + 4)
                v.wait_ge(s_v1, 8 * i + 4)
                dve_round4(v, qdd[:], upf[j][:], upi[j][:], upq[j][:],
                           sbu, s_v1, 8 * i + 4)
                v.wait_ge(s_v1, 8 * (i + 1))
                if i >= 2:
                    v.wait_ge(s_sb[j], 16 * (i // 2))
                v.tensor_tensor(oo[j][:], g2q[j][:], upq[j][:],
                                ALU.mult).then_inc(s_v2, 1)
                v.wait_ge(s_v2, i + 1)
                v.reduce_max(mxcol[:, i:i + 1], oo[j][:], axis=AX.X,
                             apply_absolute_value=True).then_inc(s_rd, 1)
        blk.vector(dve1)

        def g_st(g):
            for i in range(NR):
                j = i % 2
                g.wait_ge(s_rd, i + 1)
                g.dma_start(out=oT[i * 128:(i + 1) * 128, :],
                            in_=oo[j][:]).then_inc(s_sb[j], 16)
        blk.gpsimd(g_st)

        def gw(g):
            for j in range(2):
                g.wait_ge(s_sb[j], 16 * (NR // 2))
        _xpart_max_store(nc, blk, idf_s, mxcol, NR, sc128, psT, lmo, lm11,
                         s_rd, NR, s_f, s_fd, g_extra_wait=gw)

    return nc


def build_L5():
    nc = bass.Bass("TRN2", target_bir_lowering=False)
    oTin = nc.dram_tensor("oTin", [ID, TOK], BF16, kind="ExternalInput").ap()
    lmo8 = nc.dram_tensor("lmo8", [1, 8], F32, kind="ExternalInput").ap()
    wdowndq = nc.dram_tensor("wdowndq", [CM, ID], BF16, kind="ExternalInput").ap()
    x2in = nc.dram_tensor("x2in", [TOK, CM], F32, kind="ExternalInput").ap()
    consts, onesb, identf, onesf = common_inputs(nc)
    yout = nc.dram_tensor("yout", [TOK, CM], F32, kind="ExternalOutput").ap()

    NR = ID // 128
    NT = TOK // 128

    with ExitStack() as est:
        e = est.enter_context
        cs = e(nc.sbuf_tensor("cs", [128, NCONST], F32))
        ones_s = e(nc.sbuf_tensor("ones_s", [128, 128], BF16))
        onesf_s = e(nc.sbuf_tensor("onesf_s", [128, 128], F32))
        idf_s = e(nc.sbuf_tensor("idf_s", [128, 128], F32))
        lm8 = e(nc.sbuf_tensor("lm8", [1, 8], F32))
        srow = e(nc.sbuf_tensor("srow", [1, 4], F32))
        sb = e(nc.sbuf_tensor("sb", [128, 3], F32))
        ot = [e(nc.sbuf_tensor(f"ot{i}", [128, TOK], BF16)) for i in range(2)]
        oi = [e(nc.sbuf_tensor(f"oi{i}", [128, TOK], I32)) for i in range(2)]
        of = [e(nc.sbuf_tensor(f"of{i}", [128, TOK], F32)) for i in range(2)]
        od = e(nc.sbuf_tensor("od", [128, TOK], F32))
        oq = [e(nc.sbuf_tensor(f"oq{i}", [128, TOK], BF16)) for i in range(NR)]
        wdT = [e(nc.sbuf_tensor(f"wdT{i}", [128, CM], BF16)) for i in range(NR)]
        x2s = [e(nc.sbuf_tensor(f"x2s{i}", [128, CM], F32)) for i in range(2)]
        stats = e(nc.sbuf_tensor("stats", [128, 16], F32))
        fsb = [of[0][:, 0:CM], of[1][:, 0:CM]]
        yo = [of[0][:, CM:2 * CM], of[1][:, CM:2 * CM]]
        dump = od
        psb = e(nc.psum_tensor("psb", [128, 128], F32))
        pm = [e(nc.psum_tensor(f"pm{i}", [128, CM], F32)) for i in range(2)]

        s_ld0 = e(nc.semaphore("s_ld0"))
        s_sc = e(nc.semaphore("s_sc"))
        s_bc = e(nc.semaphore("s_bc"))
        s_l = [e(nc.semaphore(f"s_l{j}")) for j in range(2)]
        s_wl = e(nc.semaphore("s_wl"))
        s_xl = [e(nc.semaphore(f"s_xl{j}")) for j in range(2)]
        s_a1 = e(nc.semaphore("s_a1"))
        s_a1x = e(nc.semaphore("s_a1x"))
        s_v1 = e(nc.semaphore("s_v1"))
        s_mm = e(nc.semaphore("s_mm"))
        s_v2 = e(nc.semaphore("s_v2"))
        s_a2 = e(nc.semaphore("s_a2"))
        s_v3 = e(nc.semaphore("s_v3"))
        s_sb = [e(nc.semaphore(f"s_sb{j}")) for j in range(2)]
        blk = e(nc.Block())

        def sync_b(s):
            s.dma_start(out=cs[:], in_=consts).then_inc(s_ld0, 16)
            s.dma_start(out=ones_s[:], in_=onesb).then_inc(s_ld0, 16)
            s.dma_start(out=onesf_s[:], in_=onesf).then_inc(s_ld0, 16)
            s.dma_start(out=idf_s[:], in_=identf).then_inc(s_ld0, 16)
            s.dma_start(out=lm8[:], in_=lmo8).then_inc(s_ld0, 16)
            for c in range(NR):
                s.dma_start(out=wdT[c][:], in_=wdowndq[:, c * 128:(c + 1) * 128],
                            transpose=True).then_inc(s_wl, 16)
            for i in range(NR):
                j = i % 2
                if i >= 2:
                    s.wait_ge(s_a1, 2 * i - 3)
                s.dma_start(out=ot[j][:], in_=oTin[i * 128:(i + 1) * 128, :])\
                    .then_inc(s_l[j], 16)
            for tb in range(NT):
                j = tb % 2
                if tb >= 2:
                    s.wait_ge(s_v3, 2 * (tb - 1))
                s.dma_start(out=x2s[j][:],
                            in_=x2in[tb * 128:(tb + 1) * 128, :]).then_inc(s_xl[j], 16)
        blk.sync(sync_b)

        def dve_pre(v):
            v.wait_ge(s_ld0, 80)
        blk.vector(dve_pre)
        _scale_prep(nc, blk, cs, onesf_s, lm8, srow, sb, psb, s_sc, s_bc, 3)

        def act1(a):
            a.wait_ge(s_bc, 2)
            for i in range(NR):
                j = i % 2
                a.wait_ge(s_l[j], 16 * (i // 2 + 1))
                if i >= 2:
                    a.wait_ge(s_v1, 4 * (i - 2) + 3)
                a.activation(of[j][:], ot[j][:], AF.Copy,
                             scale=sb[:, 1:2]).then_inc(s_a1, 1)
                a.wait_ge(s_a1, 2 * i + 1)
                a.activation(oi[j][:], of[j][:], AF.Copy).then_inc(s_a1, 1)
        blk.scalar(act1)

        def dve1(v):
            for i in range(NR):
                j = i % 2
                v.wait_ge(s_a1, 2 * i + 2)
                if i >= 1:
                    v.wait_ge(s_v1, 4 * i)
                dve_round4(v, od[:], of[j][:], oi[j][:], oq[i][:],
                           sb, s_v1, 4 * i)
        blk.vector(dve1)

        def pe_mm(t):
            t.wait_ge(s_wl, 16 * NR)
            t.wait_ge(s_v1, 4 * NR)
            for tb in range(NT):
                if tb >= 2:
                    t.wait_ge(s_a1x, tb - 1)
                mm = None
                for half in range(2):
                    for c in range(NR):
                        mm = t.matmul(pm[tb % 2][:, half * 512:(half + 1) * 512],
                                      oq[c][:, tb * 128:(tb + 1) * 128],
                                      wdT[c][:, half * 512:(half + 1) * 512],
                                      start=(c == 0), stop=(c == NR - 1))
                mm.then_inc(s_mm, 1)
        blk.tensor(pe_mm)

        def act_tail(a):
            for i in range(NT + 1):
                if i < NT:
                    tb = i
                    a.wait_ge(s_mm, tb + 1)
                    if tb >= 2:
                        a.wait_ge(s_v3, 2 * (tb - 1) - 1)
                    a.activation(fsb[tb % 2], pm[tb % 2][:], AF.Copy)\
                        .then_inc(s_a1x, 1)
                if i >= 1:
                    k = i - 1
                    kc = 8 * (k % 2)
                    a.wait_ge(s_v2, 7 * (k + 1))
                    a.activation(stats[:, kc + 6:kc + 7], stats[:, kc + 5:kc + 6],
                                 AF.Ln, bias=cc(cs, "ln_eps")).then_inc(s_a2, 1)
                    a.wait_ge(s_a2, 2 * k + 1)
                    a.activation(stats[:, kc + 7:kc + 8], stats[:, kc + 6:kc + 7],
                                 AF.Exp, scale=cc(cs, "neg_half")).then_inc(s_a2, 1)
        blk.scalar(act_tail)

        def dve_tail(v):
            for i in range(NT + 1):
                if i < NT:
                    tb = i
                    tc = 8 * (tb % 2)
                    base = 7 * tb
                    v.wait_ge(s_a1x, tb + 1)
                    if tb >= 2:
                        v.wait_ge(s_v3, 2 * (tb - 1))
                    v.reduce_sum(stats[:, tc + 0:tc + 1], fsb[tb % 2], axis=AX.X)\
                        .then_inc(s_v2, 1)
                    v.scalar_tensor_tensor(dump[:, 8:8 + CM], fsb[tb % 2], 1.0,
                                           fsb[tb % 2], ALU.bypass, ALU.mult,
                                           accum_out=stats[:, tc + 1:tc + 2])\
                        .then_inc(s_v2, 1)
                    v.wait_ge(s_v2, base + 2)
                    v.tensor_scalar(stats[:, tc + 2:tc + 3], stats[:, tc + 0:tc + 1],
                                    cc(cs, "inv_cm"), None, ALU.mult).then_inc(s_v2, 1)
                    v.wait_ge(s_v2, base + 3)
                    v.tensor_copy(dump[:, 0:8], cs[:, 0:8])
                    v.tensor_scalar(stats[:, tc + 3:tc + 4], stats[:, tc + 2:tc + 3],
                                    stats[:, tc + 2:tc + 3], cc(cs, "neg_one"),
                                    ALU.mult, ALU.mult).then_inc(s_v2, 1)
                    v.wait_ge(s_v2, base + 4)
                    v.tensor_scalar(stats[:, tc + 4:tc + 5], stats[:, tc + 1:tc + 2],
                                    cc(cs, "inv_cm"), None, ALU.mult).then_inc(s_v2, 1)
                    v.wait_ge(s_v2, base + 5)
                    v.tensor_copy(dump[:, 0:8], cs[:, 0:8])
                    v.tensor_tensor(stats[:, tc + 5:tc + 6], stats[:, tc + 4:tc + 5],
                                    stats[:, tc + 3:tc + 4], ALU.add).then_inc(s_v2, 2)
                if i >= 1:
                    k = i - 1
                    jk = k % 2
                    kc = 8 * jk
                    v.wait_ge(s_a2, 2 * (k + 1))
                    v.wait_ge(s_xl[jk], 16 * (k // 2 + 1))
                    if k >= 2:
                        v.wait_ge(s_sb[jk], 16 * (k // 2))
                    v.tensor_scalar(yo[jk], fsb[jk], stats[:, kc + 2:kc + 3],
                                    stats[:, kc + 7:kc + 8], ALU.subtract, ALU.mult)\
                        .then_inc(s_v3, 1)
                    v.wait_ge(s_v3, 2 * k + 1)
                    v.tensor_tensor(yo[jk], yo[jk], x2s[jk][:],
                                    ALU.add).then_inc(s_v3, 1)
        blk.vector(dve_tail)

        def g_st(g):
            for tb in range(NT):
                j = tb % 2
                g.wait_ge(s_v3, 2 * (tb + 1))
                g.dma_start(out=yout[tb * 128:(tb + 1) * 128, :],
                            in_=yo[j]).then_inc(s_sb[j], 16)
            for j in range(2):
                g.wait_ge(s_sb[j], 16 * (NT // 2))
        blk.gpsimd(g_st)

    return nc


def blocks_for_half(h):
    pairs = [(k, 31 - k) for k in range(16)]
    sel = []
    for i, (a, b) in enumerate(pairs):
        if i % 2 == h:
            sel += [a, b]
    return sorted(sel)


def core_rows(c):
    h = c % 2
    rows = []
    for blkid in blocks_for_half(h):
        rows += list(range(blkid * 128, (blkid + 1) * 128))
    return rows


"""Builder for launch L1 of the sandwich kernel (per-core, parity h).

LN1, QKV, forgetting-attention (psum-resident FF cumsum, window-min bisection
pruning threshold, E=exp(-FF)*keep), bounded-softmax, PV, proj, LN2+residual,
LN3, FFN-weight dequant slices.

Host-transposed weight layouts (pure layout): watT = W_attn.T [1024,3072],
wpjT = W_proj.T [1024,1024]. headsel [128,2] f32: col0 = 1 on partitions
0..63, col1 = 1 on 64..127.
"""



TOK = 2048
CM = 1024
ID = 3072
T = 4096
NPAIR = 8


def ff_sched(h, pass_b):
    lo, hi = (4, 8) if pass_b else (0, 4)
    istart = 16 if pass_b else 0
    out = []
    for I in range(istart, 32):
        w = 128 * (I + 1)
        for jc in range(lo, min(hi, (w + 511) // 512)):
            nj = min(512, w - 512 * jc)
            out.append((I, jc, nj))
    return out


def build_L1(h, dbg=False):
    my_blocks = blocks_for_half(h)
    my_pos = {blk: i for i, blk in enumerate(my_blocks)}
    mine = set(my_blocks)
    icq = [my_blocks[4 * k:4 * k + 4] for k in range(4)]
    JCs = [max(bl) + 1 for bl in icq]
    myB = [b for b in my_blocks if b >= 16]
    myBpos = {b: i for i, b in enumerate(myB)}

    SCH = [(I, jc, nj, False) for (I, jc, nj) in ff_sched(h, False)] + \
          [(I, jc, nj, True) for (I, jc, nj) in ff_sched(h, True)]
    first_use, last_use = {}, {}
    for idx, (I, jc, nj, pb) in enumerate(SCH):
        key = (pb, jc)
        if key not in first_use:
            first_use[key] = idx
        last_use[key] = idx

    nc = bass.Bass("TRN2", target_bir_lowering=False)
    xbD = nc.dram_tensor("xb", [T, CM], F32, kind="ExternalInput").ap()
    watT = nc.dram_tensor("watT", [CM, 3 * CM], F32, kind="ExternalInput").ap()
    battn24 = nc.dram_tensor("battn24", [24, 128], F32, kind="ExternalInput").ap()
    wpjT = nc.dram_tensor("wpjT", [CM, CM], F32, kind="ExternalInput").ap()
    bvbD = nc.dram_tensor("bvb", [128, CM], F32, kind="ExternalInput").ap()
    bprojbD = nc.dram_tensor("bprojb", [128, CM], F32, kind="ExternalInput").ap()
    wup_sl = nc.dram_tensor("wup_sl", [384, CM], F32, kind="ExternalInput").ap()
    wgate_sl = nc.dram_tensor("wgate_sl", [384, CM], F32, kind="ExternalInput").ap()
    wdown_sl = nc.dram_tensor("wdown_sl", [128, ID], F32, kind="ExternalInput").ap()
    constsD = nc.dram_tensor("consts", [128, NCONST], F32, kind="ExternalInput").ap()
    onesbD = nc.dram_tensor("onesb", [128, 128], BF16, kind="ExternalInput").ap()
    identfD = nc.dram_tensor("identf", [128, 128], F32, kind="ExternalInput").ap()
    onesfD = nc.dram_tensor("onesf", [128, 128], F32, kind="ExternalInput").ap()
    cumTD = nc.dram_tensor("cumT", [128, 128], BF16, kind="ExternalInput").ap()
    lowincD = nc.dram_tensor("lowinc", [128, 128], BF16, kind="ExternalInput").ap()
    ltstrD = nc.dram_tensor("ltstr", [128, 128], BF16, kind="ExternalInput").ap()
    ltincD = nc.dram_tensor("ltinc", [128, 128], BF16, kind="ExternalInput").ap()
    headselD = nc.dram_tensor("headsel", [128, 2], BF16, kind="ExternalInput").ap()
    iotaD = nc.dram_tensor("iota32", [128, 32], F32, kind="ExternalInput").ap()

    x2o = nc.dram_tensor("x2o", [TOK, CM], F32, kind="ExternalOutput").ap()
    x3o = nc.dram_tensor("x3o", [TOK, CM], BF16, kind="ExternalOutput").ap()
    lmxo = nc.dram_tensor("lmxo", [1, 1], F32, kind="ExternalOutput").ap()
    wupdq_sl = nc.dram_tensor("wupdq_sl", [384, CM], BF16, kind="ExternalOutput").ap()
    wgatedq_sl = nc.dram_tensor("wgatedq_sl", [384, CM], BF16, kind="ExternalOutput").ap()
    wdowndq_sl = nc.dram_tensor("wdowndq_sl", [128, ID], BF16, kind="ExternalOutput").ap()
    if dbg:
        dbg_x1 = nc.dram_tensor("dbg_x1", [T, CM], BF16, kind="ExternalOutput").ap()
        dbg_E = nc.dram_tensor("dbg_E", [TOK, T], BF16, kind="ExternalOutput").ap()
        dbg_y = nc.dram_tensor("dbg_y", [TOK, CM], BF16, kind="ExternalOutput").ap()
        dbg_kT = nc.dram_tensor("dbg_kT", [CM, T], BF16, kind="ExternalOutput").ap()
        dbg_m = nc.dram_tensor("dbg_m", [1, 1], F32, kind="ExternalOutput").ap()

    x1d = nc.dram_tensor("x1d", [T, CM], BF16, kind="Internal").ap()
    qTd = nc.dram_tensor("qTd", [CM, T], BF16, kind="Internal").ap()
    kTd = nc.dram_tensor("kTd", [CM, T], BF16, kind="Internal").ap()
    Ed = nc.dram_tensor("Ed", [TOK, T], BF16, kind="Internal").ap()
    ffAd = nc.dram_tensor("ffAd", [8 * 128, TOK], F32, kind="Internal").ap()
    ydD = nc.dram_tensor("yd", [TOK, CM], BF16, kind="Internal").ap()

    est = ExitStack()
    with est:
        e = est.enter_context

        def sem(name):
            return e(nc.semaphore(name))

        cs = e(nc.sbuf_tensor("cs", [128, NCONST], F32))
        ones_s = e(nc.sbuf_tensor("ones_s", [128, 128], BF16))
        onesf_s = e(nc.sbuf_tensor("onesf_s", [128, 128], F32))
        idf_s = e(nc.sbuf_tensor("idf_s", [128, 128], F32))
        cum_s = e(nc.sbuf_tensor("cum_s", [128, 128], BF16))
        low_s = e(nc.sbuf_tensor("low_s", [128, 128], BF16))
        lts_s = e(nc.sbuf_tensor("lts_s", [128, 128], BF16))
        lti_s = e(nc.sbuf_tensor("lti_s", [128, 128], BF16))
        hsel_s = e(nc.sbuf_tensor("hsel_s", [128, 2], BF16))
        iota_s = e(nc.sbuf_tensor("iota_s", [128, 32], F32))
        bT_s = e(nc.sbuf_tensor("bT_s", [128, 24], F32))
        bvb_s = e(nc.sbuf_tensor("bvb_s", [128, CM], F32))
        bpb_s = e(nc.sbuf_tensor("bpb_s", [128, CM], F32))
        ebias = e(nc.sbuf_tensor("ebias", [128, 1], F32))
        mbc = e(nc.sbuf_tensor("mbc", [128, 144], F32))
        sc128 = e(nc.sbuf_tensor("sc128", [128, 128], F32))
        mrow = e(nc.sbuf_tensor("mrow", [2, 8], F32))
        ldump = e(nc.sbuf_tensor("ldump", [128, 8 + CM], F32))
        zer_s = e(nc.sbuf_tensor("zer_s", [128, 512], BF16))
        lm11 = e(nc.sbuf_tensor("lm11", [1, 1], F32))

        s_c0 = sem("s_c0")
        s_bt = sem("s_bt")
        s_dql = [sem(f"s_dql{j}") for j in range(2)]
        s_dqw = sem("s_dqw"); s_dqc = sem("s_dqc"); s_dqi = sem("s_dqi")
        s_dqr = sem("s_dqr"); s_dqb = sem("s_dqb")
        s_dqs = [sem(f"s_dqs{j}") for j in range(2)]
        s_x0 = [sem(f"s_x0{j}") for j in range(2)]
        s_ssq = sem("s_ssq")
        s_rstd = sem("s_rstd"); s_app = sem("s_app")
        s_l1s = [sem(f"s_l1s{j}") for j in range(2)]
        s_wl = [sem(f"s_wl{j}") for j in range(2)]
        s_wc = sem("s_wc")
        s_xT = sem("s_xT")
        s_kv = sem("s_kv")
        s_kve = [sem(f"s_kve{j}") for j in range(2)]
        s_krd = sem("s_krd"); s_qrd = sem("s_qrd")
        s_kst = [sem(f"s_kst{j}") for j in range(2)]
        s_qst = [sem(f"s_qst{j}") for j in range(2)]
        s_mb1 = sem("s_mb1"); s_mb2 = sem("s_mb2"); s_mb3 = sem("s_mb3")
        s_mb4 = sem("s_mb4"); s_mb5 = sem("s_mb5"); s_mb6 = sem("s_mb6")
        s_mb7 = sem("s_mb7"); s_mb8 = sem("s_mb8")
        s_q0l = sem("s_q0l"); s_k0l = sem("s_k0l")
        s_s0w = [sem(f"s_s0w{j}") for j in range(2)]
        s_s0r = [sem(f"s_s0r{j}") for j in range(2)]
        s_sm = sem("s_sm")
        s_ff1 = sem("s_ff1"); s_ffe = sem("s_ffe")
        s_cs = sem("s_cs"); s_cv = sem("s_cv")
        s_wmn = sem("s_wmn"); s_bis = sem("s_bis"); s_th = sem("s_th")
        s_ein = sem("s_ein"); s_ee0 = sem("s_ee0"); s_eml = sem("s_eml")
        s_est = sem("s_est")
        s_ktl = sem("s_ktl")
        s_el = sem("s_el"); s_ql = sem("s_ql")
        s_qkd = sem("s_qkd")
        s_ex = [sem(f"s_ex{j}") for j in range(2)]
        s_pm = [sem(f"s_pm{j}") for j in range(2)]
        s_pv = [sem(f"s_pv{j}") for j in range(2)]
        s_ye = [sem(f"s_ye{j}") for j in range(2)]
        s_yn = [sem(f"s_yn{j}") for j in range(2)]
        s_ys = [sem(f"s_ys{j}") for j in range(4)]
        s_ytl = [sem(f"s_ytl{j}") for j in range(2)]
        s_pj = sem("s_pj"); s_bia = sem("s_bia")
        s_xol = [sem(f"s_xol{j}") for j in range(2)]
        s_sq2 = sem("s_sq2")
        s_rs2 = sem("s_rs2"); s_re2 = sem("s_re2")
        s_sq3 = sem("s_sq3")
        s_rs3 = sem("s_rs3"); s_ap3 = sem("s_ap3"); s_mx3 = sem("s_mx3")
        s_o2 = [sem(f"s_o2{j}") for j in range(2)]
        s_o3 = [sem(f"s_o3{j}") for j in range(2)]
        s_fx = sem("s_fx"); s_fxd = sem("s_fxd")

        blk = e(nc.Block())

        def _stats(v, st, dst, src_t, base, s_cnt):
            v.reduce_sum(st[:, dst + 0:dst + 1], src_t[:], axis=AX.X)\
                .then_inc(s_cnt, 1)
            v.scalar_tensor_tensor(ldump[:, 8:8 + CM], src_t[:], 1.0, src_t[:],
                                   ALU.bypass, ALU.mult,
                                   accum_out=st[:, dst + 1:dst + 2])\
                .then_inc(s_cnt, 1)
            v.wait_ge(s_cnt, base + 2)
            v.tensor_scalar(st[:, dst + 2:dst + 3], st[:, dst + 0:dst + 1],
                            cc(cs, "inv_cm"), None, ALU.mult).then_inc(s_cnt, 1)
            v.wait_ge(s_cnt, base + 3)
            v.tensor_copy(ldump[:, 0:8], cs[:, 0:8])
            v.tensor_scalar(st[:, dst + 3:dst + 4], st[:, dst + 2:dst + 3],
                            st[:, dst + 2:dst + 3], cc(cs, "neg_one"),
                            ALU.mult, ALU.mult).then_inc(s_cnt, 1)
            v.wait_ge(s_cnt, base + 4)
            v.tensor_scalar(st[:, dst + 4:dst + 5], st[:, dst + 1:dst + 2],
                            cc(cs, "inv_cm"), None, ALU.mult).then_inc(s_cnt, 1)
            v.wait_ge(s_cnt, base + 5)
            v.tensor_copy(ldump[:, 0:8], cs[:, 0:8])
            v.tensor_tensor(st[:, dst + 5:dst + 6], st[:, dst + 4:dst + 5],
                            st[:, dst + 3:dst + 4], ALU.add).then_inc(s_cnt, 2)

        with ExitStack() as sc0:
            e0 = sc0.enter_context
            b24_s = e0(nc.sbuf_tensor("b24_s", [24, 128], F32))
            dq_f = [e0(nc.sbuf_tensor(f"dq_f{j}", [128, ID], F32)) for j in range(2)]
            dq_sc = e0(nc.sbuf_tensor("dq_sc", [128, ID // 32], F32))
            dq_si = e0(nc.sbuf_tensor("dq_si", [128, ID // 32], F32))
            dq_i = e0(nc.sbuf_tensor("dq_i", [128, ID], I32))
            dq_d = e0(nc.sbuf_tensor("dq_d", [128, ID], F32))
            dq_ns = e0(nc.sbuf_tensor("dq_ns", [128, ID // 32], F32))
            dq_b = [e0(nc.sbuf_tensor(f"dq_b{j}", [128, ID], BF16)) for j in range(2)]
            xt = [e0(nc.sbuf_tensor(f"xt{j}", [128, CM], F32)) for j in range(2)]
            x1t = [e0(nc.sbuf_tensor(f"x1t{j}", [128, CM], BF16)) for j in range(2)]
            lst = e0(nc.sbuf_tensor("lst", [128, 16], F32))
            px = e0(nc.psum_tensor("px", [128, 512], F32))

            def sy01(s):
                for ap_, dst in ((constsD, cs), (onesbD, ones_s), (onesfD, onesf_s),
                                 (identfD, idf_s), (cumTD, cum_s), (lowincD, low_s),
                                 (ltstrD, lts_s), (ltincD, lti_s), (headselD, hsel_s),
                                 (iotaD, iota_s), (bvbD, bvb_s), (bprojbD, bpb_s),
                                 (battn24, b24_s)):
                    s.dma_start(out=dst[:], in_=ap_).then_inc(s_c0, 16)
                for i in range(7):
                    j = i % 2
                    if i >= 2:
                        s.wait_ge(s_dqr, 6 * i - 7)
                    if i < 3:
                        s.dma_start(out=dq_f[j][:, 0:CM],
                                    in_=wup_sl[i * 128:(i + 1) * 128, :])\
                            .then_inc(s_dql[j], 16)
                    elif i < 6:
                        s.dma_start(out=dq_f[j][:, 0:CM],
                                    in_=wgate_sl[(i - 3) * 128:(i - 2) * 128, :])\
                            .then_inc(s_dql[j], 16)
                    else:
                        s.dma_start(out=dq_f[j][:], in_=wdown_sl)\
                            .then_inc(s_dql[j], 16)
                for i in range(32):
                    j = i % 2
                    if i >= 2:
                        s.wait_ge(s_app, i - 1)
                    s.dma_start(out=xt[j][:], in_=xbD[i * 128:(i + 1) * 128, :])\
                        .then_inc(s_x0[j], 16)
            blk.sync(sy01)

            def pe01(t):
                t.wait_ge(s_c0, 16 * 13)
                t.transpose(px[:, 0:24], b24_s[:], idf_s[0:24, 0:24])\
                    .then_inc(s_bt, 1)
            blk.tensor(pe01)

            def act01(a):
                a.wait_ge(s_bt, 1)
                a.copy(bT_s[:], px[:, 0:24]).then_inc(s_mb8, 1)
                for k in range(32):
                    kc = 8 * (k % 2)
                    a.wait_ge(s_ssq, 7 * (k + 1))
                    a.activation(lst[:, kc + 6:kc + 7], lst[:, kc + 5:kc + 6],
                                 AF.Ln, bias=cc(cs, "ln_eps")).then_inc(s_rstd, 1)
                    a.wait_ge(s_rstd, 2 * k + 1)
                    a.activation(lst[:, kc + 7:kc + 8], lst[:, kc + 6:kc + 7],
                                 AF.Exp, scale=cc(cs, "neg_half")).then_inc(s_rstd, 1)
            blk.scalar(act01)

            def dve01(v):
                v.wait_ge(s_c0, 16 * 13)
                v.memset(zer_s[:], 0.0)
                for i in range(7):
                    j = i % 2
                    w = CM if i < 6 else ID
                    nwin = w // 32
                    v.wait_ge(s_dql[j], 16 * (i // 2 + 1))
                    if i >= 1:
                        v.wait_ge(s_dqb, i)
                    v.reduce_max(dq_sc[:, 0:nwin],
                                 dq_f[j][:, 0:w].rearrange("p (a b) -> p a b", b=32),
                                 axis=AX.X, apply_absolute_value=True)\
                        .then_inc(s_dqw, 1)
                    v.wait_ge(s_dqw, i + 1)
                    v.tensor_copy(ldump[:, 0:8], cs[:, 0:8])
                    v.tensor_scalar(dq_sc[:, 0:nwin], dq_sc[:, 0:nwin],
                                    cc(cs, "inv7"), cc(cs, "eps12"),
                                    ALU.mult, ALU.max).then_inc(s_dqc, 1)
                    v.wait_ge(s_dqc, i + 1)
                    v.tensor_copy(ldump[:, 0:8], cs[:, 0:8])
                    v.tensor_scalar(dq_ns[:, 0:nwin], dq_sc[:, 0:nwin], -1.0,
                                    None, ALU.mult)
                    v.reciprocal(dq_si[:, 0:nwin], dq_sc[:, 0:nwin])\
                        .then_inc(s_dqi, 1)
                    v.wait_ge(s_dqi, i + 1)
                    v.tensor_copy(ldump[:, 0:8], cs[:, 0:8])
                    si3 = dq_si[:, 0:nwin].rearrange("p (a o) -> p a o", o=1)\
                        .broadcast_to([128, nwin, 32])
                    v.tensor_tensor(dq_f[j][:, 0:w].rearrange("p (a b) -> p a b",
                                                              b=32),
                                    dq_f[j][:, 0:w].rearrange("p (a b) -> p a b",
                                                              b=32),
                                    si3, ALU.mult).then_inc(s_dqr, 1)
                    v.wait_ge(s_dqr, 6 * i + 1)
                    v.tensor_copy(ldump[:, 0:8], cs[:, 0:8])
                    v.tensor_scalar(dq_i[:, 0:w], dq_f[j][:, 0:w], 1.0, None,
                                    ALU.mult).then_inc(s_dqr, 1)
                    v.wait_ge(s_dqr, 6 * i + 2)
                    v.tensor_copy(ldump[:, 0:8], cs[:, 0:8])
                    v.scalar_tensor_tensor(dq_d[:, 0:w], dq_i[:, 0:w], -1.0,
                                           dq_f[j][:, 0:w], ALU.mult, ALU.add)\
                        .then_inc(s_dqr, 1)
                    v.wait_ge(s_dqr, 6 * i + 3)
                    v.tensor_copy(ldump[:, 0:8], cs[:, 0:8])
                    v.scalar_tensor_tensor(dq_f[j][:, 0:w], dq_d[:, 0:w], 0.5,
                                           dq_i[:, 0:w], ALU.is_ge, ALU.add)\
                        .then_inc(s_dqr, 1)
                    v.wait_ge(s_dqr, 6 * i + 4)
                    v.tensor_copy(ldump[:, 0:8], cs[:, 0:8])
                    v.scalar_tensor_tensor(dq_d[:, 0:w], dq_d[:, 0:w], -0.5,
                                           dq_f[j][:, 0:w], ALU.is_ge, ALU.add)\
                        .then_inc(s_dqr, 1)
                    v.wait_ge(s_dqr, 6 * i + 5)
                    if i >= 2:
                        v.wait_ge(s_dqs[j], 16 * (i // 2))
                    v.tensor_copy(ldump[:, 0:8], cs[:, 0:8])
                    sc3 = dq_sc[:, 0:nwin].rearrange("p (a o) -> p a o", o=1)\
                        .broadcast_to([128, nwin, 32])
                    nsc3 = dq_ns[:, 0:nwin].rearrange("p (a o) -> p a o", o=1)\
                        .broadcast_to([128, nwin, 32])
                    v.tensor_tensor(dq_d[:, 0:w].rearrange("p (a b) -> p a b",
                                                           b=32),
                                    dq_d[:, 0:w].rearrange("p (a b) -> p a b",
                                                           b=32),
                                    sc3, ALU.mult).then_inc(s_dqr, 1)
                    v.wait_ge(s_dqr, 6 * i + 6)
                    v.tensor_copy(ldump[:, 0:8], cs[:, 0:8])
                    v.tensor_tensor(dq_b[j][:, 0:w].rearrange("p (a b) -> p a b",
                                                              b=32),
                                    dq_d[:, 0:w].rearrange("p (a b) -> p a b",
                                                           b=32),
                                    nsc3, ALU.add).then_inc(s_dqb, 1)
                for i in range(33):
                    if i < 32:
                        tb = i
                        j = tb % 2
                        tc = 8 * j
                        v.wait_ge(s_x0[j], 16 * (tb // 2 + 1))
                        if tb >= 2:
                            v.wait_ge(s_app, tb - 1)
                        _stats(v, lst, tc, xt[j], 7 * tb, s_ssq)
                    if i >= 1:
                        k = i - 1
                        jk = k % 2
                        kc = 8 * jk
                        v.wait_ge(s_rstd, 2 * (k + 1))
                        if k >= 2:
                            v.wait_ge(s_l1s[jk], 16 * (k // 2))
                        v.tensor_scalar(x1t[jk][:], xt[jk][:], lst[:, kc + 2:kc + 3],
                                        lst[:, kc + 7:kc + 8], ALU.subtract,
                                        ALU.mult).then_inc(s_app, 1)
            blk.vector(dve01)

            def g01(g):
                for i in range(7):
                    j = i % 2
                    g.wait_ge(s_dqb, i + 1)
                    if i < 3:
                        g.dma_start(out=wupdq_sl[i * 128:(i + 1) * 128, :],
                                    in_=dq_b[j][:, 0:CM]).then_inc(s_dqs[j], 16)
                    elif i < 6:
                        g.dma_start(out=wgatedq_sl[(i - 3) * 128:(i - 2) * 128, :],
                                    in_=dq_b[j][:, 0:CM]).then_inc(s_dqs[j], 16)
                    else:
                        g.dma_start(out=wdowndq_sl, in_=dq_b[j][:])\
                            .then_inc(s_dqs[j], 16)
                for k in range(32):
                    jk = k % 2
                    g.wait_ge(s_app, k + 1)
                    g.dma_start(out=x1d[k * 128:(k + 1) * 128, :],
                                in_=x1t[jk][:]).then_inc(s_l1s[jk], 16)
            blk.gpsimd(g01)

        with ExitStack() as scA:
            eA = scA.enter_context
            vaug = eA(nc.sbuf_tensor("vaug", [128, 32 * 1040], BF16))

            with ExitStack() as sc2:
                e2 = sc2.enter_context
                waT = [e2(nc.sbuf_tensor(f"waT{c}", [128, 3 * CM], BF16))
                       for c in range(8)]
                wfl = [e2(nc.sbuf_tensor(f"wfl{j}", [128, CM], F32))
                       for j in range(2)]
                x1T = [e2(nc.sbuf_tensor(f"x1T{c}", [128, T], BF16))
                       for c in range(8)]
                kev = [e2(nc.sbuf_tensor(f"kev{j}", [128, 512], BF16))
                       for j in range(2)]
                sqb = [e2(nc.sbuf_tensor(f"sqb{j}", [128, 512], BF16))
                       for j in range(2)]
                qev = [e2(nc.sbuf_tensor(f"qev{j}", [128, 512], BF16))
                       for j in range(2)]
                pq = [e2(nc.psum_tensor(f"pq{j}", [128, 512], F32))
                      for j in range(2)]
                pxb = e2(nc.psum_tensor("pxb", [128, 512], F32))
                psq = [e2(nc.psum_tensor(f"psq{j}", [2, 512], F32))
                       for j in range(2)]

                TILES = ([("k", p, tc4) for p in range(NPAIR) for tc4 in range(8)]
                         + [("v", tb, hf) for tb in range(32) for hf in range(2)]
                         + [("q", p, tc4) for p in range(NPAIR) for tc4 in range(8)])

                def sy2(s):
                    s.wait_ge(s_dqs[0], 16 * 4)
                    s.wait_ge(s_dqs[1], 16 * 3)
                    s.wait_ge(s_l1s[0], 16 * 16)
                    s.wait_ge(s_l1s[1], 16 * 16)
                    for u in range(24):
                        c, part = u // 3, u % 3
                        j = u % 2
                        if u >= 2:
                            s.wait_ge(s_wc, u - 1)
                        s.dma_start(out=wfl[j][:],
                                    in_=watT[c * 128:(c + 1) * 128,
                                             part * CM:(part + 1) * CM])\
                            .then_inc(s_wl[j], 16)
                    for j in range(2):
                        s.wait_ge(s_l1s[j], 16 * 16)
                    for c in range(8):
                        s.dma_start(out=x1T[c][:],
                                    in_=x1d[:, c * 128:(c + 1) * 128],
                                    transpose=True).then_inc(s_xT, 16)
                blk.sync(sy2)

                def pe2(t):
                    t.wait_ge(s_mb8, 1)
                    t.wait_ge(s_wc, 24)
                    t.wait_ge(s_xT, 16 * 8)
                    pend_kq = []
                    def flush_psq(t):
                        tkq = pend_kq.pop(0)
                        t.wait_ge(s_mb3, tkq + 1)
                        if tkq >= 2:
                            if tkq - 2 < 64:
                                t.wait_ge(s_krd, tkq - 1)
                            else:
                                t.wait_ge(s_qrd, tkq - 65)
                        t.matmul(psq[tkq % 2][0:2, :], hsel_s[:],
                                 sqb[tkq % 2][:], start=True, stop=True)\
                            .then_inc(s_mb5, 1)
                    for k, tile in enumerate(TILES):
                        kind = tile[0]
                        bank = k % 2
                        if k >= 2:
                            t.wait_ge(s_kve[bank], k // 2)
                        if kind in ("k", "q"):
                            _, p, tc4 = tile
                            off = CM if kind == "k" else 0
                            tkq = p * 8 + tc4 + (0 if kind == "k" else 64)
                            mm = None
                            for c in range(8):
                                mm = t.matmul(
                                    pq[bank][:],
                                    waT[c][:, off + 128 * p:off + 128 * p + 128],
                                    x1T[c][:, tc4 * 512:(tc4 + 1) * 512],
                                    start=(c == 0), stop=(c == 7))
                            mm.then_inc(s_kv, 1)
                            pend_kq.append(tkq)
                            while len(pend_kq) > 1:
                                flush_psq(t)
                        else:
                            _, tb, hf = tile
                            mm = None
                            for c in range(8):
                                mm = t.matmul(
                                    pq[bank][:],
                                    x1T[c][:, tb * 128:(tb + 1) * 128],
                                    waT[c][:, 2 * CM + 512 * hf:2 * CM + 512 * hf + 512],
                                    start=(c == 0), stop=(c == 7))
                            mm.then_inc(s_kv, 1)
                            while pend_kq:
                                flush_psq(t)
                    while pend_kq:
                        flush_psq(t)
                    t.wait_ge(s_mb2, 2)
                    t.transpose(pxb[:, 64:192], sc128[:], idf_s[:])\
                        .then_inc(s_mb6, 1)
                    t.wait_ge(s_mb7, 2)
                    t.matmul(pxb[:, 256:257], onesf_s[0:1, 0:128], mrow[0:1, 5:6],
                             start=True, stop=True).then_inc(s_mb8, 1)
                blk.tensor(pe2)

                def act2(a):
                    for k, tile in enumerate(TILES):
                        kind = tile[0]
                        bank = k % 2
                        if kind == "v":
                            continue
                        a.wait_ge(s_kv, k + 1)
                        if kind == "k":
                            _, p, tc4 = tile
                            ki = p * 8 + tc4
                            if ki >= 2:
                                a.wait_ge(s_kst[ki % 2], 16 * (ki // 2))
                                a.wait_ge(s_mb3, ki - 1)
                            a.activation(kev[ki % 2][:], pq[bank][:], AF.Identity,
                                         bias=bT_s[:, 8 + p:9 + p])\
                                .then_inc(s_kve[bank], 1)
                        else:
                            _, p, tc4 = tile
                            qi_ = p * 8 + tc4
                            if qi_ >= 2:
                                a.wait_ge(s_qst[qi_ % 2], 16 * (qi_ // 2))
                                a.wait_ge(s_mb3, 64 + qi_ - 1)
                            a.activation(qev[qi_ % 2][:], pq[bank][:], AF.Identity,
                                         bias=bT_s[:, p:p + 1])\
                                .then_inc(s_kve[bank], 1)
                    a.wait_ge(s_mb4, 1)
                    a.activation(mrow[0:1, 4:5], mrow[0:1, 3:4], AF.Sqrt)\
                        .then_inc(s_mb7, 1)
                    a.wait_ge(s_mb8, 2)
                    a.copy(ebias[:], pxb[:, 256:257]).then_inc(s_mb8, 1)
                blk.scalar(act2)

                def dve2_(v):
                    for u in range(24):
                        c, part = u // 3, u % 3
                        j = u % 2
                        v.wait_ge(s_wl[j], 16 * (u // 2 + 1))
                        v.tensor_copy(waT[c][:, part * CM:(part + 1) * CM],
                                      wfl[j][:]).then_inc(s_wc, 1)
                    for k, tile in enumerate(TILES):
                        kind = tile[0]
                        bank = k % 2
                        if kind == "v":
                            _, tb, hf = tile
                            v.wait_ge(s_kv, k + 1)
                            base = tb * 1040 + hf * 8 * 65
                            dst = vaug[:, base:base + 8 * 65]\
                                .rearrange("p (hh d) -> p hh d", d=65)[:, :, 0:64]
                            v.tensor_copy(
                                dst, pq[bank][:].rearrange("p (hh d) -> p hh d",
                                                           d=64))\
                                .then_inc(s_kve[bank], 1)
                        else:
                            _, p, tc4 = tile
                            ev_ = kev if kind == "k" else qev
                            ti_ = p * 8 + tc4
                            tkq = ti_ + (0 if kind == "k" else 64)
                            v.wait_ge(s_kve[bank], k // 2 + 1)
                            if tkq >= 2:
                                v.wait_ge(s_mb5, tkq - 1)
                            v.tensor_tensor(sqb[tkq % 2][:], ev_[ti_ % 2][:],
                                            ev_[ti_ % 2][:], ALU.mult)\
                                .then_inc(s_mb3, 1)
                            v.wait_ge(s_mb5, tkq + 1)
                            v.reduce_max(mbc[0:2, tkq:tkq + 1],
                                         psq[tkq % 2][0:2, :], axis=AX.X)\
                                .then_inc(s_krd if kind == "k" else s_qrd, 1)
                    for tb in range(32):
                        dst = vaug[:, tb * 1040:(tb + 1) * 1040]\
                            .rearrange("p (hh d) -> p hh d", d=65)[:, :, 64:65]
                        v.memset(dst, 1.0)
                    v.wait_ge(s_krd, 64)
                    v.wait_ge(s_qrd, 64)
                    v.memset(sc128[:], 0.0)
                    v.tensor_copy(ldump[:, 0:8], cs[:, 0:8])
                    v.reduce_max(mbc[0:2, 128:136],
                                 mbc[0:2, 0:64].rearrange("p (a b) -> p a b",
                                                          b=8),
                                 axis=AX.X).then_inc(s_mb1, 1)
                    v.reduce_max(mbc[0:2, 136:144],
                                 mbc[0:2, 64:128].rearrange("p (a b) -> p a b",
                                                            b=8),
                                 axis=AX.X).then_inc(s_mb1, 1)
                    v.wait_ge(s_mb1, 2)
                    v.tensor_copy(ldump[:, 0:8], cs[:, 0:8])
                    v.tensor_tensor(sc128[0:2, 8:16], mbc[0:2, 128:136],
                                    mbc[0:2, 136:144], ALU.mult)\
                        .then_inc(s_mb2, 1)
                    v.wait_ge(s_mb2, 1)
                    v.tensor_copy(ldump[:, 0:8], cs[:, 0:8])
                    v.reduce_max(sc128[0:2, 0:1], sc128[0:2, 8:16], axis=AX.X)\
                        .then_inc(s_mb2, 1)
                    v.wait_ge(s_mb6, 1)
                    v.tensor_copy(ldump[:, 0:8], cs[:, 0:8])
                    v.reduce_max(mrow[0:1, 3:4], pxb[0:1, 64:66], axis=AX.X)\
                        .then_inc(s_mb4, 1)
                    v.wait_ge(s_mb7, 1)
                    v.tensor_copy(ldump[:, 0:8], cs[:, 0:8])
                    v.tensor_scalar(mrow[0:1, 5:6], mrow[0:1, 4:5],
                                    cc(cs, "att_scale")[0:1, :],
                                    cc(cs, "neg_one")[0:1, :],
                                    ALU.mult, ALU.mult).then_inc(s_mb7, 1)
                blk.vector(dve2_)

                def g2_(g):
                    for k, tile in enumerate(TILES):
                        kind = tile[0]
                        if kind == "k":
                            _, p, tc4 = tile
                            ki = p * 8 + tc4
                            g.wait_ge(s_mb3, ki + 1)
                            g.dma_start(out=kTd[p * 128:(p + 1) * 128,
                                                tc4 * 512:(tc4 + 1) * 512],
                                        in_=kev[ki % 2][:]).then_inc(s_kst[ki % 2], 16)
                        elif kind == "q":
                            _, p, tc4 = tile
                            qi_ = p * 8 + tc4
                            g.wait_ge(s_mb3, 64 + qi_ + 1)
                            g.dma_start(out=qTd[p * 128:(p + 1) * 128,
                                                tc4 * 512:(tc4 + 1) * 512],
                                        in_=qev[qi_ % 2][:]).then_inc(s_qst[qi_ % 2], 16)
                    if dbg:
                        for j in range(2):
                            g.wait_ge(s_kst[j], 16 * 32)
                        g.dma_start(out=dbg_kT, in_=kTd).then_inc(s_fxd, 16)
                        g.wait_ge(s_mb7, 2)
                        g.dma_start(out=dbg_m, in_=mrow[0:1, 4:5]).then_inc(s_fxd, 16)
                blk.gpsimd(g2_)

            with ExitStack() as sc3:
                e3 = sc3.enter_context
                q0s = e3(nc.sbuf_tensor("q0s", [64, T], BF16))
                k0s = e3(nc.sbuf_tensor("k0s", [64, T], BF16))
                sSc = [e3(nc.sbuf_tensor(f"sSc{j}", [128, 512], BF16))
                       for j in range(2)]
                ffsb = e3(nc.sbuf_tensor("ffsb", [128, T], F32))
                carv = e3(nc.sbuf_tensor("carv", [1, T], F32))
                carvb = e3(nc.sbuf_tensor("carvb", [1, T], BF16))
                ffw = e3(nc.sbuf_tensor("ffw", [128, 128], F32))
                bis = e3(nc.sbuf_tensor("bis", [128, 12], F32))
                eind = e3(nc.sbuf_tensor("eind", [128, T], BF16))
                ee0 = e3(nc.sbuf_tensor("ee0", [128, T], BF16))
                eout = e3(nc.sbuf_tensor("eout", [128, T], BF16))
                dgt = e3(nc.sbuf_tensor("dgt", [128, 128], BF16))
                ps0 = [e3(nc.psum_tensor(f"ps0_{j}", [128, 512], F32))
                       for j in range(2)]
                pff = [e3(nc.psum_tensor(f"pff{j}", [128, 512], F32))
                       for j in range(2)]
                pcs = [e3(nc.psum_tensor(f"pcs{j}", [1, 512], F32))
                       for j in range(2)]

                SCH = []
                for I in range(32):
                    w = 128 * (I + 1)
                    for jc in range((w + 511) // 512):
                        SCH.append((I, jc, min(512, w - 512 * jc)))
                E_before = {}
                for idx, (I, jc, nj) in enumerate(SCH):
                    if I not in E_before:
                        E_before[I] = idx
                done_at = {}
                for idx, (I, jc, nj) in enumerate(SCH):
                    if I in mine:
                        done_at[I] = idx
                ev = []
                nS0 = [0, 0]; nSR = [0, 0]; nSM = 0
                nF1 = 0; nFE = 0
                nTH = 0; nEST = 0; nEE = 0; nEM = 0; nBIS = 0
                seen_first_evac = set()
                for idx, (I, jc, nj) in enumerate(SCH):
                    slot = idx % 2
                    d = {"slot": slot, "I": I, "jc": jc, "nj": nj}
                    nS0[slot] += 1; d["s0w"] = nS0[slot]
                    d["m0"] = (jc == 0)
                    d["md"] = (jc == I // 4)
                    d["mask"] = d["m0"] or d["md"]
                    d["sm_before"] = nSM
                    nSM += int(d["m0"]) + int(d["md"])
                    d["sm"] = nSM
                    nSR[slot] += 1; d["s0r"] = nSR[slot]
                    if I in mine:
                        d["fslot"] = nF1 % 2
                        nF1 += 1; d["f1"] = nF1
                        nFE += 1; d["fe"] = nFE
                        d["first_evac"] = I not in seen_first_evac
                        seen_first_evac.add(I)
                    d["th_before"] = nTH
                    d["est_before"] = nEST
                    d["ee_before"] = nEE
                    d["em_before"] = nEM
                    d["bis_before"] = nBIS
                    d["done"] = (I in mine) and (idx == done_at[I])
                    if d["done"]:
                        nTH += 1
                        nEE += 1
                        nEM += 4
                        if I > 5:
                            nBIS += 1
                        W = 128 * (I + 1)
                        Wpad = JCs[my_pos[I] // 4] * 128
                        d["n_st"] = 1 + (Wpad - W + 511) // 512
                        nEST += d["n_st"]
                    d["est_after"] = nEST
                    d["th_after"] = nTH
                    d["ee_after"] = nEE
                    d["em_after"] = nEM
                    d["bis_after"] = nBIS
                    ev.append(d)
                NSCH = len(SCH)
                EST_TOTAL = nEST
                FE_TOTAL = nFE
                F1_TOTAL = nF1
                SR_TOT = (nSR[0], nSR[1])

                def sy3(s):
                    for j in range(2):
                        s.wait_ge(s_qst[j], 16 * 32)
                        s.wait_ge(s_kst[j], 16 * 32)
                    s.dma_start(out=q0s[:], in_=qTd[0:64, :]).then_inc(s_q0l, 16)
                    s.dma_start(out=k0s[:], in_=kTd[0:64, :]).then_inc(s_k0l, 16)
                blk.sync(sy3)

                def pe3(t):
                    t.wait_ge(s_kve[0], 96)
                    t.wait_ge(s_kve[1], 96)
                    t.wait_ge(s_krd, 64)
                    t.wait_ge(s_qrd, 64)
                    t.wait_ge(s_mb8, 3)
                    t.wait_ge(s_q0l, 16)
                    t.wait_ge(s_k0l, 16)
                    t.wait_ge(s_cv, 1)
                    for ti in range(NSCH + 1):
                        if ti < NSCH:
                            idx = ti
                            I, jc, nj = SCH[idx]
                            d = ev[idx]
                            slot = d["slot"]
                            if idx >= 2:
                                t.wait_ge(s_s0r[slot], ev[idx - 2]["s0r"])
                            t.matmul(ps0[slot][:, 0:nj],
                                     q0s[:, I * 128:(I + 1) * 128],
                                     k0s[:, jc * 512:jc * 512 + nj],
                                     start=True, stop=True)\
                                .then_inc(s_s0w[slot], 1)
                        if ti >= 1:
                            idx = ti - 1
                            I, jc, nj = SCH[idx]
                            d = ev[idx]
                            slot = d["slot"]
                            t.wait_ge(s_s0r[slot], d["s0r"])
                            if d["mask"]:
                                t.wait_ge(s_sm, d["sm"])
                            if I in mine:
                                if d["f1"] >= 3:
                                    t.wait_ge(s_ffe, d["f1"] - 2)
                                t.wait_ge(s_cv, 1 + 2 * E_before[I])
                                t.matmul(pff[d["fslot"]][:, 0:nj], cum_s[:],
                                         sSc[slot][:, 0:nj],
                                         start=True, stop=False)
                                t.matmul(pff[d["fslot"]][:, 0:nj],
                                         ones_s[0:1, 0:128],
                                         carvb[0:1, jc * 512:jc * 512 + nj],
                                         start=False, stop=True)\
                                    .then_inc(s_ff1, 1)
                            if idx >= 2:
                                t.wait_ge(s_cv, 1 + 2 * (idx - 1))
                            t.matmul(pcs[idx % 2][0:1, 0:nj], ones_s[:, 0:1],
                                     sSc[slot][:, 0:nj],
                                     start=True, stop=True).then_inc(s_cs, 1)
                blk.tensor(pe3)

                def act3(a):
                    for idx, (I, jc, nj) in enumerate(SCH):
                        d = ev[idx]
                        slot = d["slot"]
                        a.wait_ge(s_s0w[slot], d["s0w"])
                        if idx >= 2:
                            a.wait_ge(s_cs, idx - 1)
                        a.activation(sSc[slot][:, 0:nj], ps0[slot][:, 0:nj],
                                     AF.Relu, scale=cc(cs, "att_scale"))\
                            .then_inc(s_s0r[slot], 1)
                        if d["done"]:
                            W = 128 * (I + 1)
                            a.wait_ge(s_th, d["th_after"])
                            if d["ee_before"] >= 1:
                                a.wait_ge(s_eml, d["em_before"])
                            a.activation(ee0[:, 0:W], ffsb[:, 0:W], AF.Exp,
                                         scale=cc(cs, "neg_one"))\
                                .then_inc(s_ee0, 1)
                blk.scalar(act3)

                def dve3_(v):
                    v.memset(carv[:], 0.0)
                    v.memset(carvb[:], 0.0).then_inc(s_cv, 1)
                    for idx, (I, jc, nj) in enumerate(SCH):
                        d = ev[idx]
                        slot = d["slot"]
                        if d["mask"]:
                            v.wait_ge(s_s0r[slot], d["s0r"])
                            n0 = d["sm_before"]
                            if d["m0"]:
                                v.memset(sSc[slot][:, 0:1], 0.0)\
                                    .then_inc(s_sm, 1)
                                n0 += 1
                            if d["md"]:
                                if d["m0"]:
                                    v.wait_ge(s_sm, n0)
                                off = I * 128 - jc * 512
                                v.tensor_tensor(sSc[slot][:, off:off + 128],
                                                sSc[slot][:, off:off + 128],
                                                lts_s[:], ALU.mult)\
                                    .then_inc(s_sm, 1)
                        if I in mine:
                            v.wait_ge(s_ff1, d["f1"])
                            if d.get("first_evac"):
                                v.wait_ge(s_ee0, d["ee_before"])
                                v.wait_ge(s_ein, d["bis_before"])
                            v.tensor_copy(ffsb[:, jc * 512:jc * 512 + nj],
                                          pff[d["fslot"]][:, 0:nj])\
                                .then_inc(s_ffe, 1)
                        v.wait_ge(s_cs, idx + 1)
                        v.tensor_copy(ldump[:, 0:8], cs[:, 0:8])
                        v.tensor_tensor(carv[0:1, jc * 512:jc * 512 + nj],
                                        carv[0:1, jc * 512:jc * 512 + nj],
                                        pcs[idx % 2][0:1, 0:nj], ALU.add)\
                            .then_inc(s_cv, 1)
                        v.wait_ge(s_cv, 2 * idx + 2)
                        v.tensor_copy(carvb[0:1, jc * 512:jc * 512 + nj],
                                      carv[0:1, jc * 512:jc * 512 + nj])\
                            .then_inc(s_cv, 1)
                        if d["done"]:
                            W = 128 * (I + 1)
                            nw = W // 32
                            m = my_pos[I]
                            v.wait_ge(s_ffe, d["fe"])
                            v.tensor_copy(ldump[:, 0:8], cs[:, 0:8])
                            wm = 3 * d["th_before"]
                            v.tensor_scalar(ee0[:, 0:W], ffsb[:, 0:W],
                                            cc(cs, "neg_one"), None, ALU.mult)\
                                .then_inc(s_wmn, 1)
                            v.wait_ge(s_wmn, wm + 1)
                            v.tensor_copy(ldump[:, 0:8], cs[:, 0:8])
                            v.reduce_max(ffw[:, 0:nw],
                                         ee0[:, 0:W].rearrange(
                                             "p (a b) -> p a b", b=32),
                                         axis=AX.X).then_inc(s_wmn, 1)
                            v.wait_ge(s_wmn, wm + 2)
                            v.tensor_copy(ldump[:, 0:8], cs[:, 0:8])
                            v.tensor_scalar(ffw[:, 0:nw], ffw[:, 0:nw],
                                            cc(cs, "neg_one"), None, ALU.mult)\
                                .then_inc(s_wmn, 1)
                            v.wait_ge(s_wmn, wm + 3)
                            if I > 5:
                                bb = 50 * d["bis_before"]
                                v.memset(bis[:, 0:1], 0.0)
                                v.tensor_copy(bis[:, 1:2], cc(cs, "c1024"))\
                                    .then_inc(s_bis, 1)
                                v.wait_ge(s_bis, bb + 1)
                                for it in range(8):
                                    ib = bb + 1 + 6 * it
                                    if it >= 1:
                                        v.wait_ge(s_bis, ib)
                                    v.tensor_tensor(bis[:, 2:3], bis[:, 0:1],
                                                    bis[:, 1:2], ALU.add)\
                                        .then_inc(s_bis, 1)
                                    v.wait_ge(s_bis, ib + 1)
                                    v.tensor_scalar(bis[:, 2:3], bis[:, 2:3],
                                                    cc(cs, "half"), None,
                                                    ALU.mult).then_inc(s_bis, 1)
                                    v.wait_ge(s_bis, ib + 2)
                                    v.tensor_tensor(bis[:, 3:4], bis[:, 2:3],
                                                    bis[:, 0:1], ALU.subtract)
                                    v.tensor_tensor(bis[:, 4:5], bis[:, 2:3],
                                                    bis[:, 1:2], ALU.subtract)
                                    v.tensor_scalar(ldump[:, 8:8 + nw],
                                                    ffw[:, 0:nw], bis[:, 2:3],
                                                    0.0, ALU.is_lt, ALU.add,
                                                    accum_out=bis[:, 5:6])\
                                        .then_inc(s_bis, 1)
                                    v.wait_ge(s_bis, ib + 3)
                                    v.tensor_copy(ldump[:, 0:8], cs[:, 0:8])
                                    v.tensor_scalar(bis[:, 6:7], bis[:, 5:6],
                                                    cc(cs, "win_budget"), None,
                                                    ALU.is_le).then_inc(s_bis, 1)
                                    v.wait_ge(s_bis, ib + 4)
                                    v.tensor_scalar(bis[:, 7:8], bis[:, 6:7],
                                                    cc(cs, "neg_one"), 1.0,
                                                    ALU.mult, ALU.add)\
                                        .then_inc(s_bis, 1)
                                    v.wait_ge(s_bis, ib + 5)
                                    v.scalar_tensor_tensor(bis[:, 0:1],
                                                           bis[:, 3:4],
                                                           bis[:, 6:7],
                                                           bis[:, 0:1],
                                                           ALU.mult, ALU.add)
                                    v.scalar_tensor_tensor(bis[:, 1:2],
                                                           bis[:, 4:5],
                                                           bis[:, 7:8],
                                                           bis[:, 1:2],
                                                           ALU.mult, ALU.add)\
                                        .then_inc(s_bis, 1)
                                v.wait_ge(s_bis, bb + 49)
                                v.tensor_scalar(bis[:, 8:9], iota_s[:, I:I + 1],
                                                cc(cs, "c817"), None, ALU.is_le)\
                                    .then_inc(s_bis, 1)
                                v.wait_ge(s_bis, bb + 50)
                                v.scalar_tensor_tensor(bis[:, 9:10], bis[:, 8:9],
                                                       cc(cs, "big"),
                                                       bis[:, 0:1],
                                                       ALU.mult, ALU.add)\
                                    .then_inc(s_th, 1)
                                v.wait_ge(s_th, d["th_before"] + 1)
                                v.tensor_copy(ldump[:, 0:8], cs[:, 0:8])
                                v.tensor_scalar(eind[:, 0:W], ffsb[:, 0:W],
                                                bis[:, 9:10], None, ALU.is_lt)\
                                    .then_inc(s_ein, 1)
                            else:
                                v.tensor_copy(ldump[:, 0:8], cs[:, 0:8])\
                                    .then_inc(s_th, 1)
                            v.wait_ge(s_ee0, d["ee_before"] + 1)
                            v.wait_ge(s_est, 16 * d["est_before"])
                            if I > 5:
                                v.wait_ge(s_ein, d["bis_before"] + 1)
                                v.tensor_tensor(eout[:, 0:W], ee0[:, 0:W],
                                                eind[:, 0:W], ALU.mult)\
                                    .then_inc(s_eml, 1)
                            else:
                                v.tensor_copy(eout[:, 0:W], ee0[:, 0:W])\
                                    .then_inc(s_eml, 1)
                            v.wait_ge(s_eml, d["em_before"] + 1)
                            v.tensor_copy(ldump[:, 0:8], cs[:, 0:8])
                            v.tensor_tensor(dgt[:],
                                            eout[:, I * 128:(I + 1) * 128],
                                            lti_s[:], ALU.mult)\
                                .then_inc(s_eml, 1)
                            v.wait_ge(s_eml, d["em_before"] + 2)
                            v.tensor_copy(ldump[:, 0:8], cs[:, 0:8])
                            v.tensor_copy(eout[:, I * 128:(I + 1) * 128],
                                          dgt[:]).then_inc(s_eml, 2)
                blk.vector(dve3_)

                def g3(g):
                    for idx, (I, jc, nj) in enumerate(SCH):
                        d = ev[idx]
                        if d["done"]:
                            W = 128 * (I + 1)
                            m = my_pos[I]
                            Wpad = JCs[m // 4] * 128
                            g.wait_ge(s_eml, d["em_before"] + 4)
                            g.dma_start(out=Ed[m * 128:(m + 1) * 128, 0:W],
                                        in_=eout[:, 0:W]).then_inc(s_est, 16)
                            o = W
                            while o < Wpad:
                                wz = min(512, Wpad - o)
                                g.dma_start(out=Ed[m * 128:(m + 1) * 128,
                                                   o:o + wz],
                                            in_=zer_s[:, 0:wz])\
                                    .then_inc(s_est, 16)
                                o += wz
                blk.gpsimd(g3)

            with ExitStack() as sc4:
                e4 = sc4.enter_context
                kTb = [e4(nc.sbuf_tensor(f"kTb{p}", [128, T], BF16))
                       for p in range(NPAIR)]
                ebuf = e4(nc.sbuf_tensor("ebuf", [128, 32 * 512], BF16))
                qbuf = [e4(nc.sbuf_tensor(f"qbuf{p}", [128, 512], BF16))
                        for p in range(NPAIR)]
                p2 = [e4(nc.sbuf_tensor(f"p2_{j}", [128, 1024], BF16))
                      for j in range(2)]
                pmul = [e4(nc.sbuf_tensor(f"pmul{j}", [128, 1024], BF16))
                        for j in range(2)]
                ysb = [e4(nc.sbuf_tensor(f"ysb{j}", [128, 520], F32))
                       for j in range(2)]
                rz = [e4(nc.sbuf_tensor(f"rz{j}", [128, 8], F32)) for j in range(2)]
                ybf = [e4(nc.sbuf_tensor(f"ybf{s}", [128, CM], BF16))
                       for s in range(4)]
                pqk = [e4(nc.psum_tensor(f"pqk{j}", [128, 1024], F32))
                       for j in range(2)]
                yb = [e4(nc.psum_tensor(f"yb{j}", [128, 512], F32))
                      for j in range(4)]
                s_pvt = [sem(f"s_pvt{j}") for j in range(2)]

                QKT = []
                for ic in range(4):
                    for p in range(NPAIR):
                        for jc in range(JCs[ic]):
                            QKT.append((ic, p, jc))
                NQK = len(QKT)
                EL_TOT, QL_TOT = [], []
                elc = qlc = 0
                for ic in range(4):
                    elc += JCs[ic]
                    qlc += NPAIR * 4
                    EL_TOT.append(elc)
                    QL_TOT.append(qlc)
                PM_CNT = []
                PM_TOT = [[0] * 4, [0] * 4]
                pmc = [0, 0]
                GATE = []
                pv_c = [0, 0]
                pvt_c = [0, 0]
                for i, (ic, p, jc) in enumerate(QKT):
                    pmc[i % 2] += 1
                    PM_CNT.append(pmc[i % 2])
                    PM_TOT[0][ic] = pmc[0]
                    PM_TOT[1][ic] = pmc[1]
                    if jc == JCs[ic] - 1:
                        pv_c[p % 2] += 1
                        GATE.append(("pv", p % 2, pv_c[p % 2]))
                    else:
                        pvt_c[i % 2] += 1
                        GATE.append(("pvt", i % 2, pvt_c[i % 2]))

                def sy4(s):
                    s.wait_ge(s_est, 16 * EST_TOTAL)
                    s.wait_ge(s_cv, 1 + 2 * NSCH)
                    s.wait_ge(s_cs, NSCH)
                    s.wait_ge(s_s0r[0], SR_TOT[0])
                    s.wait_ge(s_s0r[1], SR_TOT[1])
                    for j in range(2):
                        s.wait_ge(s_kst[j], 16 * 32)
                    for p in range(NPAIR):
                        s.dma_start(out=kTb[p][:],
                                    in_=kTd[p * 128:(p + 1) * 128, :])\
                            .then_inc(s_ktl, 16)
                    for ic in range(4):
                        if ic == 0:
                            s.wait_ge(s_est, 16 * EST_TOTAL)
                        else:
                            s.wait_ge(s_pm[0], PM_TOT[0][ic - 1])
                            s.wait_ge(s_pm[1], PM_TOT[1][ic - 1])
                        for jc in range(JCs[ic]):
                            s.dma_start(out=ebuf[:, jc * 512:(jc + 1) * 512],
                                        in_=Ed[ic * 512:(ic + 1) * 512,
                                               jc * 128:(jc + 1) * 128],
                                        transpose=True).then_inc(s_el, 16)
                        for p in range(NPAIR):
                            for z, b_ in enumerate(icq[ic]):
                                s.dma_start(
                                    out=qbuf[p][:, z * 128:(z + 1) * 128],
                                    in_=qTd[p * 128:(p + 1) * 128,
                                            b_ * 128:(b_ + 1) * 128])\
                                    .then_inc(s_ql, 16)
                blk.sync(sy4)

                def pe4(t):
                    t.wait_ge(s_ffe, FE_TOTAL)
                    t.wait_ge(s_cv, 1 + 2 * NSCH)
                    t.wait_ge(s_s0r[0], SR_TOT[0])
                    t.wait_ge(s_s0r[1], SR_TOT[1])
                    t.wait_ge(s_ktl, 16 * NPAIR)
                    for ti in range(NQK + 1):
                        if ti < NQK:
                            i = ti
                            ic, p, jc = QKT[i]
                            bank = i % 2
                            if jc == 0:
                                t.wait_ge(s_el, 16 * EL_TOT[ic])
                                t.wait_ge(s_ql, 16 * QL_TOT[ic])
                            if i >= 2:
                                t.wait_ge(s_ex[bank], i // 2)
                            t.matmul(pqk[bank][:, 0:512],
                                     kTb[p][0:64, jc * 128:(jc + 1) * 128],
                                     qbuf[p][0:64, :], start=True, stop=True,
                                     tile_position=(0, 0))
                            t.matmul(pqk[bank][:, 512:1024],
                                     kTb[p][64:128, jc * 128:(jc + 1) * 128],
                                     qbuf[p][64:128, :], start=True, stop=True,
                                     tile_position=(64, 0)).then_inc(s_qkd, 1)
                        if ti >= 1:
                            i = ti - 1
                            ic, p, jc = QKT[i]
                            bank = i % 2
                            par = p % 2
                            gp = ic * NPAIR + p
                            t.wait_ge(s_pm[bank], PM_CNT[i])
                            if jc == 0 and gp >= 2:
                                t.wait_ge(s_ye[par], gp // 2)
                            last_tile = (jc == JCs[ic] - 1)
                            for hh in range(2):
                                bk = par * 2 + hh
                                mm = None
                                for sub in range(4):
                                    mm = t.matmul(
                                        yb[bk][:, 65 * sub:65 * sub + 65],
                                        pmul[bank][:, hh * 512 + sub * 128:
                                                   hh * 512 + (sub + 1) * 128],
                                        vaug[:, jc * 1040 + (2 * p + hh) * 65:
                                             jc * 1040 + (2 * p + hh) * 65 + 65],
                                        start=(jc == 0 and sub == 0),
                                        stop=(last_tile and sub == 3))
                                if hh == 1:
                                    if last_tile:
                                        mm.then_inc(s_pv[par], 1)
                                    else:
                                        mm.then_inc(s_pvt[bank], 1)
                blk.tensor(pe4)

                def act4(a):
                    a.wait_ge(s_mb8, 3)
                    for i, (ic, p, jc) in enumerate(QKT):
                        bank = i % 2
                        par = p % 2
                        gp = ic * NPAIR + p
                        a.wait_ge(s_qkd, i + 1)
                        if i >= 2:
                            a.wait_ge(s_pm[bank], PM_CNT[i] - 1)
                        a.activation(p2[bank][:], pqk[bank][:], AF.Exp,
                                     bias=ebias[:, 0:1],
                                     scale=cc(cs, "att_scale"))\
                            .then_inc(s_ex[bank], 1)
                        if jc == JCs[ic] - 1:
                            a.wait_ge(s_pv[par], gp // 2 + 1)
                            if gp >= 2:
                                a.wait_ge(s_yn[par], 2 * (gp // 2))
                            a.copy(ysb[par][:, 0:260], yb[par * 2][:, 0:260])
                            a.copy(ysb[par][:, 260:520],
                                   yb[par * 2 + 1][:, 0:260]).then_inc(s_ye[par], 1)
                blk.scalar(act4)

                def dve4(v):
                    for i, (ic, p, jc) in enumerate(QKT):
                        bank = i % 2
                        par = p % 2
                        gp = ic * NPAIR + p
                        v.wait_ge(s_ex[bank], i // 2 + 1)
                        if i >= 2:
                            gk, gj, gc = GATE[i - 2]
                            v.wait_ge(s_pv[gj] if gk == "pv" else s_pvt[gj], gc)
                        v.tensor_tensor(pmul[bank][:, 0:512],
                                        p2[bank][:, 0:512],
                                        ebuf[:, jc * 512:(jc + 1) * 512],
                                        ALU.mult)
                        v.tensor_tensor(pmul[bank][:, 512:1024],
                                        p2[bank][:, 512:1024],
                                        ebuf[:, jc * 512:(jc + 1) * 512],
                                        ALU.mult).then_inc(s_pm[bank], 1)
                        if jc == JCs[ic] - 1:
                            v.wait_ge(s_ye[par], gp // 2 + 1)
                            if p == 0 and ic >= 1:
                                for sbi in range(4):
                                    v.wait_ge(s_ys[sbi], 16 * ic)
                            zv = ysb[par][:, 0:520].rearrange(
                                "p (hh d) -> p hh d", d=65)[:, :, 64:65]
                            v.reciprocal(rz[par][:, 0:8].rearrange(
                                "p (a o) -> p a o", o=1), zv)
                            v.tensor_copy(ldump[:, 0:8], cs[:, 0:8])
                            v.tensor_copy(ldump[:, 8:16], cs[:, 0:8])\
                                .then_inc(s_yn[par], 1)
                            v.wait_ge(s_yn[par], 2 * (gp // 2) + 1)
                            for hh in range(2):
                                head = 2 * p + hh
                                for sbi in range(4):
                                    ins_ = v.scalar_tensor_tensor(
                                        ybf[sbi][:, head * 64:(head + 1) * 64],
                                        ysb[par][:, hh * 260 + sbi * 65:
                                                 hh * 260 + sbi * 65 + 64],
                                        rz[par][:, hh * 4 + sbi:hh * 4 + sbi + 1],
                                        bvb_s[:, head * 64:(head + 1) * 64],
                                        ALU.mult, ALU.add)
                            ins_.then_inc(s_yn[par], 1)
                blk.vector(dve4)

                def g4(g):
                    for ic in range(4):
                        g.wait_ge(s_yn[0], 8 * (ic + 1))
                        g.wait_ge(s_yn[1], 8 * (ic + 1))
                        for sbi in range(4):
                            g.dma_start(
                                out=ydD[(ic * 4 + sbi) * 128:
                                        (ic * 4 + sbi + 1) * 128, :],
                                in_=ybf[sbi][:]).then_inc(s_ys[sbi], 16)
                    if dbg:
                        for icq_ in range(4):
                            wq = JCs[icq_] * 128
                            g.dma_start(
                                out=dbg_E[icq_ * 512:(icq_ + 1) * 512, 0:wq],
                                in_=Ed[icq_ * 512:(icq_ + 1) * 512, 0:wq])\
                                .then_inc(s_fxd, 16)
                        for sbi in range(4):
                            g.wait_ge(s_ys[sbi], 16 * 4)
                        g.dma_start(out=dbg_y, in_=ydD).then_inc(s_fxd, 16)
                blk.gpsimd(g4)

        with ExitStack() as sc5:
            e5 = sc5.enter_context
            wpf = [e5(nc.sbuf_tensor(f"wpf{j}", [128, CM], F32)) for j in range(2)]
            wpT = [e5(nc.sbuf_tensor(f"wpT{c}", [128, CM], BF16)) for c in range(8)]
            yT = [e5(nc.sbuf_tensor(f"yT{c}", [128, TOK], BF16)) for c in range(8)]
            fsb = [e5(nc.sbuf_tensor(f"fsb{j}", [128, CM], F32)) for j in range(2)]
            xot = [e5(nc.sbuf_tensor(f"xot{j}", [128, CM], F32)) for j in range(2)]
            x2t = [e5(nc.sbuf_tensor(f"x2t{j}", [128, CM], F32)) for j in range(2)]
            x3t = [e5(nc.sbuf_tensor(f"x3t{j}", [128, CM], BF16)) for j in range(2)]
            lst2 = e5(nc.sbuf_tensor("lst2", [128, 16], F32))
            lst3 = e5(nc.sbuf_tensor("lst3", [128, 16], F32))
            mxc = e5(nc.sbuf_tensor("mxc", [128, 16], F32))
            pp = [e5(nc.psum_tensor(f"pp{j}", [128, CM], F32)) for j in range(2)]
            s_wpf = [sem(f"s_wpf{j}") for j in range(2)]
            s_wpc = sem("s_wpc")

            def sy5(s):
                for q in range(4):
                    s.wait_ge(s_ys[q], 16 * 4)
                for c in range(8):
                    j = c % 2
                    if c >= 2:
                        s.wait_ge(s_wpc, c - 1)
                    s.dma_start(out=wpf[j][:], in_=wpjT[c * 128:(c + 1) * 128, :])\
                        .then_inc(s_wpf[j], 16)
                for tb in range(16):
                    if tb >= 2:
                        s.wait_ge(s_ytl[tb % 2], 16 * 8 * (tb // 2))
                    for c in range(8):
                        s.dma_start(out=yT[c][:, tb * 128:(tb + 1) * 128],
                                    in_=ydD[tb * 128:(tb + 1) * 128,
                                            c * 128:(c + 1) * 128],
                                    transpose=True).then_inc(s_ytl[tb % 2], 16)
                for tb in range(16):
                    j = tb % 2
                    if tb >= 2:
                        s.wait_ge(s_re2, 2 * tb - 2)
                    gb = my_blocks[tb]
                    s.dma_start(out=xot[j][:], in_=xbD[gb * 128:(gb + 1) * 128, :])\
                        .then_inc(s_xol[j], 16)
            blk.sync(sy5)

            def pe5(t):
                t.wait_ge(s_wpc, 8)
                for tb in range(16):
                    t.wait_ge(s_ytl[tb % 2], 16 * 8 * (tb // 2 + 1))
                    if tb >= 2:
                        t.wait_ge(s_bia, tb - 1)
                    mm = None
                    for half in range(2):
                        for c in range(8):
                            mm = t.matmul(pp[tb % 2][:, half * 512:(half + 1) * 512],
                                          yT[c][:, tb * 128:(tb + 1) * 128],
                                          wpT[c][:, half * 512:(half + 1) * 512],
                                          start=(c == 0), stop=(c == 7))
                    mm.then_inc(s_pj, 1)
            blk.tensor(pe5)

            def act5(a):
                for i in range(18):
                    if 1 <= i <= 16:
                        k = i - 1
                        kc = 8 * (k % 2)
                        a.wait_ge(s_sq2, 7 * (k + 1))
                        a.activation(lst2[:, kc + 6:kc + 7], lst2[:, kc + 5:kc + 6],
                                     AF.Ln, bias=cc(cs, "ln_eps")).then_inc(s_rs2, 1)
                        a.wait_ge(s_rs2, 2 * k + 1)
                        a.activation(lst2[:, kc + 7:kc + 8], lst2[:, kc + 6:kc + 7],
                                     AF.Exp, scale=cc(cs, "neg_half"))\
                            .then_inc(s_rs2, 1)
                    if i >= 2:
                        k = i - 2
                        kc = 8 * (k % 2)
                        a.wait_ge(s_sq3, 7 * (k + 1))
                        a.activation(lst3[:, kc + 6:kc + 7], lst3[:, kc + 5:kc + 6],
                                     AF.Ln, bias=cc(cs, "ln_eps")).then_inc(s_rs3, 1)
                        a.wait_ge(s_rs3, 2 * k + 1)
                        a.activation(lst3[:, kc + 7:kc + 8], lst3[:, kc + 6:kc + 7],
                                     AF.Exp, scale=cc(cs, "neg_half"))\
                            .then_inc(s_rs3, 1)
            blk.scalar(act5)

            def dve5(v):
                for c in range(8):
                    j = c % 2
                    v.wait_ge(s_wpf[j], 16 * (c // 2 + 1))
                    v.tensor_copy(wpT[c][:], wpf[j][:]).then_inc(s_wpc, 1)
                for i in range(18):
                    if i < 16:
                        tb = i
                        j = tb % 2
                        tc = 8 * j
                        v.wait_ge(s_pj, tb + 1)
                        if tb >= 2:
                            v.wait_ge(s_re2, 2 * tb - 3)
                            v.wait_ge(s_rs2, 2 * (tb - 1))
                        v.scalar_tensor_tensor(fsb[j][:], pp[j][:], 1.0,
                                               bpb_s[:], ALU.bypass, ALU.add)\
                            .then_inc(s_bia, 1)
                        v.wait_ge(s_bia, tb + 1)
                        _stats(v, lst2, tc, fsb[j], 7 * tb, s_sq2)
                    if 1 <= i <= 16:
                        k = i - 1
                        jk = k % 2
                        kc = 8 * jk
                        v.wait_ge(s_rs2, 2 * (k + 1))
                        v.wait_ge(s_xol[jk], 16 * (k // 2 + 1))
                        if k >= 2:
                            v.wait_ge(s_o2[jk], 16 * (k // 2))
                            v.wait_ge(s_ap3, k - 1)
                            v.wait_ge(s_rs3, 2 * (k - 1))
                        v.tensor_scalar(x2t[jk][:], fsb[jk][:],
                                        lst2[:, kc + 2:kc + 3],
                                        lst2[:, kc + 7:kc + 8], ALU.subtract,
                                        ALU.mult).then_inc(s_re2, 1)
                        v.wait_ge(s_re2, 2 * k + 1)
                        v.tensor_tensor(x2t[jk][:], x2t[jk][:], xot[jk][:],
                                        ALU.add).then_inc(s_re2, 1)
                        v.wait_ge(s_re2, 2 * k + 2)
                        _stats(v, lst3, kc, x2t[jk], 7 * k, s_sq3)
                    if i >= 2:
                        k = i - 2
                        jk = k % 2
                        kc = 8 * jk
                        v.wait_ge(s_rs3, 2 * (k + 1))
                        if k >= 2:
                            v.wait_ge(s_o3[jk], 16 * (k // 2))
                        v.tensor_scalar(x3t[jk][:], x2t[jk][:],
                                        lst3[:, kc + 2:kc + 3],
                                        lst3[:, kc + 7:kc + 8], ALU.subtract,
                                        ALU.mult).then_inc(s_ap3, 1)
                        v.wait_ge(s_ap3, k + 1)
                        v.reduce_max(mxc[:, k:k + 1], x3t[jk][:], axis=AX.X,
                                     apply_absolute_value=True).then_inc(s_mx3, 1)
                v.wait_ge(s_mx3, 16)
                v.tensor_copy(ldump[:, 0:8], cs[:, 0:8])
                v.reduce_max(sc128[:, 0:1], mxc[:], axis=AX.X).then_inc(s_fx, 1)
            blk.vector(dve5)

            def pe5b(t):
                t.wait_ge(s_fx, 1)
                t.transpose(pp[0][:, 0:128], sc128[:, 0:128], idf_s[:])\
                    .then_inc(s_fx, 1)
            blk.tensor(pe5b)

            def dve5b(v):
                v.wait_ge(s_fx, 2)
                v.reduce_max(lm11[0:1, 0:1], pp[0][0:1, 0:128], axis=AX.X)\
                    .then_inc(s_fx, 1)
            blk.vector(dve5b)

            def g5(g):
                for i in range(16):
                    jk = i % 2
                    g.wait_ge(s_re2, 2 * (i + 1))
                    g.dma_start(out=x2o[i * 128:(i + 1) * 128, :],
                                in_=x2t[jk][:]).then_inc(s_o2[jk], 16)
                    g.wait_ge(s_mx3, i + 1)
                    g.dma_start(out=x3o[i * 128:(i + 1) * 128, :],
                                in_=x3t[jk][:]).then_inc(s_o3[jk], 16)
                g.wait_ge(s_fx, 3)
                g.dma_start(out=lmxo, in_=lm11[0:1, 0:1]).then_inc(s_fxd, 16)
                if dbg:
                    g.dma_start(out=dbg_x1, in_=x1d).then_inc(s_fxd, 16)
                for j in range(2):
                    g.wait_ge(s_o2[j], 16 * 8)
                    g.wait_ge(s_o3[j], 16 * 8)
                    g.wait_ge(s_dqs[j], 16 * (4 if j == 0 else 3))
                g.wait_ge(s_fxd, 16 * (1 + (8 if dbg else 0)))
            blk.gpsimd(g5)

    return nc


import ml_dtypes
from concourse.bass_utils import run_bass_kernel_spmd

_BF = ml_dtypes.bfloat16


def _host_consts():
    rr = np.arange(128)
    cumT = (rr[:, None] < rr[None, :]).astype(_BF)
    lowinc = (rr[:, None] >= rr[None, :]).astype(_BF)
    ltstr = (rr[None, :] < rr[:, None]).astype(_BF)
    ltinc = (rr[None, :] <= rr[:, None]).astype(_BF)
    headsel = np.zeros((128, 2), _BF)
    headsel[0:64, 0] = 1.0
    headsel[64:128, 1] = 1.0
    iota32 = (rr[:, None] + 128.0 * np.arange(32)[None, :]).astype(np.float32)
    return cumT, lowinc, ltstr, ltinc, headsel, iota32


def kernel(**inputs):
    x = np.asarray(inputs["L0_attn_x"], dtype=np.float32)
    W_attn = np.asarray(inputs["L0_attn_W_attn"], dtype=np.float32)
    b_attn = np.asarray(inputs["L0_attn_b_attn"], dtype=np.float32)
    W_proj = np.asarray(inputs["L0_attn_W_proj"], dtype=np.float32)
    b_proj = np.asarray(inputs["L0_attn_b_proj"], dtype=np.float32)
    w_up = np.asarray(inputs["L0_ffn_w_up"], dtype=np.float32)
    w_gate = np.asarray(inputs["L0_ffn_w_gate"], dtype=np.float32)
    w_down = np.asarray(inputs["L0_ffn_w_down"], dtype=np.float32)

    cumT, lowinc, ltstr, ltinc, headsel, iota32 = _host_consts()
    common = dict(consts=consts_array(),
                  onesb=np.ones((128, 128), _BF),
                  identf=np.eye(128, dtype=np.float32),
                  onesf=np.ones((128, 128), np.float32))
    l1_common = dict(common,
                     cumT=cumT, lowinc=lowinc, ltstr=ltstr, ltinc=ltinc,
                     headsel=headsel, iota32=iota32,
                     watT=np.ascontiguousarray(W_attn.T),
                     battn24=b_attn.reshape(24, 128),
                     wpjT=np.ascontiguousarray(W_proj.T),
                     bvb=np.tile(b_attn[2048:3072], (128, 1)),
                     bprojb=np.tile(b_proj, (128, 1)))

    res1 = {}
    for h in (0, 1):
        nc1 = build_L1(h)
        ims = []
        for b in range(4):
            c = 2 * b + h
            im = dict(l1_common)
            im["xb"] = x[b]
            im["wup_sl"] = w_up[384 * c:384 * (c + 1)]
            im["wgate_sl"] = w_gate[384 * c:384 * (c + 1)]
            im["wdown_sl"] = w_down[128 * c:128 * (c + 1)]
            ims.append(im)
        r = run_bass_kernel_spmd(nc1, ims, core_ids=list(range(4)))
        for b in range(4):
            res1[2 * b + h] = r.results[b]

    lmx8 = np.array([res1[c]["lmxo"][0, 0] for c in range(8)],
                    dtype=np.float32).reshape(1, 8)
    wupdq = np.concatenate([res1[c]["wupdq_sl"] for c in range(8)], axis=0)
    wgatedq = np.concatenate([res1[c]["wgatedq_sl"] for c in range(8)], axis=0)
    wdowndq = np.concatenate([res1[c]["wdowndq_sl"] for c in range(8)], axis=0)

    nc2 = build_L2()
    im2 = [dict(x3bf=res1[c]["x3o"], lmx8=lmx8, wupdq=wupdq, wgatedq=wgatedq,
                **common) for c in range(8)]
    r2 = run_bass_kernel_spmd(nc2, im2, core_ids=list(range(8)))
    lmu8 = np.array([r2.results[c]["lmug"][0, 0] for c in range(8)],
                    dtype=np.float32).reshape(1, 8)
    lmg8 = np.array([r2.results[c]["lmug"][1, 0] for c in range(8)],
                    dtype=np.float32).reshape(1, 8)

    nc3 = build_L3()
    im3 = [dict(gateT=r2.results[c]["gateT"], lmg8=lmg8, **common)
           for c in range(8)]
    r3 = run_bass_kernel_spmd(nc3, im3, core_ids=list(range(8)))
    lmg28 = np.array([r3.results[c]["lmg2"][0, 0] for c in range(8)],
                     dtype=np.float32).reshape(1, 8)

    nc4 = build_L4()
    im4 = [dict(g2Tin=r3.results[c]["g2T"], upTin=r2.results[c]["upT"],
                lmg28=lmg28, lmu8=lmu8, **common) for c in range(8)]
    r4 = run_bass_kernel_spmd(nc4, im4, core_ids=list(range(8)))
    lmo8 = np.array([r4.results[c]["lmo"][0, 0] for c in range(8)],
                    dtype=np.float32).reshape(1, 8)

    nc5 = build_L5()
    im5 = [dict(oTin=r4.results[c]["oT"], lmo8=lmo8, wdowndq=wdowndq,
                x2in=res1[c]["x2o"], **common) for c in range(8)]
    r5 = run_bass_kernel_spmd(nc5, im5, core_ids=list(range(8)))

    out = np.zeros((4, 4096, 1024), dtype=np.float32)
    for c in range(8):
        out[c // 2, np.array(core_rows(c))] = r5.results[c]["yout"]
    return out


# version 05 (final)
# speedup vs baseline: 1.3589x; 1.0150x over previous
from contextlib import ExitStack

import numpy as np
import concourse.bass as bass
import concourse.mybir as mybir

F32 = mybir.dt.float32
BF16 = mybir.dt.bfloat16
I32 = mybir.dt.int32
AF = mybir.ActivationFunctionType
ALU = mybir.AluOpType
AX = mybir.AxisListType

TOK = 2048
CM = 1024
ID = 3072
QMAX = 32767.0

CONST_LIST = [
    ("inv_qmax", 1.0 / QMAX),
    ("eps12", 1e-12),
    ("c65536", 65536.0),
    ("c65535", 65535.0),
    ("sc216", 1.0 / 65536.0),
    ("nsc216", -1.0 / 65536.0),
    ("inv7", 1.0 / 7.0),
    ("ln_eps", 1e-5),
    ("neg_half", -0.5),
    ("inv_cm", 1.0 / CM),
    ("att_scale", 0.125),
    ("big", 30000.0),
    ("win_budget", 25.5),
    ("c64", 64.0),
    ("half", 0.5),
    ("neg_one", -1.0),
    ("neg_big", -30000.0),
    ("c1024", 1024.0),
    ("c817", 817.5),
]
CIX = {k: i for i, (k, _) in enumerate(CONST_LIST)}
NCONST = len(CONST_LIST)


def consts_array():
    v = np.array([val for _, val in CONST_LIST], dtype=np.float32)
    return np.tile(v[None, :], (128, 1))


def cc(cs, name):
    i = CIX[name]
    return cs[:, i:i + 1]


def common_inputs(nc):
    consts = nc.dram_tensor("consts", [128, NCONST], F32, kind="ExternalInput").ap()
    onesb = nc.dram_tensor("onesb", [128, 128], BF16, kind="ExternalInput").ap()
    identf = nc.dram_tensor("identf", [128, 128], F32, kind="ExternalInput").ap()
    onesf = nc.dram_tensor("onesf", [128, 128], F32, kind="ExternalInput").ap()
    return consts, onesb, identf, onesf


def _scale_prep(nc, blk, cs, onesf_s, lmin8, srow, sb, psb, s_sc, s_bc, nsc):
    def dve_scale(v):
        v.reduce_max(srow[0:1, 0:1], lmin8[:], axis=AX.X).then_inc(s_sc, 1)
        v.wait_ge(s_sc, 1)
        v.tensor_copy(srow[0:1, 3:4], cs[0:1, 0:1])
        v.tensor_scalar(srow[0:1, 0:1], srow[0:1, 0:1],
                        cc(cs, "inv_qmax")[0:1, :], cc(cs, "eps12")[0:1, :],
                        ALU.mult, ALU.max).then_inc(s_sc, 1)
        v.wait_ge(s_sc, 2)
        v.tensor_copy(srow[0:1, 3:4], cs[0:1, 0:1])
        v.reciprocal(srow[0:1, 1:2], srow[0:1, 0:1]).then_inc(s_sc, 1)
        v.wait_ge(s_sc, 3)
        v.tensor_copy(srow[0:1, 3:4], cs[0:1, 0:1])
        v.tensor_scalar(srow[0:1, 2:3], srow[0:1, 0:1],
                        cc(cs, "neg_one")[0:1, :], None, ALU.mult)\
            .then_inc(s_sc, 1)
    blk.vector(dve_scale)

    def pe_bc(t):
        t.wait_ge(s_sc, 4)
        t.matmul(psb[:, 0:3], onesf_s[0:1, 0:128], srow[0:1, 0:3],
                 start=True, stop=True).then_inc(s_bc, 1)
    blk.tensor(pe_bc)

    def act_bc(a):
        a.wait_ge(s_bc, 1)
        a.copy(sb[:], psb[:, 0:3]).then_inc(s_bc, 1)
    blk.scalar(act_bc)


def dve_round4(v, dq, yq, qi, out, sb, s_cnt, base):
    v.scalar_tensor_tensor(dq, qi, -1.0, yq, ALU.mult, ALU.add)\
        .then_inc(s_cnt, 1)
    v.wait_ge(s_cnt, base + 1)
    v.scalar_tensor_tensor(yq, dq, 0.5, qi, ALU.is_ge, ALU.add)\
        .then_inc(s_cnt, 1)
    v.wait_ge(s_cnt, base + 2)
    v.scalar_tensor_tensor(dq, dq, -0.5, yq, ALU.is_ge, ALU.add)\
        .then_inc(s_cnt, 1)
    v.wait_ge(s_cnt, base + 3)
    v.tensor_scalar(out, dq, sb[:, 0:1], sb[:, 2:3], ALU.mult, ALU.add)\
        .then_inc(s_cnt, 1)


def _xpart_max_store(nc, blk, idf_s, mxcol, ncols, sc128, psT, lmout_dram,
                     lm11, s_pre, pre_val, s_f, s_fd, g_extra_wait=None):
    def dve_a(v):
        v.wait_ge(s_pre, pre_val)
        v.reduce_max(sc128[:, 0:1], mxcol[:, 0:ncols], axis=AX.X).then_inc(s_f, 1)
    blk.vector(dve_a)

    def pe_a(t):
        t.wait_ge(s_f, 1)
        t.transpose(psT[:], sc128[:], idf_s[:]).then_inc(s_f, 1)
    blk.tensor(pe_a)

    def dve_b(v):
        v.wait_ge(s_f, 2)
        v.reduce_max(lm11[0:1, 0:1], psT[0:1, 0:128], axis=AX.X).then_inc(s_f, 1)
    blk.vector(dve_b)

    def g_a(g):
        g.wait_ge(s_f, 3)
        g.dma_start(out=lmout_dram, in_=lm11[0:1, 0:1]).then_inc(s_fd, 16)
        g.wait_ge(s_fd, 16)
        if g_extra_wait is not None:
            g_extra_wait(g)
    blk.gpsimd(g_a)


def build_L2():
    nc = bass.Bass("TRN2", target_bir_lowering=False)
    x3bf = nc.dram_tensor("x3bf", [TOK, CM], BF16, kind="ExternalInput").ap()
    lmx8 = nc.dram_tensor("lmx8", [1, 8], F32, kind="ExternalInput").ap()
    wupdq = nc.dram_tensor("wupdq", [ID, CM], BF16, kind="ExternalInput").ap()
    wgatedq = nc.dram_tensor("wgatedq", [ID, CM], BF16, kind="ExternalInput").ap()
    consts, onesb, identf, onesf = common_inputs(nc)

    upT = nc.dram_tensor("upT", [ID, TOK], BF16, kind="ExternalOutput").ap()
    gateT = nc.dram_tensor("gateT", [ID, TOK], BF16, kind="ExternalOutput").ap()
    lmug = nc.dram_tensor("lmug", [2, 1], F32, kind="ExternalOutput").ap()

    xq_dram = nc.dram_tensor("xq_dram", [TOK, CM], BF16, kind="Internal").ap()

    NT = TOK // 128
    NM = ID // 128
    NTC = TOK // 512
    NC8 = CM // 128
    NK = NM * NTC

    with ExitStack() as est:
        e = est.enter_context
        cs = e(nc.sbuf_tensor("cs", [128, NCONST], F32))
        ones_s = e(nc.sbuf_tensor("ones_s", [128, 128], BF16))
        onesf_s = e(nc.sbuf_tensor("onesf_s", [128, 128], F32))
        idf_s = e(nc.sbuf_tensor("idf_s", [128, 128], F32))
        lm8 = e(nc.sbuf_tensor("lm8", [1, 8], F32))
        sx = e(nc.sbuf_tensor("sx", [1, 4], F32))
        sxb = e(nc.sbuf_tensor("sxb", [128, 3], F32))
        x3d = [e(nc.sbuf_tensor(f"x3d{i}", [128, CM], BF16)) for i in range(2)]
        xqi = [e(nc.sbuf_tensor(f"xqi{i}", [128, CM], I32)) for i in range(2)]
        xqf = [e(nc.sbuf_tensor(f"xqf{i}", [128, CM], F32)) for i in range(2)]
        xqd = [e(nc.sbuf_tensor(f"xqd{i}", [128, CM], F32)) for i in range(2)]
        xqb = [e(nc.sbuf_tensor(f"xqb{i}", [128, CM], BF16)) for i in range(2)]
        wT = [e(nc.sbuf_tensor(f"wT{i}", [128, ID], BF16)) for i in range(NC8)]
        gT = [e(nc.sbuf_tensor(f"gT{i}", [128, ID], BF16)) for i in range(NC8)]
        xqT = [e(nc.sbuf_tensor(f"xqT{i}", [128, TOK], BF16)) for i in range(NC8)]
        ev = [e(nc.sbuf_tensor(f"ev{i}", [128, 512], BF16)) for i in range(4)]
        mxcol = e(nc.sbuf_tensor("mxcol", [128, 2 * NK], F32))
        sc128 = e(nc.sbuf_tensor("sc128", [128, 128], F32))
        lm2 = e(nc.sbuf_tensor("lm2", [2, 1], F32))
        ps = [e(nc.psum_tensor(f"ps{i}", [128, 512], F32)) for i in range(4)]
        psb = e(nc.psum_tensor("psb", [128, 128], F32))
        psT = e(nc.psum_tensor("psT", [128, 128], F32))

        s_ld0 = e(nc.semaphore("s_ld0"))
        s_sc = e(nc.semaphore("s_sc"))
        s_bc = e(nc.semaphore("s_bc"))
        s_xl = [e(nc.semaphore(f"s_xl{j}")) for j in range(2)]
        s_xa = e(nc.semaphore("s_xa"))
        s_xv = e(nc.semaphore("s_xv"))
        s_xs = [e(nc.semaphore(f"s_xs{j}")) for j in range(2)]
        s_wl = e(nc.semaphore("s_wl"))
        s_mm = e(nc.semaphore("s_mm"))
        s_ev = e(nc.semaphore("s_ev"))
        s_rd = e(nc.semaphore("s_rd"))
        s_sb = [e(nc.semaphore(f"s_sb{j}")) for j in range(4)]
        s_f = e(nc.semaphore("s_f"))
        s_fd = e(nc.semaphore("s_fd"))
        blk = e(nc.Block())

        def sync_b(s):
            s.dma_start(out=cs[:], in_=consts).then_inc(s_ld0, 16)
            s.dma_start(out=ones_s[:], in_=onesb).then_inc(s_ld0, 16)
            s.dma_start(out=onesf_s[:], in_=onesf).then_inc(s_ld0, 16)
            s.dma_start(out=idf_s[:], in_=identf).then_inc(s_ld0, 16)
            s.dma_start(out=lm8[:], in_=lmx8).then_inc(s_ld0, 16)
            for c in range(NC8):
                s.dma_start(out=wT[c][:], in_=wupdq[:, c * 128:(c + 1) * 128],
                            transpose=True).then_inc(s_wl, 16)
            for c in range(NC8):
                s.dma_start(out=gT[c][:], in_=wgatedq[:, c * 128:(c + 1) * 128],
                            transpose=True).then_inc(s_wl, 16)
            for i in range(NT):
                j = i % 2
                if i >= 2:
                    s.wait_ge(s_xa, 2 * i - 3)
                s.dma_start(out=x3d[j][:], in_=x3bf[i * 128:(i + 1) * 128, :])\
                    .then_inc(s_xl[j], 16)
            for j in range(2):
                s.wait_ge(s_xs[j], 16 * (NT // 2))
            for c in range(NC8):
                s.dma_start(out=xqT[c][:], in_=xq_dram[:, c * 128:(c + 1) * 128],
                            transpose=True).then_inc(s_wl, 16)
        blk.sync(sync_b)

        def dve_pre(v):
            v.wait_ge(s_ld0, 80)
        blk.vector(dve_pre)
        _scale_prep(nc, blk, cs, onesf_s, lm8, sx, sxb, psb, s_sc, s_bc, 3)

        def act_xq(a):
            a.wait_ge(s_bc, 2)
            for i in range(NT):
                j = i % 2
                a.wait_ge(s_xl[j], 16 * (i // 2 + 1))
                if i >= 2:
                    a.wait_ge(s_xv, 4 * (i - 2) + 3)
                a.activation(xqf[j][:], x3d[j][:], AF.Copy,
                             scale=sxb[:, 1:2]).then_inc(s_xa, 1)
                a.wait_ge(s_xa, 2 * i + 1)
                a.activation(xqi[j][:], xqf[j][:], AF.Copy).then_inc(s_xa, 1)
        blk.scalar(act_xq)

        def dve_xq(v):
            for i in range(NT):
                j = i % 2
                v.wait_ge(s_xa, 2 * i + 2)
                if i >= 2:
                    v.wait_ge(s_xs[j], 16 * (i // 2))
                dve_round4(v, xqd[j][:], xqf[j][:], xqi[j][:], xqb[j][:],
                           sxb, s_xv, 4 * i)
        blk.vector(dve_xq)

        def g_xq(g):
            for i in range(NT):
                j = i % 2
                g.wait_ge(s_xv, 4 * (i + 1))
                g.dma_start(out=xq_dram[i * 128:(i + 1) * 128, :],
                            in_=xqb[j][:]).then_inc(s_xs[j], 16)
        blk.gpsimd(g_xq)

        def pe_mm(t):
            t.wait_ge(s_wl, 16 * 3 * NC8)
            k = 0
            for wsel in range(2):
                W = wT if wsel == 0 else gT
                for m in range(NM):
                    for tc in range(NTC):
                        pk = k % 4
                        if k >= 4:
                            t.wait_ge(s_ev, k - 3)
                        mm = None
                        for c in range(NC8):
                            mm = t.matmul(ps[pk][:],
                                          W[c][:, m * 128:(m + 1) * 128],
                                          xqT[c][:, tc * 512:(tc + 1) * 512],
                                          start=(c == 0), stop=(c == NC8 - 1))
                        mm.then_inc(s_mm, 1)
                        k += 1
        blk.tensor(pe_mm)

        def act_ev(a):
            for k in range(2 * NK):
                a.wait_ge(s_mm, k + 1)
                if k >= 4:
                    a.wait_ge(s_sb[k % 4], 16 * (k // 4))
                a.activation(ev[k % 4][:], ps[k % 4][:], AF.Copy).then_inc(s_ev, 1)
        blk.scalar(act_ev)

        def dve_rd(v):
            for k in range(2 * NK):
                v.wait_ge(s_ev, k + 1)
                v.reduce_max(mxcol[:, k:k + 1], ev[k % 4][:], axis=AX.X,
                             apply_absolute_value=True).then_inc(s_rd, 1)
        blk.vector(dve_rd)

        def g_st(g):
            for k in range(2 * NK):
                g.wait_ge(s_rd, k + 1)
                wsel, r = divmod(k, NK)
                m, tc = divmod(r, NTC)
                dst = upT if wsel == 0 else gateT
                g.dma_start(out=dst[m * 128:(m + 1) * 128,
                                    tc * 512:(tc + 1) * 512],
                            in_=ev[k % 4][:]).then_inc(s_sb[k % 4], 16)
        blk.gpsimd(g_st)

        def dve_fin(v):
            v.wait_ge(s_rd, 2 * NK)
            v.reduce_max(sc128[:, 0:1], mxcol[:, 0:NK], axis=AX.X).then_inc(s_f, 1)
            v.reduce_max(sc128[:, 1:2], mxcol[:, NK:2 * NK], axis=AX.X)\
                .then_inc(s_f, 1)
        blk.vector(dve_fin)

        def pe_fin(t):
            t.wait_ge(s_f, 2)
            t.transpose(psT[:], sc128[:], idf_s[:]).then_inc(s_f, 1)
        blk.tensor(pe_fin)

        def dve_fin2(v):
            v.wait_ge(s_f, 3)
            v.reduce_max(lm2[0:2, 0:1], psT[0:2, 0:128], axis=AX.X).then_inc(s_f, 1)
        blk.vector(dve_fin2)

        def g_fin(g):
            g.wait_ge(s_f, 4)
            g.dma_start(out=lmug, in_=lm2[0:2, 0:1]).then_inc(s_fd, 16)
            g.wait_ge(s_fd, 16)
            for j in range(4):
                g.wait_ge(s_sb[j], 16 * (2 * NK // 4))
        blk.gpsimd(g_fin)

    return nc


def build_L3():
    nc = bass.Bass("TRN2", target_bir_lowering=False)
    gateT = nc.dram_tensor("gateT", [ID, TOK], BF16, kind="ExternalInput").ap()
    lmg8 = nc.dram_tensor("lmg8", [1, 8], F32, kind="ExternalInput").ap()
    consts, onesb, identf, onesf = common_inputs(nc)
    g2T = nc.dram_tensor("g2T", [ID, TOK], BF16, kind="ExternalOutput").ap()
    lmg2 = nc.dram_tensor("lmg2", [1, 1], F32, kind="ExternalOutput").ap()

    NR = ID // 128

    with ExitStack() as est:
        e = est.enter_context
        cs = e(nc.sbuf_tensor("cs", [128, NCONST], F32))
        ones_s = e(nc.sbuf_tensor("ones_s", [128, 128], BF16))
        onesf_s = e(nc.sbuf_tensor("onesf_s", [128, 128], F32))
        idf_s = e(nc.sbuf_tensor("idf_s", [128, 128], F32))
        lm8 = e(nc.sbuf_tensor("lm8", [1, 8], F32))
        srow = e(nc.sbuf_tensor("srow", [1, 4], F32))
        sb = e(nc.sbuf_tensor("sb", [128, 3], F32))
        gt = [e(nc.sbuf_tensor(f"gt{i}", [128, TOK], BF16)) for i in range(2)]
        gqi = [e(nc.sbuf_tensor(f"gqi{i}", [128, TOK], I32)) for i in range(2)]
        gq = [e(nc.sbuf_tensor(f"gq{i}", [128, TOK], F32)) for i in range(2)]
        ygf = [e(nc.sbuf_tensor(f"ygf{i}", [128, TOK], F32)) for i in range(2)]
        gqd = [e(nc.sbuf_tensor(f"gqd{i}", [128, TOK], F32)) for i in range(2)]
        sig = [e(nc.sbuf_tensor(f"sig{i}", [128, TOK], F32)) for i in range(2)]
        yuf = [e(nc.sbuf_tensor(f"yuf{i}", [128, TOK], F32)) for i in range(2)]
        qd = [e(nc.sbuf_tensor(f"qd{i}", [128, TOK], F32)) for i in range(2)]
        qi = [e(nc.sbuf_tensor(f"qi{i}", [128, TOK], I32)) for i in range(2)]
        sfin = [e(nc.sbuf_tensor(f"sfin{i}", [128, TOK], F32)) for i in range(2)]
        g2o = [e(nc.sbuf_tensor(f"g2o{i}", [128, TOK], BF16)) for i in range(2)]
        mxcol = e(nc.sbuf_tensor("mxcol", [128, NR], F32))
        sc128 = e(nc.sbuf_tensor("sc128", [128, 128], F32))
        lm11 = e(nc.sbuf_tensor("lm11", [1, 1], F32))
        psb = e(nc.psum_tensor("psb", [128, 128], F32))
        psT = e(nc.psum_tensor("psT", [128, 128], F32))

        s_ld0 = e(nc.semaphore("s_ld0"))
        s_sc = e(nc.semaphore("s_sc"))
        s_bc = e(nc.semaphore("s_bc"))
        s_l = [e(nc.semaphore(f"s_l{j}")) for j in range(2)]
        s_a1 = e(nc.semaphore("s_a1"))
        s_v1 = e(nc.semaphore("s_v1"))
        s_a2 = e(nc.semaphore("s_a2"))
        s_v2 = e(nc.semaphore("s_v2"))
        s_v3 = e(nc.semaphore("s_v3"))
        s_v4 = e(nc.semaphore("s_v4"))
        s_rd = e(nc.semaphore("s_rd"))
        s_sb = [e(nc.semaphore(f"s_sb{j}")) for j in range(2)]
        s_f = e(nc.semaphore("s_f"))
        s_fd = e(nc.semaphore("s_fd"))
        blk = e(nc.Block())

        def sync_b(s):
            s.dma_start(out=cs[:], in_=consts).then_inc(s_ld0, 16)
            s.dma_start(out=ones_s[:], in_=onesb).then_inc(s_ld0, 16)
            s.dma_start(out=onesf_s[:], in_=onesf).then_inc(s_ld0, 16)
            s.dma_start(out=idf_s[:], in_=identf).then_inc(s_ld0, 16)
            s.dma_start(out=lm8[:], in_=lmg8).then_inc(s_ld0, 16)
            for i in range(NR):
                j = i % 2
                if i >= 2:
                    s.wait_ge(s_a1, 2 * i - 3)
                s.dma_start(out=gt[j][:], in_=gateT[i * 128:(i + 1) * 128, :])\
                    .then_inc(s_l[j], 16)
        blk.sync(sync_b)

        def dve_pre(v):
            v.wait_ge(s_ld0, 80)
        blk.vector(dve_pre)
        _scale_prep(nc, blk, cs, onesf_s, lm8, srow, sb, psb, s_sc, s_bc, 3)

        def act_all(a):
            a.wait_ge(s_bc, 2)
            for i in range(NR + 1):
                if i < NR:
                    j = i % 2
                    a.wait_ge(s_l[j], 16 * (i // 2 + 1))
                    if i >= 2:
                        a.wait_ge(s_v1, 4 * (i - 2) + 3)
                    a.activation(ygf[j][:], gt[j][:], AF.Copy,
                                 scale=sb[:, 1:2]).then_inc(s_a1, 1)
                    a.wait_ge(s_a1, 2 * i + 1)
                    a.activation(gqi[j][:], ygf[j][:], AF.Copy)\
                        .then_inc(s_a1, 1)
                if i >= 1:
                    k = i - 1
                    jk = k % 2
                    a.wait_ge(s_v1, 4 * k + 4)
                    if k >= 2:
                        a.wait_ge(s_v2, 4 * (k - 2) + 3)
                    a.activation(sig[jk][:], gq[jk][:], AF.Sigmoid)\
                        .then_inc(s_a2, 1)
                    a.wait_ge(s_a2, 3 * k + 1)
                    a.activation(yuf[jk][:], sig[jk][:], AF.Copy,
                                 scale=cc(cs, "c65536")).then_inc(s_a2, 1)
                    a.wait_ge(s_a2, 3 * k + 2)
                    a.activation(qi[jk][:], yuf[jk][:], AF.Copy)\
                        .then_inc(s_a2, 1)
        blk.scalar(act_all)

        def dve_all(v):
            for i in range(NR + 1):
                if i < NR:
                    j = i % 2
                    v.wait_ge(s_a1, 2 * i + 2)
                    if i >= 2:
                        v.wait_ge(s_a2, 3 * (i - 2) + 1)
                        v.wait_ge(s_v4, i - 1)
                    dve_round4(v, gqd[j][:], ygf[j][:], gqi[j][:], gq[j][:],
                               sb, s_v1, 4 * i)
                if i >= 1:
                    k = i - 1
                    jk = k % 2
                    v.wait_ge(s_a2, 3 * k + 3)
                    v.scalar_tensor_tensor(qd[jk][:], qi[jk][:], -1.0,
                                           yuf[jk][:], ALU.mult, ALU.add)\
                        .then_inc(s_v2, 1)
                    v.wait_ge(s_v2, 4 * k + 1)
                    v.scalar_tensor_tensor(yuf[jk][:], qd[jk][:], 0.5,
                                           qi[jk][:], ALU.is_ge, ALU.add)\
                        .then_inc(s_v2, 1)
                    v.wait_ge(s_v2, 4 * k + 2)
                    v.scalar_tensor_tensor(qd[jk][:], qd[jk][:], -0.5,
                                           yuf[jk][:], ALU.is_ge, ALU.add)\
                        .then_inc(s_v2, 1)
                    v.wait_ge(s_v2, 4 * k + 3)
                    v.tensor_scalar(sfin[jk][:], qd[jk][:], cc(cs, "c65536"),
                                    cc(cs, "sc216"), ALU.min, ALU.mult)\
                        .then_inc(s_v2, 1)
                    v.wait_ge(s_v2, 4 * k + 4)
                    if k >= 2:
                        v.wait_ge(s_sb[jk], 16 * (k // 2))
                    v.scalar_tensor_tensor(g2o[jk][:], sfin[jk][:],
                                           cc(cs, "nsc216"), gq[jk][:],
                                           ALU.add, ALU.mult).then_inc(s_v4, 1)
                    v.wait_ge(s_v4, k + 1)
                    v.reduce_max(mxcol[:, k:k + 1], g2o[jk][:], axis=AX.X,
                                 apply_absolute_value=True).then_inc(s_rd, 1)
        blk.vector(dve_all)

        def g_st(g):
            for i in range(NR):
                j = i % 2
                g.wait_ge(s_rd, i + 1)
                g.dma_start(out=g2T[i * 128:(i + 1) * 128, :],
                            in_=g2o[j][:]).then_inc(s_sb[j], 16)
        blk.gpsimd(g_st)

        def gw(g):
            for j in range(2):
                g.wait_ge(s_sb[j], 16 * (NR // 2))
        _xpart_max_store(nc, blk, idf_s, mxcol, NR, sc128, psT, lmg2, lm11,
                         s_rd, NR, s_f, s_fd, g_extra_wait=gw)

    return nc


def build_L4():
    nc = bass.Bass("TRN2", target_bir_lowering=False)
    g2Tin = nc.dram_tensor("g2Tin", [ID, TOK], BF16, kind="ExternalInput").ap()
    upTin = nc.dram_tensor("upTin", [ID, TOK], BF16, kind="ExternalInput").ap()
    lmg28 = nc.dram_tensor("lmg28", [1, 8], F32, kind="ExternalInput").ap()
    lmu8 = nc.dram_tensor("lmu8", [1, 8], F32, kind="ExternalInput").ap()
    consts, onesb, identf, onesf = common_inputs(nc)
    oT = nc.dram_tensor("oT", [ID, TOK], BF16, kind="ExternalOutput").ap()
    lmo = nc.dram_tensor("lmo", [1, 1], F32, kind="ExternalOutput").ap()

    NR = ID // 128

    with ExitStack() as est:
        e = est.enter_context
        cs = e(nc.sbuf_tensor("cs", [128, NCONST], F32))
        ones_s = e(nc.sbuf_tensor("ones_s", [128, 128], BF16))
        onesf_s = e(nc.sbuf_tensor("onesf_s", [128, 128], F32))
        idf_s = e(nc.sbuf_tensor("idf_s", [128, 128], F32))
        lmg = e(nc.sbuf_tensor("lmg", [1, 8], F32))
        lmu = e(nc.sbuf_tensor("lmu", [1, 8], F32))
        srow = e(nc.sbuf_tensor("srow", [1, 4], F32))
        srow2 = e(nc.sbuf_tensor("srow2", [1, 4], F32))
        sbg = e(nc.sbuf_tensor("sbg", [128, 3], F32))
        sbu = e(nc.sbuf_tensor("sbu", [128, 3], F32))
        g2t = [e(nc.sbuf_tensor(f"g2t{i}", [128, TOK], BF16)) for i in range(2)]
        upt = [e(nc.sbuf_tensor(f"upt{i}", [128, TOK], BF16)) for i in range(2)]
        g2i = [e(nc.sbuf_tensor(f"g2i{i}", [128, TOK], I32)) for i in range(2)]
        upi = [e(nc.sbuf_tensor(f"upi{i}", [128, TOK], I32)) for i in range(2)]
        g2f = [e(nc.sbuf_tensor(f"g2f{i}", [128, TOK], F32)) for i in range(2)]
        upf = [e(nc.sbuf_tensor(f"upf{i}", [128, TOK], F32)) for i in range(2)]
        qdd = e(nc.sbuf_tensor("qdd", [128, TOK], F32))
        g2q = [e(nc.sbuf_tensor(f"g2q{i}", [128, TOK], F32)) for i in range(2)]
        upq = [e(nc.sbuf_tensor(f"upq{i}", [128, TOK], F32)) for i in range(2)]
        oo = [e(nc.sbuf_tensor(f"oo{i}", [128, TOK], BF16)) for i in range(2)]
        mxcol = e(nc.sbuf_tensor("mxcol", [128, NR], F32))
        sc128 = e(nc.sbuf_tensor("sc128", [128, 128], F32))
        lm11 = e(nc.sbuf_tensor("lm11", [1, 1], F32))
        psb = e(nc.psum_tensor("psb", [128, 128], F32))
        psb2 = e(nc.psum_tensor("psb2", [128, 128], F32))
        psT = e(nc.psum_tensor("psT", [128, 128], F32))

        s_ld0 = e(nc.semaphore("s_ld0"))
        s_sc = e(nc.semaphore("s_sc"))
        s_bc = e(nc.semaphore("s_bc"))
        s_sc2 = e(nc.semaphore("s_sc2"))
        s_bc2 = e(nc.semaphore("s_bc2"))
        s_l = [e(nc.semaphore(f"s_l{j}")) for j in range(2)]
        s_a1 = e(nc.semaphore("s_a1"))
        s_v1 = e(nc.semaphore("s_v1"))
        s_v2 = e(nc.semaphore("s_v2"))
        s_rd = e(nc.semaphore("s_rd"))
        s_sb = [e(nc.semaphore(f"s_sb{j}")) for j in range(2)]
        s_f = e(nc.semaphore("s_f"))
        s_fd = e(nc.semaphore("s_fd"))
        blk = e(nc.Block())

        def sync_b(s):
            s.dma_start(out=cs[:], in_=consts).then_inc(s_ld0, 16)
            s.dma_start(out=ones_s[:], in_=onesb).then_inc(s_ld0, 16)
            s.dma_start(out=onesf_s[:], in_=onesf).then_inc(s_ld0, 16)
            s.dma_start(out=idf_s[:], in_=identf).then_inc(s_ld0, 16)
            s.dma_start(out=lmg[:], in_=lmg28).then_inc(s_ld0, 16)
            s.dma_start(out=lmu[:], in_=lmu8).then_inc(s_ld0, 16)
            for i in range(NR):
                j = i % 2
                if i >= 2:
                    s.wait_ge(s_a1, 4 * i - 5)
                s.dma_start(out=g2t[j][:],
                            in_=g2Tin[i * 128:(i + 1) * 128, :]).then_inc(s_l[j], 16)
                s.dma_start(out=upt[j][:],
                            in_=upTin[i * 128:(i + 1) * 128, :]).then_inc(s_l[j], 16)
        blk.sync(sync_b)

        def dve_pre(v):
            v.wait_ge(s_ld0, 96)
        blk.vector(dve_pre)
        _scale_prep(nc, blk, cs, onesf_s, lmg, srow, sbg, psb, s_sc, s_bc, 3)
        _scale_prep(nc, blk, cs, onesf_s, lmu, srow2, sbu, psb2, s_sc2, s_bc2, 3)

        def act1(a):
            a.wait_ge(s_bc, 2)
            a.wait_ge(s_bc2, 2)
            for i in range(NR):
                j = i % 2
                a.wait_ge(s_l[j], 32 * (i // 2 + 1))
                if i >= 2:
                    a.wait_ge(s_v1, 8 * (i - 2) + 7)
                a.activation(g2f[j][:], g2t[j][:], AF.Copy,
                             scale=sbg[:, 1:2]).then_inc(s_a1, 1)
                a.wait_ge(s_a1, 4 * i + 1)
                a.activation(g2i[j][:], g2f[j][:], AF.Copy).then_inc(s_a1, 1)
                a.wait_ge(s_a1, 4 * i + 2)
                a.activation(upf[j][:], upt[j][:], AF.Copy,
                             scale=sbu[:, 1:2]).then_inc(s_a1, 1)
                a.wait_ge(s_a1, 4 * i + 3)
                a.activation(upi[j][:], upf[j][:], AF.Copy).then_inc(s_a1, 1)
        blk.scalar(act1)

        def dve1(v):
            for i in range(NR):
                j = i % 2
                v.wait_ge(s_a1, 4 * i + 2)
                if i >= 1:
                    v.wait_ge(s_v1, 8 * i)
                if i >= 2:
                    v.wait_ge(s_v2, i - 1)
                dve_round4(v, qdd[:], g2f[j][:], g2i[j][:], g2q[j][:],
                           sbg, s_v1, 8 * i)
                v.wait_ge(s_a1, 4 * i + 4)
                v.wait_ge(s_v1, 8 * i + 4)
                dve_round4(v, qdd[:], upf[j][:], upi[j][:], upq[j][:],
                           sbu, s_v1, 8 * i + 4)
                v.wait_ge(s_v1, 8 * (i + 1))
                if i >= 2:
                    v.wait_ge(s_sb[j], 16 * (i // 2))
                v.tensor_tensor(oo[j][:], g2q[j][:], upq[j][:],
                                ALU.mult).then_inc(s_v2, 1)
                v.wait_ge(s_v2, i + 1)
                v.reduce_max(mxcol[:, i:i + 1], oo[j][:], axis=AX.X,
                             apply_absolute_value=True).then_inc(s_rd, 1)
        blk.vector(dve1)

        def g_st(g):
            for i in range(NR):
                j = i % 2
                g.wait_ge(s_rd, i + 1)
                g.dma_start(out=oT[i * 128:(i + 1) * 128, :],
                            in_=oo[j][:]).then_inc(s_sb[j], 16)
        blk.gpsimd(g_st)

        def gw(g):
            for j in range(2):
                g.wait_ge(s_sb[j], 16 * (NR // 2))
        _xpart_max_store(nc, blk, idf_s, mxcol, NR, sc128, psT, lmo, lm11,
                         s_rd, NR, s_f, s_fd, g_extra_wait=gw)

    return nc


def build_L5():
    nc = bass.Bass("TRN2", target_bir_lowering=False)
    oTin = nc.dram_tensor("oTin", [ID, TOK], BF16, kind="ExternalInput").ap()
    lmo8 = nc.dram_tensor("lmo8", [1, 8], F32, kind="ExternalInput").ap()
    wdowndq = nc.dram_tensor("wdowndq", [CM, ID], BF16, kind="ExternalInput").ap()
    x2in = nc.dram_tensor("x2in", [TOK, CM], F32, kind="ExternalInput").ap()
    consts, onesb, identf, onesf = common_inputs(nc)
    yout = nc.dram_tensor("yout", [TOK, CM], F32, kind="ExternalOutput").ap()

    NR = ID // 128
    NT = TOK // 128

    with ExitStack() as est:
        e = est.enter_context
        cs = e(nc.sbuf_tensor("cs", [128, NCONST], F32))
        ones_s = e(nc.sbuf_tensor("ones_s", [128, 128], BF16))
        onesf_s = e(nc.sbuf_tensor("onesf_s", [128, 128], F32))
        idf_s = e(nc.sbuf_tensor("idf_s", [128, 128], F32))
        lm8 = e(nc.sbuf_tensor("lm8", [1, 8], F32))
        srow = e(nc.sbuf_tensor("srow", [1, 4], F32))
        sb = e(nc.sbuf_tensor("sb", [128, 3], F32))
        ot = [e(nc.sbuf_tensor(f"ot{i}", [128, TOK], BF16)) for i in range(2)]
        oi = [e(nc.sbuf_tensor(f"oi{i}", [128, TOK], I32)) for i in range(2)]
        of = [e(nc.sbuf_tensor(f"of{i}", [128, TOK], F32)) for i in range(2)]
        od = e(nc.sbuf_tensor("od", [128, TOK], F32))
        oq = [e(nc.sbuf_tensor(f"oq{i}", [128, TOK], BF16)) for i in range(NR)]
        wdT = [e(nc.sbuf_tensor(f"wdT{i}", [128, CM], BF16)) for i in range(NR)]
        x2s = [e(nc.sbuf_tensor(f"x2s{i}", [128, CM], F32)) for i in range(2)]
        stats = e(nc.sbuf_tensor("stats", [128, 16], F32))
        fsb = [of[0][:, 0:CM], of[1][:, 0:CM]]
        yo = [of[0][:, CM:2 * CM], of[1][:, CM:2 * CM]]
        dump = od
        psb = e(nc.psum_tensor("psb", [128, 128], F32))
        pm = [e(nc.psum_tensor(f"pm{i}", [128, CM], F32)) for i in range(2)]

        s_ld0 = e(nc.semaphore("s_ld0"))
        s_sc = e(nc.semaphore("s_sc"))
        s_bc = e(nc.semaphore("s_bc"))
        s_l = [e(nc.semaphore(f"s_l{j}")) for j in range(2)]
        s_wl = e(nc.semaphore("s_wl"))
        s_xl = [e(nc.semaphore(f"s_xl{j}")) for j in range(2)]
        s_a1 = e(nc.semaphore("s_a1"))
        s_a1x = e(nc.semaphore("s_a1x"))
        s_v1 = e(nc.semaphore("s_v1"))
        s_mm = e(nc.semaphore("s_mm"))
        s_v2 = e(nc.semaphore("s_v2"))
        s_a2 = e(nc.semaphore("s_a2"))
        s_v3 = e(nc.semaphore("s_v3"))
        s_sb = [e(nc.semaphore(f"s_sb{j}")) for j in range(2)]
        blk = e(nc.Block())

        def sync_b(s):
            s.dma_start(out=cs[:], in_=consts).then_inc(s_ld0, 16)
            s.dma_start(out=ones_s[:], in_=onesb).then_inc(s_ld0, 16)
            s.dma_start(out=onesf_s[:], in_=onesf).then_inc(s_ld0, 16)
            s.dma_start(out=idf_s[:], in_=identf).then_inc(s_ld0, 16)
            s.dma_start(out=lm8[:], in_=lmo8).then_inc(s_ld0, 16)
            for c in range(NR):
                s.dma_start(out=wdT[c][:], in_=wdowndq[:, c * 128:(c + 1) * 128],
                            transpose=True).then_inc(s_wl, 16)
            for i in range(NR):
                j = i % 2
                if i >= 2:
                    s.wait_ge(s_a1, 2 * i - 3)
                s.dma_start(out=ot[j][:], in_=oTin[i * 128:(i + 1) * 128, :])\
                    .then_inc(s_l[j], 16)
            for tb in range(NT):
                j = tb % 2
                if tb >= 2:
                    s.wait_ge(s_v3, 2 * (tb - 1))
                s.dma_start(out=x2s[j][:],
                            in_=x2in[tb * 128:(tb + 1) * 128, :]).then_inc(s_xl[j], 16)
        blk.sync(sync_b)

        def dve_pre(v):
            v.wait_ge(s_ld0, 80)
        blk.vector(dve_pre)
        _scale_prep(nc, blk, cs, onesf_s, lm8, srow, sb, psb, s_sc, s_bc, 3)

        def act1(a):
            a.wait_ge(s_bc, 2)
            for i in range(NR):
                j = i % 2
                a.wait_ge(s_l[j], 16 * (i // 2 + 1))
                if i >= 2:
                    a.wait_ge(s_v1, 4 * (i - 2) + 3)
                a.activation(of[j][:], ot[j][:], AF.Copy,
                             scale=sb[:, 1:2]).then_inc(s_a1, 1)
                a.wait_ge(s_a1, 2 * i + 1)
                a.activation(oi[j][:], of[j][:], AF.Copy).then_inc(s_a1, 1)
        blk.scalar(act1)

        def dve1(v):
            for i in range(NR):
                j = i % 2
                v.wait_ge(s_a1, 2 * i + 2)
                if i >= 1:
                    v.wait_ge(s_v1, 4 * i)
                dve_round4(v, od[:], of[j][:], oi[j][:], oq[i][:],
                           sb, s_v1, 4 * i)
        blk.vector(dve1)

        def pe_mm(t):
            t.wait_ge(s_wl, 16 * NR)
            t.wait_ge(s_v1, 4 * NR)
            for tb in range(NT):
                if tb >= 2:
                    t.wait_ge(s_a1x, tb - 1)
                mm = None
                for half in range(2):
                    for c in range(NR):
                        mm = t.matmul(pm[tb % 2][:, half * 512:(half + 1) * 512],
                                      oq[c][:, tb * 128:(tb + 1) * 128],
                                      wdT[c][:, half * 512:(half + 1) * 512],
                                      start=(c == 0), stop=(c == NR - 1))
                mm.then_inc(s_mm, 1)
        blk.tensor(pe_mm)

        def act_tail(a):
            for i in range(NT + 1):
                if i < NT:
                    tb = i
                    a.wait_ge(s_mm, tb + 1)
                    if tb >= 2:
                        a.wait_ge(s_v3, 2 * (tb - 1) - 1)
                    a.activation(fsb[tb % 2], pm[tb % 2][:], AF.Copy)\
                        .then_inc(s_a1x, 1)
                if i >= 1:
                    k = i - 1
                    kc = 8 * (k % 2)
                    a.wait_ge(s_v2, 7 * (k + 1))
                    a.activation(stats[:, kc + 6:kc + 7], stats[:, kc + 5:kc + 6],
                                 AF.Ln, bias=cc(cs, "ln_eps")).then_inc(s_a2, 1)
                    a.wait_ge(s_a2, 2 * k + 1)
                    a.activation(stats[:, kc + 7:kc + 8], stats[:, kc + 6:kc + 7],
                                 AF.Exp, scale=cc(cs, "neg_half")).then_inc(s_a2, 1)
        blk.scalar(act_tail)

        def dve_tail(v):
            for i in range(NT + 1):
                if i < NT:
                    tb = i
                    tc = 8 * (tb % 2)
                    base = 7 * tb
                    v.wait_ge(s_a1x, tb + 1)
                    if tb >= 2:
                        v.wait_ge(s_v3, 2 * (tb - 1))
                    v.reduce_sum(stats[:, tc + 0:tc + 1], fsb[tb % 2], axis=AX.X)\
                        .then_inc(s_v2, 1)
                    v.scalar_tensor_tensor(dump[:, 8:8 + CM], fsb[tb % 2], 1.0,
                                           fsb[tb % 2], ALU.bypass, ALU.mult,
                                           accum_out=stats[:, tc + 1:tc + 2])\
                        .then_inc(s_v2, 1)
                    v.wait_ge(s_v2, base + 2)
                    v.tensor_scalar(stats[:, tc + 2:tc + 3], stats[:, tc + 0:tc + 1],
                                    cc(cs, "inv_cm"), None, ALU.mult).then_inc(s_v2, 1)
                    v.wait_ge(s_v2, base + 3)
                    v.tensor_copy(dump[:, 0:8], cs[:, 0:8])
                    v.tensor_scalar(stats[:, tc + 3:tc + 4], stats[:, tc + 2:tc + 3],
                                    stats[:, tc + 2:tc + 3], cc(cs, "neg_one"),
                                    ALU.mult, ALU.mult).then_inc(s_v2, 1)
                    v.wait_ge(s_v2, base + 4)
                    v.tensor_scalar(stats[:, tc + 4:tc + 5], stats[:, tc + 1:tc + 2],
                                    cc(cs, "inv_cm"), None, ALU.mult).then_inc(s_v2, 1)
                    v.wait_ge(s_v2, base + 5)
                    v.tensor_copy(dump[:, 0:8], cs[:, 0:8])
                    v.tensor_tensor(stats[:, tc + 5:tc + 6], stats[:, tc + 4:tc + 5],
                                    stats[:, tc + 3:tc + 4], ALU.add).then_inc(s_v2, 2)
                if i >= 1:
                    k = i - 1
                    jk = k % 2
                    kc = 8 * jk
                    v.wait_ge(s_a2, 2 * (k + 1))
                    v.wait_ge(s_xl[jk], 16 * (k // 2 + 1))
                    if k >= 2:
                        v.wait_ge(s_sb[jk], 16 * (k // 2))
                    v.tensor_scalar(yo[jk], fsb[jk], stats[:, kc + 2:kc + 3],
                                    stats[:, kc + 7:kc + 8], ALU.subtract, ALU.mult)\
                        .then_inc(s_v3, 1)
                    v.wait_ge(s_v3, 2 * k + 1)
                    v.tensor_tensor(yo[jk], yo[jk], x2s[jk][:],
                                    ALU.add).then_inc(s_v3, 1)
        blk.vector(dve_tail)

        def g_st(g):
            for tb in range(NT):
                j = tb % 2
                g.wait_ge(s_v3, 2 * (tb + 1))
                g.dma_start(out=yout[tb * 128:(tb + 1) * 128, :],
                            in_=yo[j]).then_inc(s_sb[j], 16)
            for j in range(2):
                g.wait_ge(s_sb[j], 16 * (NT // 2))
        blk.gpsimd(g_st)

    return nc


def blocks_for_half(h):
    pairs = [(k, 31 - k) for k in range(16)]
    sel = []
    for i, (a, b) in enumerate(pairs):
        if i % 2 == h:
            sel += [a, b]
    return sorted(sel)


def core_rows(c):
    h = c % 2
    rows = []
    for blkid in blocks_for_half(h):
        rows += list(range(blkid * 128, (blkid + 1) * 128))
    return rows


"""Builder for launch L1 of the sandwich kernel (per-core, parity h).

LN1, QKV, forgetting-attention (psum-resident FF cumsum, window-min bisection
pruning threshold, E=exp(-FF)*keep), bounded-softmax, PV, proj, LN2+residual,
LN3, FFN-weight dequant slices.

Host-transposed weight layouts (pure layout): watT = W_attn.T [1024,3072],
wpjT = W_proj.T [1024,1024]. headsel [128,2] f32: col0 = 1 on partitions
0..63, col1 = 1 on 64..127.
"""



TOK = 2048
CM = 1024
ID = 3072
T = 4096
NPAIR = 8


def ff_sched(h, pass_b):
    lo, hi = (4, 8) if pass_b else (0, 4)
    istart = 16 if pass_b else 0
    out = []
    for I in range(istart, 32):
        w = 128 * (I + 1)
        for jc in range(lo, min(hi, (w + 511) // 512)):
            nj = min(512, w - 512 * jc)
            out.append((I, jc, nj))
    return out


def build_L1(h, dbg=False):
    my_blocks = blocks_for_half(h)
    my_pos = {blk: i for i, blk in enumerate(my_blocks)}
    mine = set(my_blocks)
    icq = [my_blocks[4 * k:4 * k + 4] for k in range(4)]
    JCs = [max(bl) + 1 for bl in icq]
    myB = [b for b in my_blocks if b >= 16]
    myBpos = {b: i for i, b in enumerate(myB)}

    SCH = [(I, jc, nj, False) for (I, jc, nj) in ff_sched(h, False)] + \
          [(I, jc, nj, True) for (I, jc, nj) in ff_sched(h, True)]
    first_use, last_use = {}, {}
    for idx, (I, jc, nj, pb) in enumerate(SCH):
        key = (pb, jc)
        if key not in first_use:
            first_use[key] = idx
        last_use[key] = idx

    nc = bass.Bass("TRN2", target_bir_lowering=False)
    xbD = nc.dram_tensor("xb", [T, CM], F32, kind="ExternalInput").ap()
    watT = nc.dram_tensor("watT", [CM, 3 * CM], F32, kind="ExternalInput").ap()
    battn24 = nc.dram_tensor("battn24", [24, 128], F32, kind="ExternalInput").ap()
    wpjT = nc.dram_tensor("wpjT", [CM, CM], F32, kind="ExternalInput").ap()
    bvbD = nc.dram_tensor("bvb", [128, CM], F32, kind="ExternalInput").ap()
    bprojbD = nc.dram_tensor("bprojb", [128, CM], F32, kind="ExternalInput").ap()
    wup_sl = nc.dram_tensor("wup_sl", [384, CM], F32, kind="ExternalInput").ap()
    wgate_sl = nc.dram_tensor("wgate_sl", [384, CM], F32, kind="ExternalInput").ap()
    wdown_sl = nc.dram_tensor("wdown_sl", [128, ID], F32, kind="ExternalInput").ap()
    constsD = nc.dram_tensor("consts", [128, NCONST], F32, kind="ExternalInput").ap()
    onesbD = nc.dram_tensor("onesb", [128, 128], BF16, kind="ExternalInput").ap()
    identfD = nc.dram_tensor("identf", [128, 128], F32, kind="ExternalInput").ap()
    onesfD = nc.dram_tensor("onesf", [128, 128], F32, kind="ExternalInput").ap()
    cumTD = nc.dram_tensor("cumT", [128, 128], BF16, kind="ExternalInput").ap()
    lowincD = nc.dram_tensor("lowinc", [128, 128], BF16, kind="ExternalInput").ap()
    ltstrD = nc.dram_tensor("ltstr", [128, 128], BF16, kind="ExternalInput").ap()
    ltincD = nc.dram_tensor("ltinc", [128, 128], BF16, kind="ExternalInput").ap()
    headselD = nc.dram_tensor("headsel", [128, 2], BF16, kind="ExternalInput").ap()
    iotaD = nc.dram_tensor("iota32", [128, 32], F32, kind="ExternalInput").ap()
    identbD = nc.dram_tensor("identb", [128, 128], BF16, kind="ExternalInput").ap()

    x2o = nc.dram_tensor("x2o", [TOK, CM], F32, kind="ExternalOutput").ap()
    x3o = nc.dram_tensor("x3o", [TOK, CM], BF16, kind="ExternalOutput").ap()
    lmxo = nc.dram_tensor("lmxo", [1, 1], F32, kind="ExternalOutput").ap()
    wupdq_sl = nc.dram_tensor("wupdq_sl", [384, CM], BF16, kind="ExternalOutput").ap()
    wgatedq_sl = nc.dram_tensor("wgatedq_sl", [384, CM], BF16, kind="ExternalOutput").ap()
    wdowndq_sl = nc.dram_tensor("wdowndq_sl", [128, ID], BF16, kind="ExternalOutput").ap()
    if dbg:
        dbg_x1 = nc.dram_tensor("dbg_x1", [T, CM], BF16, kind="ExternalOutput").ap()
        dbg_E = nc.dram_tensor("dbg_E", [TOK, T], BF16, kind="ExternalOutput").ap()
        dbg_y = nc.dram_tensor("dbg_y", [TOK, CM], BF16, kind="ExternalOutput").ap()
        dbg_kT = nc.dram_tensor("dbg_kT", [CM, T], BF16, kind="ExternalOutput").ap()
        dbg_m = nc.dram_tensor("dbg_m", [1, 1], F32, kind="ExternalOutput").ap()

    x1d = nc.dram_tensor("x1d", [T, CM], BF16, kind="Internal").ap()
    qTd = nc.dram_tensor("qTd", [CM, T], BF16, kind="Internal").ap()
    kTd = nc.dram_tensor("kTd", [CM, T], BF16, kind="Internal").ap()
    Ed = nc.dram_tensor("Ed", [TOK, T], BF16, kind="Internal").ap()
    ffAd = nc.dram_tensor("ffAd", [8 * 128, TOK], F32, kind="Internal").ap()
    ydD = nc.dram_tensor("yd", [TOK, CM], BF16, kind="Internal").ap()

    est = ExitStack()
    with est:
        e = est.enter_context

        def sem(name):
            return e(nc.semaphore(name))

        cs = e(nc.sbuf_tensor("cs", [128, NCONST], F32))
        ones_s = e(nc.sbuf_tensor("ones_s", [128, 128], BF16))
        onesf_s = e(nc.sbuf_tensor("onesf_s", [128, 128], F32))
        idf_s = e(nc.sbuf_tensor("idf_s", [128, 128], F32))
        idb_s = e(nc.sbuf_tensor("idb_s", [128, 128], BF16))
        cum_s = e(nc.sbuf_tensor("cum_s", [128, 128], BF16))
        low_s = e(nc.sbuf_tensor("low_s", [128, 128], BF16))
        lts_s = e(nc.sbuf_tensor("lts_s", [128, 128], BF16))
        lti_s = e(nc.sbuf_tensor("lti_s", [128, 128], BF16))
        hsel_s = e(nc.sbuf_tensor("hsel_s", [128, 2], BF16))
        iota_s = e(nc.sbuf_tensor("iota_s", [128, 32], F32))
        bT_s = e(nc.sbuf_tensor("bT_s", [128, 24], F32))
        bvb_s = e(nc.sbuf_tensor("bvb_s", [128, CM], F32))
        bpb_s = e(nc.sbuf_tensor("bpb_s", [128, CM], F32))
        ebias = e(nc.sbuf_tensor("ebias", [128, 1], F32))
        mbc = e(nc.sbuf_tensor("mbc", [128, 144], F32))
        sc128 = e(nc.sbuf_tensor("sc128", [128, 128], F32))
        mrow = e(nc.sbuf_tensor("mrow", [2, 8], F32))
        ldump = e(nc.sbuf_tensor("ldump", [128, 8 + CM], F32))
        zer_s = e(nc.sbuf_tensor("zer_s", [128, 256], BF16))
        lm11 = e(nc.sbuf_tensor("lm11", [1, 1], F32))

        s_c0 = sem("s_c0")
        s_bt = sem("s_bt")
        s_dql = [sem(f"s_dql{j}") for j in range(2)]
        s_dqw = sem("s_dqw"); s_dqc = sem("s_dqc"); s_dqi = sem("s_dqi")
        s_dqr = sem("s_dqr"); s_dqb = sem("s_dqb")
        s_dqs = [sem(f"s_dqs{j}") for j in range(2)]
        s_x0 = [sem(f"s_x0{j}") for j in range(2)]
        s_ssq = sem("s_ssq")
        s_rstd = sem("s_rstd"); s_app = sem("s_app")
        s_l1s = [sem(f"s_l1s{j}") for j in range(2)]
        s_wl = [sem(f"s_wl{j}") for j in range(2)]
        s_wc = sem("s_wc")
        s_xT = sem("s_xT")
        s_kv = sem("s_kv")
        s_kve = [sem(f"s_kve{j}") for j in range(2)]
        s_krd = sem("s_krd"); s_qrd = sem("s_qrd")
        s_kst = [sem(f"s_kst{j}") for j in range(2)]
        s_qst = [sem(f"s_qst{j}") for j in range(2)]
        s_mb1 = sem("s_mb1"); s_mb2 = sem("s_mb2"); s_mb3 = sem("s_mb3")
        s_mb4 = sem("s_mb4"); s_mb5 = sem("s_mb5"); s_mb6 = sem("s_mb6")
        s_mb7 = sem("s_mb7"); s_mb8 = sem("s_mb8")
        s_q0l = sem("s_q0l"); s_k0l = sem("s_k0l")
        s_s0w = [sem(f"s_s0w{j}") for j in range(2)]
        s_s0r = [sem(f"s_s0r{j}") for j in range(2)]
        s_sm = sem("s_sm")
        s_ff1 = sem("s_ff1"); s_ffe = sem("s_ffe")
        s_cs = sem("s_cs"); s_cv = sem("s_cv")
        s_wmn = sem("s_wmn"); s_bis = sem("s_bis"); s_th = sem("s_th")
        s_ein = sem("s_ein"); s_ee0 = sem("s_ee0"); s_eml = sem("s_eml")
        s_est = sem("s_est")
        s_ktl = sem("s_ktl")
        s_el = sem("s_el"); s_ql = sem("s_ql")
        s_qkd = sem("s_qkd")
        s_ex = [sem(f"s_ex{j}") for j in range(2)]
        s_pm = [sem(f"s_pm{j}") for j in range(2)]
        s_pv = [sem(f"s_pv{j}") for j in range(2)]
        s_ye = [sem(f"s_ye{j}") for j in range(2)]
        s_yn = [sem(f"s_yn{j}") for j in range(2)]
        s_ys = [sem(f"s_ys{j}") for j in range(4)]
        s_ytl = [sem(f"s_ytl{j}") for j in range(2)]
        s_pj = sem("s_pj"); s_bia = sem("s_bia")
        s_xol = [sem(f"s_xol{j}") for j in range(2)]
        s_sq2 = sem("s_sq2")
        s_rs2 = sem("s_rs2"); s_re2 = sem("s_re2")
        s_sq3 = sem("s_sq3")
        s_rs3 = sem("s_rs3"); s_ap3 = sem("s_ap3"); s_mx3 = sem("s_mx3")
        s_o2 = [sem(f"s_o2{j}") for j in range(2)]
        s_o3 = [sem(f"s_o3{j}") for j in range(2)]
        s_fx = sem("s_fx"); s_fxd = sem("s_fxd")

        vaug = e(nc.sbuf_tensor("vaug", [128, 32 * 1040], BF16))
        blk = e(nc.Block())

        def _stats(v, st, dst, src_t, base, s_cnt):
            v.reduce_sum(st[:, dst + 0:dst + 1], src_t[:], axis=AX.X)\
                .then_inc(s_cnt, 1)
            v.scalar_tensor_tensor(ldump[:, 8:8 + CM], src_t[:], 1.0, src_t[:],
                                   ALU.bypass, ALU.mult,
                                   accum_out=st[:, dst + 1:dst + 2])\
                .then_inc(s_cnt, 1)
            v.wait_ge(s_cnt, base + 2)
            v.tensor_scalar(st[:, dst + 2:dst + 3], st[:, dst + 0:dst + 1],
                            cc(cs, "inv_cm"), None, ALU.mult).then_inc(s_cnt, 1)
            v.wait_ge(s_cnt, base + 3)
            v.tensor_copy(ldump[:, 0:8], cs[:, 0:8])
            v.tensor_scalar(st[:, dst + 3:dst + 4], st[:, dst + 2:dst + 3],
                            st[:, dst + 2:dst + 3], cc(cs, "neg_one"),
                            ALU.mult, ALU.mult).then_inc(s_cnt, 1)
            v.wait_ge(s_cnt, base + 4)
            v.tensor_scalar(st[:, dst + 4:dst + 5], st[:, dst + 1:dst + 2],
                            cc(cs, "inv_cm"), None, ALU.mult).then_inc(s_cnt, 1)
            v.wait_ge(s_cnt, base + 5)
            v.tensor_copy(ldump[:, 0:8], cs[:, 0:8])
            v.tensor_tensor(st[:, dst + 5:dst + 6], st[:, dst + 4:dst + 5],
                            st[:, dst + 3:dst + 4], ALU.add).then_inc(s_cnt, 2)

        with ExitStack() as scX:
            eX = scX.enter_context
            x1T = [eX(nc.sbuf_tensor(f"x1T{c}", [128, T], BF16))
                   for c in range(8)]
            s_xtw = sem("s_xtw"); s_xtr = sem("s_xtr")

            sc0 = scX.enter_context(ExitStack())
            e0 = sc0.enter_context
            b24_s = e0(nc.sbuf_tensor("b24_s", [24, 128], F32))
            dq_f = [e0(nc.sbuf_tensor("dq_f0", [128, ID], F32))] * 2
            dq_sc = e0(nc.sbuf_tensor("dq_sc", [128, ID // 32], F32))
            dq_si = e0(nc.sbuf_tensor("dq_si", [128, ID // 32], F32))
            dq_i = e0(nc.sbuf_tensor("dq_i", [128, ID], I32))
            dq_d = e0(nc.sbuf_tensor("dq_d", [128, ID], F32))
            dq_ns = e0(nc.sbuf_tensor("dq_ns", [128, ID // 32], F32))
            dq_b = e0(nc.sbuf_tensor("dq_b", [128, ID], BF16))
            xt = [e0(nc.sbuf_tensor(f"xt{j}", [128, CM], F32)) for j in range(2)]
            x1t = [e0(nc.sbuf_tensor(f"x1t{j}", [128, CM], BF16)) for j in range(2)]
            lst = e0(nc.sbuf_tensor("lst", [128, 16], F32))
            px = e0(nc.psum_tensor("px", [128, 512], F32))
            pxT = [e0(nc.psum_tensor(f"pxT{j}", [128, 1024], BF16))
                   for j in range(2)]

            def sy01(s):
                for ap_, dst in ((constsD, cs), (onesbD, ones_s), (onesfD, onesf_s),
                                 (identfD, idf_s), (cumTD, cum_s), (lowincD, low_s),
                                 (ltstrD, lts_s), (ltincD, lti_s), (headselD, hsel_s),
                                 (iotaD, iota_s), (bvbD, bvb_s), (bprojbD, bpb_s),
                                 (identbD, idb_s), (battn24, b24_s)):
                    s.dma_start(out=dst[:], in_=ap_).then_inc(s_c0, 16)
                for i in range(7):
                    j = i % 2
                    if i >= 1:
                        s.wait_ge(s_dqr, 6 * i - 1)
                    if i < 3:
                        s.dma_start(out=dq_f[j][:, 0:CM],
                                    in_=wup_sl[i * 128:(i + 1) * 128, :])\
                            .then_inc(s_dql[j], 16)
                    elif i < 6:
                        s.dma_start(out=dq_f[j][:, 0:CM],
                                    in_=wgate_sl[(i - 3) * 128:(i - 2) * 128, :])\
                            .then_inc(s_dql[j], 16)
                    else:
                        s.dma_start(out=dq_f[j][:], in_=wdown_sl)\
                            .then_inc(s_dql[j], 16)
                for i in range(32):
                    j = i % 2
                    if i >= 2:
                        s.wait_ge(s_app, i - 1)
                    s.dma_start(out=xt[j][:], in_=xbD[i * 128:(i + 1) * 128, :])\
                        .then_inc(s_x0[j], 16)
            blk.sync(sy01)

            def pe01(t):
                t.wait_ge(s_c0, 16 * 14)
                t.transpose(px[:, 0:24], b24_s[:], idf_s[0:24, 0:24])\
                    .then_inc(s_bt, 1)
                for k in range(32):
                    jk = k % 2
                    t.wait_ge(s_app, k + 1)
                    if k >= 2:
                        t.wait_ge(s_xtr, k - 1)
                    mm = None
                    for c in range(8):
                        mm = t.transpose(pxT[jk][:, c * 128:(c + 1) * 128],
                                         x1t[jk][:, c * 128:(c + 1) * 128],
                                         idb_s[:])
                    mm.then_inc(s_xtw, 1)
            blk.tensor(pe01)

            def act01(a):
                a.wait_ge(s_bt, 1)
                a.copy(bT_s[:], px[:, 0:24]).then_inc(s_mb8, 1)
                for k in range(32):
                    kc = 8 * (k % 2)
                    if k >= 1:
                        kk = k - 1
                        a.wait_ge(s_xtw, kk + 1)
                        for c in range(8):
                            a.copy(x1T[c][:, kk * 128:(kk + 1) * 128],
                                   pxT[kk % 2][:, c * 128:(c + 1) * 128])
                        a.copy(ldump[:, 0:8], cs[:, 0:8]).then_inc(s_xtr, 1)
                    a.wait_ge(s_ssq, 7 * (k + 1))
                    a.activation(lst[:, kc + 6:kc + 7], lst[:, kc + 5:kc + 6],
                                 AF.Ln, bias=cc(cs, "ln_eps")).then_inc(s_rstd, 1)
                    a.wait_ge(s_rstd, 2 * k + 1)
                    a.activation(lst[:, kc + 7:kc + 8], lst[:, kc + 6:kc + 7],
                                 AF.Exp, scale=cc(cs, "neg_half")).then_inc(s_rstd, 1)
                a.wait_ge(s_xtw, 32)
                for c in range(8):
                    a.copy(x1T[c][:, 31 * 128:32 * 128],
                           pxT[1][:, c * 128:(c + 1) * 128])
                a.copy(ldump[:, 0:8], cs[:, 0:8]).then_inc(s_xtr, 1)
            blk.scalar(act01)

            def dve01(v):
                v.wait_ge(s_c0, 16 * 14)
                v.memset(zer_s[:], 0.0)
                for i in range(7):
                    j = i % 2
                    w = CM if i < 6 else ID
                    nwin = w // 32
                    v.wait_ge(s_dql[j], 16 * (i // 2 + 1))
                    if i >= 1:
                        v.wait_ge(s_dqb, i)
                    v.reduce_max(dq_sc[:, 0:nwin],
                                 dq_f[j][:, 0:w].rearrange("p (a b) -> p a b", b=32),
                                 axis=AX.X, apply_absolute_value=True)\
                        .then_inc(s_dqw, 1)
                    v.wait_ge(s_dqw, i + 1)
                    v.tensor_copy(ldump[:, 0:8], cs[:, 0:8])
                    v.tensor_scalar(dq_sc[:, 0:nwin], dq_sc[:, 0:nwin],
                                    cc(cs, "inv7"), cc(cs, "eps12"),
                                    ALU.mult, ALU.max).then_inc(s_dqc, 1)
                    v.wait_ge(s_dqc, i + 1)
                    v.tensor_copy(ldump[:, 0:8], cs[:, 0:8])
                    v.tensor_scalar(dq_ns[:, 0:nwin], dq_sc[:, 0:nwin], -1.0,
                                    None, ALU.mult)
                    v.reciprocal(dq_si[:, 0:nwin], dq_sc[:, 0:nwin])\
                        .then_inc(s_dqi, 1)
                    v.wait_ge(s_dqi, i + 1)
                    v.tensor_copy(ldump[:, 0:8], cs[:, 0:8])
                    si3 = dq_si[:, 0:nwin].rearrange("p (a o) -> p a o", o=1)\
                        .broadcast_to([128, nwin, 32])
                    v.tensor_tensor(dq_f[j][:, 0:w].rearrange("p (a b) -> p a b",
                                                              b=32),
                                    dq_f[j][:, 0:w].rearrange("p (a b) -> p a b",
                                                              b=32),
                                    si3, ALU.mult).then_inc(s_dqr, 1)
                    v.wait_ge(s_dqr, 6 * i + 1)
                    v.tensor_copy(ldump[:, 0:8], cs[:, 0:8])
                    v.tensor_scalar(dq_i[:, 0:w], dq_f[j][:, 0:w], 1.0, None,
                                    ALU.mult).then_inc(s_dqr, 1)
                    v.wait_ge(s_dqr, 6 * i + 2)
                    v.tensor_copy(ldump[:, 0:8], cs[:, 0:8])
                    v.scalar_tensor_tensor(dq_d[:, 0:w], dq_i[:, 0:w], -1.0,
                                           dq_f[j][:, 0:w], ALU.mult, ALU.add)\
                        .then_inc(s_dqr, 1)
                    v.wait_ge(s_dqr, 6 * i + 3)
                    v.tensor_copy(ldump[:, 0:8], cs[:, 0:8])
                    v.scalar_tensor_tensor(dq_f[j][:, 0:w], dq_d[:, 0:w], 0.5,
                                           dq_i[:, 0:w], ALU.is_ge, ALU.add)\
                        .then_inc(s_dqr, 1)
                    v.wait_ge(s_dqr, 6 * i + 4)
                    v.tensor_copy(ldump[:, 0:8], cs[:, 0:8])
                    v.scalar_tensor_tensor(dq_d[:, 0:w], dq_d[:, 0:w], -0.5,
                                           dq_f[j][:, 0:w], ALU.is_ge, ALU.add)\
                        .then_inc(s_dqr, 1)
                    v.wait_ge(s_dqr, 6 * i + 5)
                    if i >= 1:
                        v.wait_ge(s_dqs[0], 16 * i)
                    v.tensor_copy(ldump[:, 0:8], cs[:, 0:8])
                    sc3 = dq_sc[:, 0:nwin].rearrange("p (a o) -> p a o", o=1)\
                        .broadcast_to([128, nwin, 32])
                    nsc3 = dq_ns[:, 0:nwin].rearrange("p (a o) -> p a o", o=1)\
                        .broadcast_to([128, nwin, 32])
                    v.tensor_tensor(dq_d[:, 0:w].rearrange("p (a b) -> p a b",
                                                           b=32),
                                    dq_d[:, 0:w].rearrange("p (a b) -> p a b",
                                                           b=32),
                                    sc3, ALU.mult).then_inc(s_dqr, 1)
                    v.wait_ge(s_dqr, 6 * i + 6)
                    v.tensor_copy(ldump[:, 0:8], cs[:, 0:8])
                    v.tensor_tensor(dq_b[:, 0:w].rearrange("p (a b) -> p a b",
                                                             b=32),
                                    dq_d[:, 0:w].rearrange("p (a b) -> p a b",
                                                           b=32),
                                    nsc3, ALU.add).then_inc(s_dqb, 1)
                for i in range(33):
                    if i < 32:
                        tb = i
                        j = tb % 2
                        tc = 8 * j
                        v.wait_ge(s_x0[j], 16 * (tb // 2 + 1))
                        if tb >= 2:
                            v.wait_ge(s_app, tb - 1)
                        _stats(v, lst, tc, xt[j], 7 * tb, s_ssq)
                    if i >= 1:
                        k = i - 1
                        jk = k % 2
                        kc = 8 * jk
                        v.wait_ge(s_rstd, 2 * (k + 1))
                        if k >= 2:
                            v.wait_ge(s_xtw, k - 1)
                            if dbg:
                                v.wait_ge(s_l1s[jk], 16 * (k // 2))
                        v.tensor_scalar(x1t[jk][:], xt[jk][:], lst[:, kc + 2:kc + 3],
                                        lst[:, kc + 7:kc + 8], ALU.subtract,
                                        ALU.mult).then_inc(s_app, 1)
            blk.vector(dve01)

            def g01(g):
                for i in range(7):
                    g.wait_ge(s_dqb, i + 1)
                    if i < 3:
                        g.dma_start(out=wupdq_sl[i * 128:(i + 1) * 128, :],
                                    in_=dq_b[:, 0:CM]).then_inc(s_dqs[0], 16)
                    elif i < 6:
                        g.dma_start(out=wgatedq_sl[(i - 3) * 128:(i - 2) * 128, :],
                                    in_=dq_b[:, 0:CM]).then_inc(s_dqs[0], 16)
                    else:
                        g.dma_start(out=wdowndq_sl, in_=dq_b[:])\
                            .then_inc(s_dqs[0], 16)
                if dbg:
                    for k in range(32):
                        jk = k % 2
                        g.wait_ge(s_app, k + 1)
                        g.dma_start(out=x1d[k * 128:(k + 1) * 128, :],
                                    in_=x1t[jk][:]).then_inc(s_l1s[jk], 16)
            blk.gpsimd(g01)

            sc0.close()

            sc2 = scX.enter_context(ExitStack())
            if True:
                e2 = sc2.enter_context
                waT = [e2(nc.sbuf_tensor(f"waT{c}", [128, 3 * CM], BF16))
                       for c in range(8)]
                wfl = [e2(nc.sbuf_tensor(f"wfl{j}", [128, CM], F32))
                       for j in range(2)]
                kev = [e2(nc.sbuf_tensor(f"kev{j}", [128, 512], BF16))
                       for j in range(2)]
                sqb = [e2(nc.sbuf_tensor(f"sqb{j}", [128, 512], BF16))
                       for j in range(2)]
                qev = [e2(nc.sbuf_tensor(f"qev{j}", [128, 512], BF16))
                       for j in range(2)]
                pq = [e2(nc.psum_tensor(f"pq{j}", [128, 512], F32))
                      for j in range(2)]
                pxb = e2(nc.psum_tensor("pxb", [128, 512], F32))
                psq = [e2(nc.psum_tensor(f"psq{j}", [2, 512], F32))
                       for j in range(2)]

                TILES = ([("k", p, tc4) for p in range(NPAIR) for tc4 in range(8)]
                         + [("v", tb, hf) for tb in range(32) for hf in range(2)]
                         + [("q", p, tc4) for p in range(NPAIR) for tc4 in range(8)])

                def sy2(s):
                    s.wait_ge(s_dqs[0], 16 * 7)
                    s.wait_ge(s_xtr, 32)
                    for u in range(24):
                        c, part = u // 3, u % 3
                        j = u % 2
                        if u >= 2:
                            s.wait_ge(s_wc, u - 1)
                        s.dma_start(out=wfl[j][:],
                                    in_=watT[c * 128:(c + 1) * 128,
                                             part * CM:(part + 1) * CM])\
                            .then_inc(s_wl[j], 16)
                blk.sync(sy2)

                def pe2(t):
                    t.wait_ge(s_mb8, 1)
                    t.wait_ge(s_wc, 24)
                    t.wait_ge(s_xtr, 32)
                    pend_kq = []
                    def flush_psq(t):
                        tkq = pend_kq.pop(0)
                        t.wait_ge(s_mb3, tkq + 1)
                        if tkq >= 2:
                            if tkq - 2 < 64:
                                t.wait_ge(s_krd, tkq - 1)
                            else:
                                t.wait_ge(s_qrd, tkq - 65)
                        t.matmul(psq[tkq % 2][0:2, :], hsel_s[:],
                                 sqb[tkq % 2][:], start=True, stop=True)\
                            .then_inc(s_mb5, 1)
                    for k, tile in enumerate(TILES):
                        kind = tile[0]
                        bank = k % 2
                        if k >= 2:
                            t.wait_ge(s_kve[bank], k // 2)
                        if kind in ("k", "q"):
                            _, p, tc4 = tile
                            off = CM if kind == "k" else 0
                            tkq = p * 8 + tc4 + (0 if kind == "k" else 64)
                            mm = None
                            for c in range(8):
                                mm = t.matmul(
                                    pq[bank][:],
                                    waT[c][:, off + 128 * p:off + 128 * p + 128],
                                    x1T[c][:, tc4 * 512:(tc4 + 1) * 512],
                                    start=(c == 0), stop=(c == 7))
                            mm.then_inc(s_kv, 1)
                            pend_kq.append(tkq)
                            while len(pend_kq) > 1:
                                flush_psq(t)
                        else:
                            _, tb, hf = tile
                            mm = None
                            for c in range(8):
                                mm = t.matmul(
                                    pq[bank][:],
                                    x1T[c][:, tb * 128:(tb + 1) * 128],
                                    waT[c][:, 2 * CM + 512 * hf:2 * CM + 512 * hf + 512],
                                    start=(c == 0), stop=(c == 7))
                            mm.then_inc(s_kv, 1)
                            while pend_kq:
                                flush_psq(t)
                    while pend_kq:
                        flush_psq(t)
                    t.wait_ge(s_mb2, 2)
                    t.transpose(pxb[:, 64:192], sc128[:], idf_s[:])\
                        .then_inc(s_mb6, 1)
                    t.wait_ge(s_mb7, 2)
                    t.matmul(pxb[:, 256:257], onesf_s[0:1, 0:128], mrow[0:1, 5:6],
                             start=True, stop=True).then_inc(s_mb8, 1)
                blk.tensor(pe2)

                def act2(a):
                    for k, tile in enumerate(TILES):
                        kind = tile[0]
                        bank = k % 2
                        if kind == "v":
                            continue
                        a.wait_ge(s_kv, k + 1)
                        if kind == "k":
                            _, p, tc4 = tile
                            ki = p * 8 + tc4
                            if ki >= 2:
                                a.wait_ge(s_kst[ki % 2], 16 * (ki // 2))
                                a.wait_ge(s_mb3, ki - 1)
                            a.activation(kev[ki % 2][:], pq[bank][:], AF.Identity,
                                         bias=bT_s[:, 8 + p:9 + p])\
                                .then_inc(s_kve[bank], 1)
                        else:
                            _, p, tc4 = tile
                            qi_ = p * 8 + tc4
                            if qi_ >= 2:
                                a.wait_ge(s_qst[qi_ % 2], 16 * (qi_ // 2))
                                a.wait_ge(s_mb3, 64 + qi_ - 1)
                            a.activation(qev[qi_ % 2][:], pq[bank][:], AF.Identity,
                                         bias=bT_s[:, p:p + 1])\
                                .then_inc(s_kve[bank], 1)
                    a.wait_ge(s_mb4, 1)
                    a.activation(mrow[0:1, 4:5], mrow[0:1, 3:4], AF.Sqrt)\
                        .then_inc(s_mb7, 1)
                    a.wait_ge(s_mb8, 2)
                    a.copy(ebias[:], pxb[:, 256:257]).then_inc(s_mb8, 1)
                blk.scalar(act2)

                def dve2_(v):
                    for u in range(24):
                        c, part = u // 3, u % 3
                        j = u % 2
                        v.wait_ge(s_wl[j], 16 * (u // 2 + 1))
                        v.tensor_copy(waT[c][:, part * CM:(part + 1) * CM],
                                      wfl[j][:]).then_inc(s_wc, 1)
                    for k, tile in enumerate(TILES):
                        kind = tile[0]
                        bank = k % 2
                        if kind == "v":
                            _, tb, hf = tile
                            v.wait_ge(s_kv, k + 1)
                            base = tb * 1040 + hf * 8 * 65
                            dst = vaug[:, base:base + 8 * 65]\
                                .rearrange("p (hh d) -> p hh d", d=65)[:, :, 0:64]
                            v.tensor_copy(
                                dst, pq[bank][:].rearrange("p (hh d) -> p hh d",
                                                           d=64))\
                                .then_inc(s_kve[bank], 1)
                        else:
                            _, p, tc4 = tile
                            ev_ = kev if kind == "k" else qev
                            ti_ = p * 8 + tc4
                            tkq = ti_ + (0 if kind == "k" else 64)
                            v.wait_ge(s_kve[bank], k // 2 + 1)
                            if tkq >= 2:
                                v.wait_ge(s_mb5, tkq - 1)
                            v.tensor_tensor(sqb[tkq % 2][:], ev_[ti_ % 2][:],
                                            ev_[ti_ % 2][:], ALU.mult)\
                                .then_inc(s_mb3, 1)
                            v.wait_ge(s_mb5, tkq + 1)
                            v.reduce_max(mbc[0:2, tkq:tkq + 1],
                                         psq[tkq % 2][0:2, :], axis=AX.X)\
                                .then_inc(s_krd if kind == "k" else s_qrd, 1)
                    for tb in range(32):
                        dst = vaug[:, tb * 1040:(tb + 1) * 1040]\
                            .rearrange("p (hh d) -> p hh d", d=65)[:, :, 64:65]
                        v.memset(dst, 1.0)
                    v.wait_ge(s_krd, 64)
                    v.wait_ge(s_qrd, 64)
                    v.memset(sc128[:], 0.0)
                    v.tensor_copy(ldump[:, 0:8], cs[:, 0:8])
                    v.reduce_max(mbc[0:2, 128:136],
                                 mbc[0:2, 0:64].rearrange("p (a b) -> p a b",
                                                          b=8),
                                 axis=AX.X).then_inc(s_mb1, 1)
                    v.reduce_max(mbc[0:2, 136:144],
                                 mbc[0:2, 64:128].rearrange("p (a b) -> p a b",
                                                            b=8),
                                 axis=AX.X).then_inc(s_mb1, 1)
                    v.wait_ge(s_mb1, 2)
                    v.tensor_copy(ldump[:, 0:8], cs[:, 0:8])
                    v.tensor_tensor(sc128[0:2, 8:16], mbc[0:2, 128:136],
                                    mbc[0:2, 136:144], ALU.mult)\
                        .then_inc(s_mb2, 1)
                    v.wait_ge(s_mb2, 1)
                    v.tensor_copy(ldump[:, 0:8], cs[:, 0:8])
                    v.reduce_max(sc128[0:2, 0:1], sc128[0:2, 8:16], axis=AX.X)\
                        .then_inc(s_mb2, 1)
                    v.wait_ge(s_mb6, 1)
                    v.tensor_copy(ldump[:, 0:8], cs[:, 0:8])
                    v.reduce_max(mrow[0:1, 3:4], pxb[0:1, 64:66], axis=AX.X)\
                        .then_inc(s_mb4, 1)
                    v.wait_ge(s_mb7, 1)
                    v.tensor_copy(ldump[:, 0:8], cs[:, 0:8])
                    v.tensor_scalar(mrow[0:1, 5:6], mrow[0:1, 4:5],
                                    cc(cs, "att_scale")[0:1, :],
                                    cc(cs, "neg_one")[0:1, :],
                                    ALU.mult, ALU.mult).then_inc(s_mb7, 1)
                blk.vector(dve2_)

                def g2_(g):
                    for k, tile in enumerate(TILES):
                        kind = tile[0]
                        if kind == "k":
                            _, p, tc4 = tile
                            ki = p * 8 + tc4
                            g.wait_ge(s_mb3, ki + 1)
                            g.dma_start(out=kTd[p * 128:(p + 1) * 128,
                                                tc4 * 512:(tc4 + 1) * 512],
                                        in_=kev[ki % 2][:]).then_inc(s_kst[ki % 2], 16)
                        elif kind == "q":
                            _, p, tc4 = tile
                            qi_ = p * 8 + tc4
                            g.wait_ge(s_mb3, 64 + qi_ + 1)
                            g.dma_start(out=qTd[p * 128:(p + 1) * 128,
                                                tc4 * 512:(tc4 + 1) * 512],
                                        in_=qev[qi_ % 2][:]).then_inc(s_qst[qi_ % 2], 16)
                    if dbg:
                        for j in range(2):
                            g.wait_ge(s_kst[j], 16 * 32)
                        g.dma_start(out=dbg_kT, in_=kTd).then_inc(s_fxd, 16)
                        g.wait_ge(s_mb7, 2)
                        g.dma_start(out=dbg_m, in_=mrow[0:1, 4:5]).then_inc(s_fxd, 16)
                blk.gpsimd(g2_)

            sc2.close()
            scX.close()

            with ExitStack() as sc3:
                e3 = sc3.enter_context
                q0s = e3(nc.sbuf_tensor("q0s", [64, T], BF16))
                k0s = e3(nc.sbuf_tensor("k0s", [64, T], BF16))
                sSc = [e3(nc.sbuf_tensor(f"sSc{j}", [128, 512], BF16))
                       for j in range(2)]
                ffsb = e3(nc.sbuf_tensor("ffsb", [128, T], F32))
                carv = e3(nc.sbuf_tensor("carv", [1, T], F32))
                carvb = e3(nc.sbuf_tensor("carvb", [1, T], BF16))
                ffw = e3(nc.sbuf_tensor("ffw", [128, 128], F32))
                bis = e3(nc.sbuf_tensor("bis", [128, 12], F32))
                eind = e3(nc.sbuf_tensor("eind", [128, T], BF16))
                ee0 = e3(nc.sbuf_tensor("ee0", [128, T], BF16))
                eout = e3(nc.sbuf_tensor("eout", [128, T], BF16))
                dgt = e3(nc.sbuf_tensor("dgt", [128, 128], BF16))
                ps0 = [e3(nc.psum_tensor(f"ps0_{j}", [128, 512], F32))
                       for j in range(2)]
                pff = [e3(nc.psum_tensor(f"pff{j}", [128, 512], F32))
                       for j in range(2)]
                pcs = [e3(nc.psum_tensor(f"pcs{j}", [1, 512], F32))
                       for j in range(2)]

                SCH = []
                for I in range(32):
                    w = 128 * (I + 1)
                    for jc in range((w + 511) // 512):
                        SCH.append((I, jc, min(512, w - 512 * jc)))
                E_before = {}
                for idx, (I, jc, nj) in enumerate(SCH):
                    if I not in E_before:
                        E_before[I] = idx
                done_at = {}
                for idx, (I, jc, nj) in enumerate(SCH):
                    if I in mine:
                        done_at[I] = idx
                ev = []
                nS0 = [0, 0]; nSR = [0, 0]; nSM = 0
                nF1 = 0; nFE = 0
                nTH = 0; nEST = 0; nEE = 0; nEM = 0; nBIS = 0
                seen_first_evac = set()
                for idx, (I, jc, nj) in enumerate(SCH):
                    slot = idx % 2
                    d = {"slot": slot, "I": I, "jc": jc, "nj": nj}
                    nS0[slot] += 1; d["s0w"] = nS0[slot]
                    d["m0"] = (jc == 0)
                    d["md"] = (jc == I // 4)
                    d["mask"] = d["m0"] or d["md"]
                    d["sm_before"] = nSM
                    nSM += int(d["m0"]) + int(d["md"])
                    d["sm"] = nSM
                    nSR[slot] += 1; d["s0r"] = nSR[slot]
                    if I in mine:
                        d["fslot"] = nF1 % 2
                        nF1 += 1; d["f1"] = nF1
                        nFE += 1; d["fe"] = nFE
                        d["first_evac"] = I not in seen_first_evac
                        seen_first_evac.add(I)
                    d["th_before"] = nTH
                    d["est_before"] = nEST
                    d["ee_before"] = nEE
                    d["em_before"] = nEM
                    d["bis_before"] = nBIS
                    d["done"] = (I in mine) and (idx == done_at[I])
                    if d["done"]:
                        nTH += 1
                        nEE += 1
                        nEM += 4
                        if I > 5:
                            nBIS += 1
                        W = 128 * (I + 1)
                        Wpad = JCs[my_pos[I] // 4] * 128
                        d["n_st"] = 1 + (Wpad - W + 255) // 256
                        nEST += d["n_st"]
                    d["est_after"] = nEST
                    d["th_after"] = nTH
                    d["ee_after"] = nEE
                    d["em_after"] = nEM
                    d["bis_after"] = nBIS
                    ev.append(d)
                NSCH = len(SCH)
                EST_TOTAL = nEST
                FE_TOTAL = nFE
                F1_TOTAL = nF1
                SR_TOT = (nSR[0], nSR[1])

                def sy3(s):
                    for j in range(2):
                        s.wait_ge(s_qst[j], 16 * 32)
                        s.wait_ge(s_kst[j], 16 * 32)
                    s.dma_start(out=q0s[:], in_=qTd[0:64, :]).then_inc(s_q0l, 16)
                    s.dma_start(out=k0s[:], in_=kTd[0:64, :]).then_inc(s_k0l, 16)
                blk.sync(sy3)

                def pe3(t):
                    t.wait_ge(s_kve[0], 96)
                    t.wait_ge(s_kve[1], 96)
                    t.wait_ge(s_krd, 64)
                    t.wait_ge(s_qrd, 64)
                    t.wait_ge(s_mb8, 3)
                    t.wait_ge(s_q0l, 16)
                    t.wait_ge(s_k0l, 16)
                    t.wait_ge(s_cv, 1)
                    for ti in range(NSCH + 1):
                        if ti < NSCH:
                            idx = ti
                            I, jc, nj = SCH[idx]
                            d = ev[idx]
                            slot = d["slot"]
                            if idx >= 2:
                                t.wait_ge(s_s0r[slot], ev[idx - 2]["s0r"])
                            t.matmul(ps0[slot][:, 0:nj],
                                     q0s[:, I * 128:(I + 1) * 128],
                                     k0s[:, jc * 512:jc * 512 + nj],
                                     start=True, stop=True)\
                                .then_inc(s_s0w[slot], 1)
                        if ti >= 1:
                            idx = ti - 1
                            I, jc, nj = SCH[idx]
                            d = ev[idx]
                            slot = d["slot"]
                            t.wait_ge(s_s0r[slot], d["s0r"])
                            if d["mask"]:
                                t.wait_ge(s_sm, d["sm"])
                            if I in mine:
                                if d["f1"] >= 3:
                                    t.wait_ge(s_ffe, d["f1"] - 2)
                                t.wait_ge(s_cv, 1 + 2 * E_before[I])
                                t.matmul(pff[d["fslot"]][:, 0:nj], cum_s[:],
                                         sSc[slot][:, 0:nj],
                                         start=True, stop=False)
                                t.matmul(pff[d["fslot"]][:, 0:nj],
                                         ones_s[0:1, 0:128],
                                         carvb[0:1, jc * 512:jc * 512 + nj],
                                         start=False, stop=True)\
                                    .then_inc(s_ff1, 1)
                            if idx >= 2:
                                t.wait_ge(s_cv, 1 + 2 * (idx - 1))
                            t.matmul(pcs[idx % 2][0:1, 0:nj], ones_s[:, 0:1],
                                     sSc[slot][:, 0:nj],
                                     start=True, stop=True).then_inc(s_cs, 1)
                blk.tensor(pe3)

                def act3(a):
                    for idx, (I, jc, nj) in enumerate(SCH):
                        d = ev[idx]
                        slot = d["slot"]
                        a.wait_ge(s_s0w[slot], d["s0w"])
                        if idx >= 2:
                            a.wait_ge(s_cs, idx - 1)
                        a.activation(sSc[slot][:, 0:nj], ps0[slot][:, 0:nj],
                                     AF.Relu, scale=cc(cs, "att_scale"))\
                            .then_inc(s_s0r[slot], 1)
                        if d["done"]:
                            W = 128 * (I + 1)
                            a.wait_ge(s_th, d["th_after"])
                            if d["ee_before"] >= 1:
                                a.wait_ge(s_eml, d["em_before"])
                            a.activation(ee0[:, 0:W], ffsb[:, 0:W], AF.Exp,
                                         scale=cc(cs, "neg_one"))\
                                .then_inc(s_ee0, 1)
                blk.scalar(act3)

                def dve3_(v):
                    v.memset(carv[:], 0.0)
                    v.memset(carvb[:], 0.0).then_inc(s_cv, 1)
                    for idx, (I, jc, nj) in enumerate(SCH):
                        d = ev[idx]
                        slot = d["slot"]
                        if d["mask"]:
                            v.wait_ge(s_s0r[slot], d["s0r"])
                            n0 = d["sm_before"]
                            if d["m0"]:
                                v.memset(sSc[slot][:, 0:1], 0.0)\
                                    .then_inc(s_sm, 1)
                                n0 += 1
                            if d["md"]:
                                if d["m0"]:
                                    v.wait_ge(s_sm, n0)
                                off = I * 128 - jc * 512
                                v.tensor_tensor(sSc[slot][:, off:off + 128],
                                                sSc[slot][:, off:off + 128],
                                                lts_s[:], ALU.mult)\
                                    .then_inc(s_sm, 1)
                        if I in mine:
                            v.wait_ge(s_ff1, d["f1"])
                            if d.get("first_evac"):
                                v.wait_ge(s_ee0, d["ee_before"])
                                v.wait_ge(s_ein, d["bis_before"])
                            v.tensor_copy(ffsb[:, jc * 512:jc * 512 + nj],
                                          pff[d["fslot"]][:, 0:nj])\
                                .then_inc(s_ffe, 1)
                        v.wait_ge(s_cs, idx + 1)
                        v.tensor_copy(ldump[:, 0:8], cs[:, 0:8])
                        v.tensor_tensor(carv[0:1, jc * 512:jc * 512 + nj],
                                        carv[0:1, jc * 512:jc * 512 + nj],
                                        pcs[idx % 2][0:1, 0:nj], ALU.add)\
                            .then_inc(s_cv, 1)
                        v.wait_ge(s_cv, 2 * idx + 2)
                        v.tensor_copy(carvb[0:1, jc * 512:jc * 512 + nj],
                                      carv[0:1, jc * 512:jc * 512 + nj])\
                            .then_inc(s_cv, 1)
                        if d["done"]:
                            W = 128 * (I + 1)
                            nw = W // 32
                            m = my_pos[I]
                            v.wait_ge(s_ffe, d["fe"])
                            v.tensor_copy(ldump[:, 0:8], cs[:, 0:8])
                            wm = 3 * d["th_before"]
                            v.tensor_scalar(ee0[:, 0:W], ffsb[:, 0:W],
                                            cc(cs, "neg_one"), None, ALU.mult)\
                                .then_inc(s_wmn, 1)
                            v.wait_ge(s_wmn, wm + 1)
                            v.tensor_copy(ldump[:, 0:8], cs[:, 0:8])
                            v.reduce_max(ffw[:, 0:nw],
                                         ee0[:, 0:W].rearrange(
                                             "p (a b) -> p a b", b=32),
                                         axis=AX.X).then_inc(s_wmn, 1)
                            v.wait_ge(s_wmn, wm + 2)
                            v.tensor_copy(ldump[:, 0:8], cs[:, 0:8])
                            v.tensor_scalar(ffw[:, 0:nw], ffw[:, 0:nw],
                                            cc(cs, "neg_one"), None, ALU.mult)\
                                .then_inc(s_wmn, 1)
                            v.wait_ge(s_wmn, wm + 3)
                            if I > 5:
                                bb = 50 * d["bis_before"]
                                v.memset(bis[:, 0:1], 0.0)
                                v.tensor_copy(bis[:, 1:2], cc(cs, "c1024"))\
                                    .then_inc(s_bis, 1)
                                v.wait_ge(s_bis, bb + 1)
                                for it in range(8):
                                    ib = bb + 1 + 6 * it
                                    if it >= 1:
                                        v.wait_ge(s_bis, ib)
                                    v.tensor_tensor(bis[:, 2:3], bis[:, 0:1],
                                                    bis[:, 1:2], ALU.add)\
                                        .then_inc(s_bis, 1)
                                    v.wait_ge(s_bis, ib + 1)
                                    v.tensor_scalar(bis[:, 2:3], bis[:, 2:3],
                                                    cc(cs, "half"), None,
                                                    ALU.mult).then_inc(s_bis, 1)
                                    v.wait_ge(s_bis, ib + 2)
                                    v.tensor_tensor(bis[:, 3:4], bis[:, 2:3],
                                                    bis[:, 0:1], ALU.subtract)
                                    v.tensor_tensor(bis[:, 4:5], bis[:, 2:3],
                                                    bis[:, 1:2], ALU.subtract)
                                    v.tensor_scalar(ldump[:, 8:8 + nw],
                                                    ffw[:, 0:nw], bis[:, 2:3],
                                                    0.0, ALU.is_lt, ALU.add,
                                                    accum_out=bis[:, 5:6])\
                                        .then_inc(s_bis, 1)
                                    v.wait_ge(s_bis, ib + 3)
                                    v.tensor_copy(ldump[:, 0:8], cs[:, 0:8])
                                    v.tensor_scalar(bis[:, 6:7], bis[:, 5:6],
                                                    cc(cs, "win_budget"), None,
                                                    ALU.is_le).then_inc(s_bis, 1)
                                    v.wait_ge(s_bis, ib + 4)
                                    v.tensor_scalar(bis[:, 7:8], bis[:, 6:7],
                                                    cc(cs, "neg_one"), 1.0,
                                                    ALU.mult, ALU.add)\
                                        .then_inc(s_bis, 1)
                                    v.wait_ge(s_bis, ib + 5)
                                    v.scalar_tensor_tensor(bis[:, 0:1],
                                                           bis[:, 3:4],
                                                           bis[:, 6:7],
                                                           bis[:, 0:1],
                                                           ALU.mult, ALU.add)
                                    v.scalar_tensor_tensor(bis[:, 1:2],
                                                           bis[:, 4:5],
                                                           bis[:, 7:8],
                                                           bis[:, 1:2],
                                                           ALU.mult, ALU.add)\
                                        .then_inc(s_bis, 1)
                                v.wait_ge(s_bis, bb + 49)
                                v.tensor_scalar(bis[:, 8:9], iota_s[:, I:I + 1],
                                                cc(cs, "c817"), None, ALU.is_le)\
                                    .then_inc(s_bis, 1)
                                v.wait_ge(s_bis, bb + 50)
                                v.scalar_tensor_tensor(bis[:, 9:10], bis[:, 8:9],
                                                       cc(cs, "big"),
                                                       bis[:, 0:1],
                                                       ALU.mult, ALU.add)\
                                    .then_inc(s_th, 1)
                                v.wait_ge(s_th, d["th_before"] + 1)
                                v.tensor_copy(ldump[:, 0:8], cs[:, 0:8])
                                v.tensor_scalar(eind[:, 0:W], ffsb[:, 0:W],
                                                bis[:, 9:10], None, ALU.is_lt)\
                                    .then_inc(s_ein, 1)
                            else:
                                v.tensor_copy(ldump[:, 0:8], cs[:, 0:8])\
                                    .then_inc(s_th, 1)
                            v.wait_ge(s_ee0, d["ee_before"] + 1)
                            v.wait_ge(s_est, 16 * d["est_before"])
                            if I > 5:
                                v.wait_ge(s_ein, d["bis_before"] + 1)
                                v.tensor_tensor(eout[:, 0:W], ee0[:, 0:W],
                                                eind[:, 0:W], ALU.mult)\
                                    .then_inc(s_eml, 1)
                            else:
                                v.tensor_copy(eout[:, 0:W], ee0[:, 0:W])\
                                    .then_inc(s_eml, 1)
                            v.wait_ge(s_eml, d["em_before"] + 1)
                            v.tensor_copy(ldump[:, 0:8], cs[:, 0:8])
                            v.tensor_tensor(dgt[:],
                                            eout[:, I * 128:(I + 1) * 128],
                                            lti_s[:], ALU.mult)\
                                .then_inc(s_eml, 1)
                            v.wait_ge(s_eml, d["em_before"] + 2)
                            v.tensor_copy(ldump[:, 0:8], cs[:, 0:8])
                            v.tensor_copy(eout[:, I * 128:(I + 1) * 128],
                                          dgt[:]).then_inc(s_eml, 2)
                blk.vector(dve3_)

                def g3(g):
                    for idx, (I, jc, nj) in enumerate(SCH):
                        d = ev[idx]
                        if d["done"]:
                            W = 128 * (I + 1)
                            m = my_pos[I]
                            Wpad = JCs[m // 4] * 128
                            g.wait_ge(s_eml, d["em_before"] + 4)
                            g.dma_start(out=Ed[m * 128:(m + 1) * 128, 0:W],
                                        in_=eout[:, 0:W]).then_inc(s_est, 16)
                            o = W
                            while o < Wpad:
                                wz = min(256, Wpad - o)
                                g.dma_start(out=Ed[m * 128:(m + 1) * 128,
                                                   o:o + wz],
                                            in_=zer_s[:, 0:wz])\
                                    .then_inc(s_est, 16)
                                o += wz
                blk.gpsimd(g3)

            with ExitStack() as sc4:
                e4 = sc4.enter_context
                kTb = [e4(nc.sbuf_tensor(f"kTb{p}", [128, T], BF16))
                       for p in range(NPAIR)]
                ebuf = e4(nc.sbuf_tensor("ebuf", [128, 32 * 512], BF16))
                qbuf = [e4(nc.sbuf_tensor(f"qbuf{p}", [128, 512], BF16))
                        for p in range(NPAIR)]
                p2 = [e4(nc.sbuf_tensor(f"p2_{j}", [128, 1024], BF16))
                      for j in range(2)]
                pmul = [e4(nc.sbuf_tensor(f"pmul{j}", [128, 1024], BF16))
                        for j in range(2)]
                ysb = [e4(nc.sbuf_tensor(f"ysb{j}", [128, 520], F32))
                       for j in range(2)]
                rz = [e4(nc.sbuf_tensor(f"rz{j}", [128, 8], F32)) for j in range(2)]
                ybf = [e4(nc.sbuf_tensor(f"ybf{s}", [128, CM], BF16))
                       for s in range(4)]
                pqk = [e4(nc.psum_tensor(f"pqk{j}", [128, 1024], F32))
                       for j in range(2)]
                yb = [e4(nc.psum_tensor(f"yb{j}", [128, 512], F32))
                      for j in range(4)]
                s_pvt = [sem(f"s_pvt{j}") for j in range(2)]

                QKT = []
                for ic in range(4):
                    for p in range(NPAIR):
                        for jc in range(JCs[ic]):
                            QKT.append((ic, p, jc))
                NQK = len(QKT)
                EL_TOT, QL_TOT = [], []
                elc = qlc = 0
                for ic in range(4):
                    elc += JCs[ic]
                    qlc += NPAIR * 4
                    EL_TOT.append(elc)
                    QL_TOT.append(qlc)
                PM_CNT = []
                PM_TOT = [[0] * 4, [0] * 4]
                pmc = [0, 0]
                GATE = []
                pv_c = [0, 0]
                pvt_c = [0, 0]
                for i, (ic, p, jc) in enumerate(QKT):
                    pmc[i % 2] += 1
                    PM_CNT.append(pmc[i % 2])
                    PM_TOT[0][ic] = pmc[0]
                    PM_TOT[1][ic] = pmc[1]
                    if jc == JCs[ic] - 1:
                        pv_c[p % 2] += 1
                        GATE.append(("pv", p % 2, pv_c[p % 2]))
                    else:
                        pvt_c[i % 2] += 1
                        GATE.append(("pvt", i % 2, pvt_c[i % 2]))

                def sy4(s):
                    s.wait_ge(s_est, 16 * EST_TOTAL)
                    s.wait_ge(s_cv, 1 + 2 * NSCH)
                    s.wait_ge(s_cs, NSCH)
                    s.wait_ge(s_s0r[0], SR_TOT[0])
                    s.wait_ge(s_s0r[1], SR_TOT[1])
                    for j in range(2):
                        s.wait_ge(s_kst[j], 16 * 32)
                    for p in range(NPAIR):
                        s.dma_start(out=kTb[p][:],
                                    in_=kTd[p * 128:(p + 1) * 128, :])\
                            .then_inc(s_ktl, 16)
                    for ic in range(4):
                        if ic == 0:
                            s.wait_ge(s_est, 16 * EST_TOTAL)
                        else:
                            s.wait_ge(s_pm[0], PM_TOT[0][ic - 1])
                            s.wait_ge(s_pm[1], PM_TOT[1][ic - 1])
                        for jc in range(JCs[ic]):
                            s.dma_start(out=ebuf[:, jc * 512:(jc + 1) * 512],
                                        in_=Ed[ic * 512:(ic + 1) * 512,
                                               jc * 128:(jc + 1) * 128],
                                        transpose=True).then_inc(s_el, 16)
                        for p in range(NPAIR):
                            for z, b_ in enumerate(icq[ic]):
                                s.dma_start(
                                    out=qbuf[p][:, z * 128:(z + 1) * 128],
                                    in_=qTd[p * 128:(p + 1) * 128,
                                            b_ * 128:(b_ + 1) * 128])\
                                    .then_inc(s_ql, 16)
                blk.sync(sy4)

                def pe4(t):
                    t.wait_ge(s_ffe, FE_TOTAL)
                    t.wait_ge(s_cv, 1 + 2 * NSCH)
                    t.wait_ge(s_s0r[0], SR_TOT[0])
                    t.wait_ge(s_s0r[1], SR_TOT[1])
                    t.wait_ge(s_ktl, 16 * NPAIR)
                    for ti in range(NQK + 1):
                        if ti < NQK:
                            i = ti
                            ic, p, jc = QKT[i]
                            bank = i % 2
                            if jc == 0:
                                t.wait_ge(s_el, 16 * EL_TOT[ic])
                                t.wait_ge(s_ql, 16 * QL_TOT[ic])
                            if i >= 2:
                                t.wait_ge(s_ex[bank], i // 2)
                            t.matmul(pqk[bank][:, 0:512],
                                     kTb[p][0:64, jc * 128:(jc + 1) * 128],
                                     qbuf[p][0:64, :], start=True, stop=True,
                                     tile_position=(0, 0))
                            t.matmul(pqk[bank][:, 512:1024],
                                     kTb[p][64:128, jc * 128:(jc + 1) * 128],
                                     qbuf[p][64:128, :], start=True, stop=True,
                                     tile_position=(64, 0)).then_inc(s_qkd, 1)
                        if ti >= 1:
                            i = ti - 1
                            ic, p, jc = QKT[i]
                            bank = i % 2
                            par = p % 2
                            gp = ic * NPAIR + p
                            t.wait_ge(s_pm[bank], PM_CNT[i])
                            if jc == 0 and gp >= 2:
                                t.wait_ge(s_ye[par], gp // 2)
                            last_tile = (jc == JCs[ic] - 1)
                            for hh in range(2):
                                bk = par * 2 + hh
                                mm = None
                                for sub in range(4):
                                    mm = t.matmul(
                                        yb[bk][:, 65 * sub:65 * sub + 65],
                                        pmul[bank][:, hh * 512 + sub * 128:
                                                   hh * 512 + (sub + 1) * 128],
                                        vaug[:, jc * 1040 + (2 * p + hh) * 65:
                                             jc * 1040 + (2 * p + hh) * 65 + 65],
                                        start=(jc == 0 and sub == 0),
                                        stop=(last_tile and sub == 3))
                                if hh == 1:
                                    if last_tile:
                                        mm.then_inc(s_pv[par], 1)
                                    else:
                                        mm.then_inc(s_pvt[bank], 1)
                blk.tensor(pe4)

                def act4(a):
                    a.wait_ge(s_mb8, 3)
                    for i, (ic, p, jc) in enumerate(QKT):
                        bank = i % 2
                        par = p % 2
                        gp = ic * NPAIR + p
                        a.wait_ge(s_qkd, i + 1)
                        if i >= 2:
                            a.wait_ge(s_pm[bank], PM_CNT[i] - 1)
                        a.activation(p2[bank][:], pqk[bank][:], AF.Exp,
                                     bias=ebias[:, 0:1],
                                     scale=cc(cs, "att_scale"))\
                            .then_inc(s_ex[bank], 1)
                        if jc == JCs[ic] - 1:
                            a.wait_ge(s_pv[par], gp // 2 + 1)
                            if gp >= 2:
                                a.wait_ge(s_yn[par], 2 * (gp // 2))
                            a.copy(ysb[par][:, 0:260], yb[par * 2][:, 0:260])
                            a.copy(ysb[par][:, 260:520],
                                   yb[par * 2 + 1][:, 0:260]).then_inc(s_ye[par], 1)
                blk.scalar(act4)

                def dve4(v):
                    for i, (ic, p, jc) in enumerate(QKT):
                        bank = i % 2
                        par = p % 2
                        gp = ic * NPAIR + p
                        v.wait_ge(s_ex[bank], i // 2 + 1)
                        if i >= 2:
                            gk, gj, gc = GATE[i - 2]
                            v.wait_ge(s_pv[gj] if gk == "pv" else s_pvt[gj], gc)
                        v.tensor_tensor(pmul[bank][:, 0:512],
                                        p2[bank][:, 0:512],
                                        ebuf[:, jc * 512:(jc + 1) * 512],
                                        ALU.mult)
                        v.tensor_tensor(pmul[bank][:, 512:1024],
                                        p2[bank][:, 512:1024],
                                        ebuf[:, jc * 512:(jc + 1) * 512],
                                        ALU.mult).then_inc(s_pm[bank], 1)
                        if jc == JCs[ic] - 1:
                            v.wait_ge(s_ye[par], gp // 2 + 1)
                            if p == 0 and ic >= 1:
                                for sbi in range(4):
                                    v.wait_ge(s_ys[sbi], 16 * ic)
                            zv = ysb[par][:, 0:520].rearrange(
                                "p (hh d) -> p hh d", d=65)[:, :, 64:65]
                            v.reciprocal(rz[par][:, 0:8].rearrange(
                                "p (a o) -> p a o", o=1), zv)
                            v.tensor_copy(ldump[:, 0:8], cs[:, 0:8])
                            v.tensor_copy(ldump[:, 8:16], cs[:, 0:8])\
                                .then_inc(s_yn[par], 1)
                            v.wait_ge(s_yn[par], 2 * (gp // 2) + 1)
                            for hh in range(2):
                                head = 2 * p + hh
                                for sbi in range(4):
                                    ins_ = v.scalar_tensor_tensor(
                                        ybf[sbi][:, head * 64:(head + 1) * 64],
                                        ysb[par][:, hh * 260 + sbi * 65:
                                                 hh * 260 + sbi * 65 + 64],
                                        rz[par][:, hh * 4 + sbi:hh * 4 + sbi + 1],
                                        bvb_s[:, head * 64:(head + 1) * 64],
                                        ALU.mult, ALU.add)
                            ins_.then_inc(s_yn[par], 1)
                blk.vector(dve4)

                def g4(g):
                    for ic in range(4):
                        g.wait_ge(s_yn[0], 8 * (ic + 1))
                        g.wait_ge(s_yn[1], 8 * (ic + 1))
                        for sbi in range(4):
                            g.dma_start(
                                out=ydD[(ic * 4 + sbi) * 128:
                                        (ic * 4 + sbi + 1) * 128, :],
                                in_=ybf[sbi][:]).then_inc(s_ys[sbi], 16)
                    if dbg:
                        for icq_ in range(4):
                            wq = JCs[icq_] * 128
                            g.dma_start(
                                out=dbg_E[icq_ * 512:(icq_ + 1) * 512, 0:wq],
                                in_=Ed[icq_ * 512:(icq_ + 1) * 512, 0:wq])\
                                .then_inc(s_fxd, 16)
                        for sbi in range(4):
                            g.wait_ge(s_ys[sbi], 16 * 4)
                        g.dma_start(out=dbg_y, in_=ydD).then_inc(s_fxd, 16)
                blk.gpsimd(g4)

        with ExitStack() as sc5:
            e5 = sc5.enter_context
            wpf = [e5(nc.sbuf_tensor(f"wpf{j}", [128, CM], F32)) for j in range(2)]
            wpT = [e5(nc.sbuf_tensor(f"wpT{c}", [128, CM], BF16)) for c in range(8)]
            yT = [e5(nc.sbuf_tensor(f"yT{c}", [128, TOK], BF16)) for c in range(8)]
            fsb = [e5(nc.sbuf_tensor(f"fsb{j}", [128, CM], F32)) for j in range(2)]
            xot = [e5(nc.sbuf_tensor(f"xot{j}", [128, CM], F32)) for j in range(2)]
            x2t = [e5(nc.sbuf_tensor(f"x2t{j}", [128, CM], F32)) for j in range(2)]
            x3t = [e5(nc.sbuf_tensor(f"x3t{j}", [128, CM], BF16)) for j in range(2)]
            lst2 = e5(nc.sbuf_tensor("lst2", [128, 16], F32))
            lst3 = e5(nc.sbuf_tensor("lst3", [128, 16], F32))
            mxc = e5(nc.sbuf_tensor("mxc", [128, 16], F32))
            pp = [e5(nc.psum_tensor(f"pp{j}", [128, CM], F32)) for j in range(2)]
            s_wpf = [sem(f"s_wpf{j}") for j in range(2)]
            s_wpc = sem("s_wpc")

            def sy5(s):
                for q in range(4):
                    s.wait_ge(s_ys[q], 16 * 4)
                for c in range(8):
                    j = c % 2
                    if c >= 2:
                        s.wait_ge(s_wpc, c - 1)
                    s.dma_start(out=wpf[j][:], in_=wpjT[c * 128:(c + 1) * 128, :])\
                        .then_inc(s_wpf[j], 16)
                for tb in range(16):
                    if tb >= 2:
                        s.wait_ge(s_ytl[tb % 2], 16 * 8 * (tb // 2))
                    for c in range(8):
                        s.dma_start(out=yT[c][:, tb * 128:(tb + 1) * 128],
                                    in_=ydD[tb * 128:(tb + 1) * 128,
                                            c * 128:(c + 1) * 128],
                                    transpose=True).then_inc(s_ytl[tb % 2], 16)
                for tb in range(16):
                    j = tb % 2
                    if tb >= 2:
                        s.wait_ge(s_re2, 2 * tb - 2)
                    gb = my_blocks[tb]
                    s.dma_start(out=xot[j][:], in_=xbD[gb * 128:(gb + 1) * 128, :])\
                        .then_inc(s_xol[j], 16)
            blk.sync(sy5)

            def pe5(t):
                t.wait_ge(s_wpc, 8)
                for tb in range(16):
                    t.wait_ge(s_ytl[tb % 2], 16 * 8 * (tb // 2 + 1))
                    if tb >= 2:
                        t.wait_ge(s_bia, tb - 1)
                    mm = None
                    for half in range(2):
                        for c in range(8):
                            mm = t.matmul(pp[tb % 2][:, half * 512:(half + 1) * 512],
                                          yT[c][:, tb * 128:(tb + 1) * 128],
                                          wpT[c][:, half * 512:(half + 1) * 512],
                                          start=(c == 0), stop=(c == 7))
                    mm.then_inc(s_pj, 1)
            blk.tensor(pe5)

            def act5(a):
                for i in range(18):
                    if 1 <= i <= 16:
                        k = i - 1
                        kc = 8 * (k % 2)
                        a.wait_ge(s_sq2, 7 * (k + 1))
                        a.activation(lst2[:, kc + 6:kc + 7], lst2[:, kc + 5:kc + 6],
                                     AF.Ln, bias=cc(cs, "ln_eps")).then_inc(s_rs2, 1)
                        a.wait_ge(s_rs2, 2 * k + 1)
                        a.activation(lst2[:, kc + 7:kc + 8], lst2[:, kc + 6:kc + 7],
                                     AF.Exp, scale=cc(cs, "neg_half"))\
                            .then_inc(s_rs2, 1)
                    if i >= 2:
                        k = i - 2
                        kc = 8 * (k % 2)
                        a.wait_ge(s_sq3, 7 * (k + 1))
                        a.activation(lst3[:, kc + 6:kc + 7], lst3[:, kc + 5:kc + 6],
                                     AF.Ln, bias=cc(cs, "ln_eps")).then_inc(s_rs3, 1)
                        a.wait_ge(s_rs3, 2 * k + 1)
                        a.activation(lst3[:, kc + 7:kc + 8], lst3[:, kc + 6:kc + 7],
                                     AF.Exp, scale=cc(cs, "neg_half"))\
                            .then_inc(s_rs3, 1)
            blk.scalar(act5)

            def dve5(v):
                for c in range(8):
                    j = c % 2
                    v.wait_ge(s_wpf[j], 16 * (c // 2 + 1))
                    v.tensor_copy(wpT[c][:], wpf[j][:]).then_inc(s_wpc, 1)
                for i in range(18):
                    if i < 16:
                        tb = i
                        j = tb % 2
                        tc = 8 * j
                        v.wait_ge(s_pj, tb + 1)
                        if tb >= 2:
                            v.wait_ge(s_re2, 2 * tb - 3)
                            v.wait_ge(s_rs2, 2 * (tb - 1))
                        v.scalar_tensor_tensor(fsb[j][:], pp[j][:], 1.0,
                                               bpb_s[:], ALU.bypass, ALU.add)\
                            .then_inc(s_bia, 1)
                        v.wait_ge(s_bia, tb + 1)
                        _stats(v, lst2, tc, fsb[j], 7 * tb, s_sq2)
                    if 1 <= i <= 16:
                        k = i - 1
                        jk = k % 2
                        kc = 8 * jk
                        v.wait_ge(s_rs2, 2 * (k + 1))
                        v.wait_ge(s_xol[jk], 16 * (k // 2 + 1))
                        if k >= 2:
                            v.wait_ge(s_o2[jk], 16 * (k // 2))
                            v.wait_ge(s_ap3, k - 1)
                            v.wait_ge(s_rs3, 2 * (k - 1))
                        v.tensor_scalar(x2t[jk][:], fsb[jk][:],
                                        lst2[:, kc + 2:kc + 3],
                                        lst2[:, kc + 7:kc + 8], ALU.subtract,
                                        ALU.mult).then_inc(s_re2, 1)
                        v.wait_ge(s_re2, 2 * k + 1)
                        v.tensor_tensor(x2t[jk][:], x2t[jk][:], xot[jk][:],
                                        ALU.add).then_inc(s_re2, 1)
                        v.wait_ge(s_re2, 2 * k + 2)
                        _stats(v, lst3, kc, x2t[jk], 7 * k, s_sq3)
                    if i >= 2:
                        k = i - 2
                        jk = k % 2
                        kc = 8 * jk
                        v.wait_ge(s_rs3, 2 * (k + 1))
                        if k >= 2:
                            v.wait_ge(s_o3[jk], 16 * (k // 2))
                        v.tensor_scalar(x3t[jk][:], x2t[jk][:],
                                        lst3[:, kc + 2:kc + 3],
                                        lst3[:, kc + 7:kc + 8], ALU.subtract,
                                        ALU.mult).then_inc(s_ap3, 1)
                        v.wait_ge(s_ap3, k + 1)
                        v.reduce_max(mxc[:, k:k + 1], x3t[jk][:], axis=AX.X,
                                     apply_absolute_value=True).then_inc(s_mx3, 1)
                v.wait_ge(s_mx3, 16)
                v.tensor_copy(ldump[:, 0:8], cs[:, 0:8])
                v.reduce_max(sc128[:, 0:1], mxc[:], axis=AX.X).then_inc(s_fx, 1)
            blk.vector(dve5)

            def pe5b(t):
                t.wait_ge(s_fx, 1)
                t.transpose(pp[0][:, 0:128], sc128[:, 0:128], idf_s[:])\
                    .then_inc(s_fx, 1)
            blk.tensor(pe5b)

            def dve5b(v):
                v.wait_ge(s_fx, 2)
                v.reduce_max(lm11[0:1, 0:1], pp[0][0:1, 0:128], axis=AX.X)\
                    .then_inc(s_fx, 1)
            blk.vector(dve5b)

            def g5(g):
                for i in range(16):
                    jk = i % 2
                    g.wait_ge(s_re2, 2 * (i + 1))
                    g.dma_start(out=x2o[i * 128:(i + 1) * 128, :],
                                in_=x2t[jk][:]).then_inc(s_o2[jk], 16)
                    g.wait_ge(s_mx3, i + 1)
                    g.dma_start(out=x3o[i * 128:(i + 1) * 128, :],
                                in_=x3t[jk][:]).then_inc(s_o3[jk], 16)
                g.wait_ge(s_fx, 3)
                g.dma_start(out=lmxo, in_=lm11[0:1, 0:1]).then_inc(s_fxd, 16)
                if dbg:
                    g.dma_start(out=dbg_x1, in_=x1d).then_inc(s_fxd, 16)
                for j in range(2):
                    g.wait_ge(s_o2[j], 16 * 8)
                    g.wait_ge(s_o3[j], 16 * 8)
                g.wait_ge(s_dqs[0], 16 * 7)
                g.wait_ge(s_fxd, 16 * (1 + (8 if dbg else 0)))
            blk.gpsimd(g5)

    return nc


import ml_dtypes
from concourse.bass_utils import run_bass_kernel_spmd

_BF = ml_dtypes.bfloat16


def _host_consts():
    rr = np.arange(128)
    cumT = (rr[:, None] < rr[None, :]).astype(_BF)
    lowinc = (rr[:, None] >= rr[None, :]).astype(_BF)
    ltstr = (rr[None, :] < rr[:, None]).astype(_BF)
    ltinc = (rr[None, :] <= rr[:, None]).astype(_BF)
    headsel = np.zeros((128, 2), _BF)
    headsel[0:64, 0] = 1.0
    headsel[64:128, 1] = 1.0
    iota32 = (rr[:, None] + 128.0 * np.arange(32)[None, :]).astype(np.float32)
    return cumT, lowinc, ltstr, ltinc, headsel, iota32


def kernel(**inputs):
    x = np.asarray(inputs["L0_attn_x"], dtype=np.float32)
    W_attn = np.asarray(inputs["L0_attn_W_attn"], dtype=np.float32)
    b_attn = np.asarray(inputs["L0_attn_b_attn"], dtype=np.float32)
    W_proj = np.asarray(inputs["L0_attn_W_proj"], dtype=np.float32)
    b_proj = np.asarray(inputs["L0_attn_b_proj"], dtype=np.float32)
    w_up = np.asarray(inputs["L0_ffn_w_up"], dtype=np.float32)
    w_gate = np.asarray(inputs["L0_ffn_w_gate"], dtype=np.float32)
    w_down = np.asarray(inputs["L0_ffn_w_down"], dtype=np.float32)

    cumT, lowinc, ltstr, ltinc, headsel, iota32 = _host_consts()
    common = dict(consts=consts_array(),
                  onesb=np.ones((128, 128), _BF),
                  identf=np.eye(128, dtype=np.float32),
                  onesf=np.ones((128, 128), np.float32))
    l1_common = dict(common,
                     cumT=cumT, lowinc=lowinc, ltstr=ltstr, ltinc=ltinc,
                     headsel=headsel, iota32=iota32,
                     identb=np.eye(128).astype(_BF),
                     watT=np.ascontiguousarray(W_attn.T),
                     battn24=b_attn.reshape(24, 128),
                     wpjT=np.ascontiguousarray(W_proj.T),
                     bvb=np.tile(b_attn[2048:3072], (128, 1)),
                     bprojb=np.tile(b_proj, (128, 1)))

    res1 = {}
    for h in (0, 1):
        nc1 = build_L1(h)
        ims = []
        for b in range(4):
            c = 2 * b + h
            im = dict(l1_common)
            im["xb"] = x[b]
            im["wup_sl"] = w_up[384 * c:384 * (c + 1)]
            im["wgate_sl"] = w_gate[384 * c:384 * (c + 1)]
            im["wdown_sl"] = w_down[128 * c:128 * (c + 1)]
            ims.append(im)
        r = run_bass_kernel_spmd(nc1, ims, core_ids=list(range(4)))
        for b in range(4):
            res1[2 * b + h] = r.results[b]

    lmx8 = np.array([res1[c]["lmxo"][0, 0] for c in range(8)],
                    dtype=np.float32).reshape(1, 8)
    wupdq = np.concatenate([res1[c]["wupdq_sl"] for c in range(8)], axis=0)
    wgatedq = np.concatenate([res1[c]["wgatedq_sl"] for c in range(8)], axis=0)
    wdowndq = np.concatenate([res1[c]["wdowndq_sl"] for c in range(8)], axis=0)

    nc2 = build_L2()
    im2 = [dict(x3bf=res1[c]["x3o"], lmx8=lmx8, wupdq=wupdq, wgatedq=wgatedq,
                **common) for c in range(8)]
    r2 = run_bass_kernel_spmd(nc2, im2, core_ids=list(range(8)))
    lmu8 = np.array([r2.results[c]["lmug"][0, 0] for c in range(8)],
                    dtype=np.float32).reshape(1, 8)
    lmg8 = np.array([r2.results[c]["lmug"][1, 0] for c in range(8)],
                    dtype=np.float32).reshape(1, 8)

    nc3 = build_L3()
    im3 = [dict(gateT=r2.results[c]["gateT"], lmg8=lmg8, **common)
           for c in range(8)]
    r3 = run_bass_kernel_spmd(nc3, im3, core_ids=list(range(8)))
    lmg28 = np.array([r3.results[c]["lmg2"][0, 0] for c in range(8)],
                     dtype=np.float32).reshape(1, 8)

    nc4 = build_L4()
    im4 = [dict(g2Tin=r3.results[c]["g2T"], upTin=r2.results[c]["upT"],
                lmg28=lmg28, lmu8=lmu8, **common) for c in range(8)]
    r4 = run_bass_kernel_spmd(nc4, im4, core_ids=list(range(8)))
    lmo8 = np.array([r4.results[c]["lmo"][0, 0] for c in range(8)],
                    dtype=np.float32).reshape(1, 8)

    nc5 = build_L5()
    im5 = [dict(oTin=r4.results[c]["oT"], lmo8=lmo8, wdowndq=wdowndq,
                x2in=res1[c]["x2o"], **common) for c in range(8)]
    r5 = run_bass_kernel_spmd(nc5, im5, core_ids=list(range(8)))

    out = np.zeros((4, 4096, 1024), dtype=np.float32)
    for c in range(8):
        out[c // 2, np.array(core_rows(c))] = r5.results[c]["yout"]
    return out
